# version 06
# baseline (speedup 1.0000x reference)
import contextlib
import os
import numpy as np
import concourse.bass as bass
import concourse.mybir as mybir
from concourse.bass_utils import run_bass_kernel_spmd

F32 = mybir.dt.float32
BF16 = mybir.dt.bfloat16
I32 = mybir.dt.int32
AF = mybir.ActivationFunctionType
ALU = mybir.AluOpType

H = 4096
S = 2048
ROWS = 1024
SC_A = float(1.0 / np.sqrt(128.0))
SC_H = float(SC_A ** 0.5)
SC_B = 0.125
SC_BH = float(SC_B ** 0.5)
LN10K = float(np.log(10000.0))
PI = float(np.pi)
KILL = -30000.0
STAGE = 4


class Ctx:
    pass



def _blk(w, kchunks, mslices, mwidth):
    K, M = w.shape
    assert K == kchunks * 128 and M == mslices * mwidth
    return np.ascontiguousarray(
        w.reshape(kchunks, 128, mslices, mwidth).transpose(2, 0, 1, 3))


def make_in_maps(inputs):
    x = np.asarray(inputs["L0_attn_a_x"], dtype=np.float32)
    qkvw = np.asarray(inputs["L0_attn_a_attn_qkvw"], np.float32)
    qkvb = np.asarray(inputs["L0_attn_a_attn_qkvb"], np.float32)
    ow = np.asarray(inputs["L0_attn_a_attn_ow"], np.float32)
    ob = np.asarray(inputs["L0_attn_a_attn_ob"], np.float32)
    wqv = np.asarray(inputs["L0_attn_b_w_qkv"], np.float32)
    wo_b = np.asarray(inputs["L0_attn_b_w_o"], np.float32)
    qc1w = np.asarray(inputs["L0_attn_b_qc1_w"], np.float32)
    qc1b = np.asarray(inputs["L0_attn_b_qc1_b"], np.float32)
    qc2w = np.asarray(inputs["L0_attn_b_qc2_w"], np.float32)
    qc2b = np.asarray(inputs["L0_attn_b_qc2_b"], np.float32)
    kc1w = np.asarray(inputs["L0_attn_b_kc1_w"], np.float32)
    kc1b = np.asarray(inputs["L0_attn_b_kc1_b"], np.float32)
    kc2w = np.asarray(inputs["L0_attn_b_kc2_w"], np.float32)
    kc2b = np.asarray(inputs["L0_attn_b_kc2_b"], np.float32)
    ktemp = np.asarray(inputs["L0_attn_b_key_temp"], np.float32)
    wg = np.asarray(inputs["L0_ffn_w_gate"], np.float32)
    swg = np.asarray(inputs["L0_ffn_s_wgate"], np.float32)
    wu = np.asarray(inputs["L0_ffn_w_up"], np.float32)
    swu = np.asarray(inputs["L0_ffn_s_wup"], np.float32)
    wd = np.asarray(inputs["L0_ffn_w_down"], np.float32)
    swd = np.asarray(inputs["L0_ffn_s_wdown"], np.float32)
    invg = np.asarray(inputs["L0_ffn_inv_gate"], np.float32)
    invu = np.asarray(inputs["L0_ffn_inv_up"], np.float32)
    invi = np.asarray(inputs["L0_ffn_inv_inter"], np.float32)

    ident = np.eye(128, dtype=np.float32)
    w1q = np.zeros((8, 3, 128, 128), np.float32)
    for occ in range(8):
        for tau in range(3):
            for gl in range(2):
                g = occ * 2 + gl
                w1q[occ, tau, gl * 64:(gl + 1) * 64, gl * 64:(gl + 1) * 64] = \
                    qc1w[g * 64:(g + 1) * 64, :, tau].T
    w1k = np.zeros((2, 3, 128, 128), np.float32)
    for occ in range(2):
        for tau in range(3):
            for gl in range(2):
                g = occ * 2 + gl
                w1k[occ, tau, gl * 64:(gl + 1) * 64, gl * 64:(gl + 1) * 64] = \
                    kc1w[g * 64:(g + 1) * 64, :, tau].T
    w2q = np.ascontiguousarray(
        qc2w.transpose(1, 2, 0).reshape(8, 128, 3, 8, 128).transpose(0, 2, 3, 1, 4))
    w2k = np.ascontiguousarray(
        kc2w.transpose(1, 2, 0).reshape(2, 128, 3, 2, 128).transpose(0, 2, 3, 1, 4))
    hsum16 = np.zeros((8, 128, 64), np.float32)
    for cc in range(8):
        for hl in range(2):
            hsum16[cc, hl * 64:(hl + 1) * 64, :] = np.eye(64, dtype=np.float32) / 16.0
    hsum4 = np.zeros((2, 128, 64), np.float32)
    for cc in range(2):
        for hl in range(2):
            hsum4[cc, hl * 64:(hl + 1) * 64, :] = np.eye(64, dtype=np.float32) / 4.0
    bc2 = np.zeros((64, 128), np.float32)
    bc2[:, 0:64] = np.eye(64, dtype=np.float32) * 0.5
    bc2[:, 64:128] = np.eye(64, dtype=np.float32) * 0.5
    hsum2x64 = np.zeros((128, 2), np.float32)
    hsum2x64[0:64, 0] = 1.0
    hsum2x64[64:128, 1] = 1.0
    bch2 = np.zeros((2, 128), np.float32)
    bch2[0, 0:64] = 1.0
    bch2[1, 64:128] = 1.0
    swap = np.zeros((128, 128), np.float32)
    for p in range(128):
        swap[p ^ 1, p] = 1.0
    rope_c = np.zeros((64, 128), np.float32)
    rope_s = np.zeros((64, 128), np.float32)
    for p in range(128):
        i = (p % 64) // 2
        rope_c[i, p] = 1.0
        rope_s[i, p] = -1.0 if (p % 2 == 0) else 1.0
    tri = np.triu(np.ones((128, 128), np.float32))
    masks = np.zeros((4, 128, 512), np.float32)
    for j in range(4):
        masks[j, :, j * 128:(j + 1) * 128] = tri
        masks[j, :, (j + 1) * 128:] = 1.0

    in_maps = []
    for core in range(8):
        b, half = core // 2, core % 2
        xb = x[b]
        roff = half * ROWS
        wq = qkvw[:, 2048 * half:2048 * (half + 1)]
        wk = qkvw[:, 4096 + 2048 * half:4096 + 2048 * (half + 1)]
        wv = qkvw[:, 8192 + 2048 * half:8192 + 2048 * (half + 1)]
        wqkv = np.concatenate([_blk(wq, 32, 16, 128), _blk(wk, 32, 16, 128),
                               _blk(wv, 32, 16, 128)], axis=0)
        bqkv = np.concatenate([
            qkvb[2048 * half:2048 * (half + 1)],
            qkvb[4096 + 2048 * half:4096 + 2048 * (half + 1)],
            qkvb[8192 + 2048 * half:8192 + 2048 * (half + 1)],
        ]).reshape(48, 128).T
        gates = np.zeros((128, 8), np.float32)
        gates[:, 0] = 1.0 - half
        gates[:, 1] = float(half)
        gates[:, 2] = float(half)
        gates[:, 3] = 1.0 - half
        gates[:, 4] = float(half)
        gates[:, 5] = 0.0
        gates[:, 6] = 0.0 if half else KILL
        gates[:, 7] = ktemp[0]
        m = {
            "x_tm": np.ascontiguousarray(xb),
            "x_fm": np.ascontiguousarray(xb.T),
            "x_own": np.ascontiguousarray(xb[roff:roff + ROWS]),
            "wqkv": wqkv,
            "bqkv": np.ascontiguousarray(bqkv),
            "wo": _blk(ow, 32, 8, 512),
            "ob": ob.reshape(1, 4096),
            "wgT": _blk(np.ascontiguousarray(wg.T), 32, 4, 512),
            "wuT": _blk(np.ascontiguousarray(wu.T), 32, 4, 512),
            "wdT": _blk(np.ascontiguousarray(wd.T), 16, 8, 512),
            "swg": swg.reshape(1, 2048), "swu": swu.reshape(1, 2048),
            "swd": swd.reshape(1, 4096),
            "invg": invg.reshape(1, 4096), "invu": invu.reshape(1, 4096),
            "invi": invi.reshape(1, 2048),
            "wqbT": _blk(np.ascontiguousarray(wqv.T), 32, 12, 128),
            "wobT": _blk(np.ascontiguousarray(wo_b.T), 8, 8, 512),
            "w1q": w1q, "w1k": w1k, "w2q": w2q, "w2k": w2k,
            "cb": np.ascontiguousarray(
                np.concatenate([qc1b, qc2b, kc1b, kc2b]).reshape(20, 128).T),
            "pos": np.arange(roff, roff + ROWS, dtype=np.float32).reshape(1, ROWS),
            "masks": masks,
            "ident": ident,
            "hsum16": hsum16, "hsum4": hsum4, "bc2": bc2,
            "hsum2x64": hsum2x64, "bch2": bch2, "swap": swap,
            "rope_c": rope_c, "rope_s": rope_s,
            "gates": gates,
        }
        in_maps.append(m)
    return in_maps



def inc(c, name, k=1):
    c.cnt[name] += k
    return c.cnt[name]


def barrier(c):
    h = c.sem["gp"]
    tgt = c.cnt["gp"] + 5
    c.cnt["gp"] = tgt
    for reg, own in ((c.T, "mm"), (c.V, "dve"), (c.A, "act"), (c.Y, None), (c.G, None)):
        tgt_own = c.cnt[own] if own else 0
        def f(eng, tgt=tgt, h=h, own=own, tgt_own=tgt_own):
            if own:
                eng.wait_ge(c.sem[own], tgt_own)
            eng.nop().then_inc(h, 1)
            eng.wait_ge(h, tgt)
        reg(f)


def build_nc(stage=STAGE):
    nc = bass.Bass("TRN2", target_bir_lowering=False)
    c = Ctx()
    c.nc = nc
    c.stage = stage
    c.kph = int(os.environ.get("KPH", "99"))

    def din(name, shape, dt=F32):
        setattr(c, name, nc.dram_tensor(name, list(shape), dt, kind="ExternalInput").ap())

    din("x_tm", [S, H]); din("x_fm", [H, S]); din("x_own", [ROWS, H])
    din("wqkv", [48, 32, 128, 128]); din("bqkv", [128, 48])
    din("wo", [8, 32, 128, 512]); din("ob", [1, 4096])
    din("wgT", [4, 32, 128, 512]); din("wuT", [4, 32, 128, 512])
    din("wdT", [8, 16, 128, 512])
    din("swg", [1, 2048]); din("swu", [1, 2048]); din("swd", [1, 4096])
    din("invg", [1, 4096]); din("invu", [1, 4096]); din("invi", [1, 2048])
    din("wqbT", [12, 32, 128, 128]); din("wobT", [8, 8, 128, 512])
    din("w1q", [8, 3, 128, 128]); din("w1k", [2, 3, 128, 128])
    din("w2q", [8, 3, 8, 128, 128]); din("w2k", [2, 3, 2, 128, 128])
    din("cb", [128, 20]); din("pos", [1, ROWS]); din("masks", [4, 128, 512])
    din("ident", [128, 128])
    din("hsum16", [8, 128, 64]); din("hsum4", [2, 128, 64]); din("bc2", [64, 128])
    din("hsum2x64", [128, 2]); din("bch2", [2, 128]); din("swap", [128, 128])
    din("rope_c", [64, 128]); din("rope_s", [64, 128])
    din("gates", [128, 8])

    c.out = nc.dram_tensor("out", [ROWS, H], F32, kind="ExternalOutput").ap()

    c.qA = nc.dram_tensor("qA", [16, 128, 2048], BF16, kind="Internal").ap()
    c.kA = nc.dram_tensor("kA", [16, 128, 2048], BF16, kind="Internal").ap()
    c.vA = nc.dram_tensor("vA", [16, 128, 2048], BF16, kind="Internal").ap()
    c.ccA_in = nc.dram_tensor("ccA_in", [16, 128, 2048], BF16, kind="Internal").ap()
    c.ccA_out = nc.dram_tensor("ccA_out", [2, 16, 128, 2048], BF16, kind="Internal").ap()
    c.ccB1_in = nc.dram_tensor("ccB1_in", [6, 1024], BF16, kind="Internal").ap()
    c.ccB1_out = nc.dram_tensor("ccB1_out", [2, 6, 1024], BF16, kind="Internal").ap()
    c.ccB2_in = nc.dram_tensor("ccB2_in", [512, 1024], BF16, kind="Internal").ap()
    c.ccB2_out = nc.dram_tensor("ccB2_out", [2, 512, 1024], BF16, kind="Internal").ap()

    es = contextlib.ExitStack()

    def sb(name, shape, dt):
        setattr(c, name, es.enter_context(nc.sbuf_tensor(name, list(shape), dt)))

    sb("big", [128, 65536], BF16)
    sb("wbuf", [128, 8192], BF16)
    sb("stg", [128, 4096], BF16)
    sb("stg2", [128, 2048], BF16)
    sb("kbuf", [128, 2048], BF16)
    sb("vbuf", [128, 2048], BF16)
    sb("qtl", [128, 2048], BF16)
    sb("p4", [128, 2048], BF16)
    sb("fa", [128, 2048], F32)
    sb("fb", [128, 4096], BF16)
    sb("fcc", [128, 2048], F32)
    sb("cols", [128, 160], F32)
    sb("rowf", [1, 128], F32)
    sb("browz", [1, 4096], BF16)
    sb("cb16", [128, 128], BF16)
    sb("cf32", [128, 256], F32)
    sb("onesb", [128, 8], BF16)
    sb("ones1b", [1, 128], BF16)
    sb("ones1f", [1, 128], F32)

    c.P1 = [es.enter_context(nc.psum_tensor(f"p1_{i}", [128, 512], F32)) for i in range(4)]
    c.Pf = es.enter_context(nc.psum_tensor("pf", [128, 1024], F32))
    c.Pb = es.enter_context(nc.psum_tensor("pb", [128, 2048], BF16))

    c.sem = {}
    c.cnt = {}
    for name in ["dW", "dX", "dQ", "dO", "mm", "act", "dve", "gp", "cc",
                 "dWa", "dWb", "dXa", "dXb", "dSa", "dSb", "dKs", "dVs", "dQs",
                 "dCa", "dCb"]:
        c.sem[name] = es.enter_context(nc.semaphore(name))
        c.cnt[name] = 0

    c.block = es.enter_context(nc.Block())
    c.T = lambda f: c.block.tensor(f)
    c.V = lambda f: c.block.vector(f)
    c.A = lambda f: c.block.scalar(f)
    c.Y = lambda f: c.block.sync(f)
    c.G = lambda f: c.block.gpsimd(f)

    emit_setup(c)
    emit_attn_a(c)
    if c.stage >= 2 and c.kph >= 7:
        emit_ffn(c, tag="f1", emit_out=(c.stage == 2))
    if c.stage >= 3:
        emit_attn_b(c, emit_out=(c.stage == 3))
    if c.stage >= 4:
        emit_ffn(c, tag="f2", emit_out=True)
    emit_finish(c)

    es.close()
    return nc



def emit_setup(c):
    h = c.sem["dW"]

    def g(eng):
        eng.dma_start(out=c.cf32[:, 0:128], in_=c.ident).then_inc(h, 16)
        eng.dma_start(out=c.cf32[:, 128:136], in_=c.gates).then_inc(h, 16)
        eng.dma_start(out=c.cf32[:, 136:156], in_=c.cb).then_inc(h, 16)
        eng.dma_start(out=c.cf32[:, 156:204], in_=c.bqkv).then_inc(h, 16)
        eng.dma_start(out=c.cb16[:, 0:128], in_=c.ident).then_inc(h, 16)
    c.G(g)
    n = inc(c, "dW", 16 * 5)

    def v(eng, n=n):
        eng.wait_ge(c.sem["dW"], n)
        eng.memset(c.onesb[:, :], 1.0)
        eng.memset(c.ones1b[:, :], 1.0)
        eng.memset(c.ones1f[:, :], 1.0)
        eng.memset(c.cols[:, 148:149], 1e-5)
        eng.memset(c.cols[:, 149:150], 0.0)
        eng.tensor_scalar(out=c.cols[:, 96:128], in0=c.cf32[:, 156:188],
                          scalar1=SC_H, scalar2=None, op0=ALU.mult)
        eng.tensor_copy(out=c.cols[:, 128:144],
                        in_=c.cf32[:, 188:204]).then_inc(c.sem["dve"], 1)
    c.V(v)
    inc(c, "dve")
    barrier(c)
    c.identb = c.cb16[:, 0:128]
    c.identf = c.cf32[:, 0:128]
    c.g_sel0 = c.cf32[:, 128:129]
    c.g_sel1 = c.cf32[:, 129:130]
    c.g_peer0 = c.cf32[:, 130:131]
    c.g_peer1 = c.cf32[:, 131:132]
    c.g_halo0 = c.cf32[:, 132:133]
    c.g_halo1 = c.cf32[:, 133:134]
    c.g_kill = c.cf32[:, 134:135]
    c.g_ktemp = c.cf32[:, 135:136]
    c.convb = c.cf32[:, 136:156]
    c.bqkv_s = c.cols[:, 96:144]



def emit_attn_a(c):
    sm, ct = c.sem, c.cnt

    act0 = ct["act"]
    xt = []
    for hc in range(32):
        xt.append(inc(c, "dXa" if hc % 2 == 0 else "dXb", 16))

    def y(eng):
        for hc in range(32):
            if hc >= 2:
                eng.wait_ge(sm["act"], act0 + 2 * (hc - 2) + 2)
            buf = c.fa if hc % 2 == 0 else c.fcc
            tc, hf = hc // 2, hc % 2
            eng.dma_start(out=buf[:, :],
                          in_=c.x_tm[tc * 128:(tc + 1) * 128, hf * 2048:(hf + 1) * 2048]
                          ).then_inc(sm["dXa" if hc % 2 == 0 else "dXb"], 16)
    c.Y(y)

    def a(eng):
        for hc in range(32):
            eng.wait_ge(sm["dXa" if hc % 2 == 0 else "dXb"], xt[hc])
            buf = c.fa if hc % 2 == 0 else c.fcc
            tc, hf = hc // 2, hc % 2
            if hc > 0:
                eng.wait_ge(sm["act"], act0 + 2 * hc - 1)
            eng.activation(out=c.stg[:, 0:2048], in_=buf[:, :], func=AF.Copy,
                           accum_out=c.cols[:, 16 * hf + tc:16 * hf + tc + 1]
                           ).then_inc(sm["act"], 1)
            if hc > 0:
                eng.wait_ge(sm["act"], act0 + 2 * hc)
            eng.activation(out=c.stg[:, 2048:4096], in_=buf[:, :], func=AF.Square,
                           accum_out=c.cols[:, 32 + 16 * hf + tc:32 + 16 * hf + tc + 1]
                           ).then_inc(sm["act"], 1)
    c.A(a)
    ct["act"] += 64
    barrier(c)

    if c.kph < 2:
        return
    cl = c.cols

    def v1(eng):
        d = c.cnt["dve"]
        ops = [
            lambda: eng.tensor_tensor(out=cl[:, 0:16], in0=cl[:, 0:16],
                                      in1=cl[:, 16:32], op=ALU.add),
            lambda: eng.tensor_scalar(out=cl[:, 0:16], in0=cl[:, 0:16],
                                      scalar1=1.0 / 4096.0, scalar2=None,
                                      op0=ALU.mult),
            lambda: eng.tensor_tensor(out=cl[:, 32:48], in0=cl[:, 32:48],
                                      in1=cl[:, 48:64], op=ALU.add),
            lambda: eng.tensor_scalar(out=cl[:, 32:48], in0=cl[:, 32:48],
                                      scalar1=1.0 / 4096.0, scalar2=None,
                                      op0=ALU.mult),
            lambda: eng.tensor_tensor(out=cl[:, 16:32], in0=cl[:, 0:16],
                                      in1=cl[:, 0:16], op=ALU.mult),
            lambda: eng.tensor_tensor(out=cl[:, 48:64], in0=cl[:, 32:48],
                                      in1=cl[:, 16:32], op=ALU.subtract),
        ]
        for i, op in enumerate(ops):
            op().then_inc(sm["dve"], 1)
            d += 1
            eng.wait_ge(sm["dve"], d)
    c.V(v1)
    inc(c, "dve", 6)
    barrier(c)

    def a1(eng):
        eng.activation(out=cl[:, 48:64], in_=cl[:, 48:64], func=AF.Sqrt,
                       bias=cl[:, 148:149]).then_inc(sm["act"], 1)
    c.A(a1)
    inc(c, "act")
    barrier(c)

    def v2(eng):
        d = c.cnt["dve"]
        ops = [
            lambda: eng.reciprocal(out=cl[:, 64:80], in_=cl[:, 48:64]),
            lambda: eng.tensor_tensor(out=cl[:, 16:32], in0=cl[:, 0:16],
                                      in1=cl[:, 64:80], op=ALU.mult),
            lambda: eng.tensor_scalar(out=cl[:, 80:96], in0=cl[:, 16:32],
                                      scalar1=-1.0, scalar2=None, op0=ALU.mult),
        ]
        for op in ops:
            op().then_inc(sm["dve"], 1)
            d += 1
            eng.wait_ge(sm["dve"], d)
    c.V(v2)
    inc(c, "dve", 3)
    barrier(c)

    def t1(eng):
        eng.transpose(c.Pf[0:32, 0:128], cl[:, 64:96],
                      c.identf).then_inc(sm["mm"], 1)
    c.T(t1)
    inc(c, "mm")
    barrier(c)

    def v3(eng):
        eng.tensor_copy(out=c.fcc[0:32, 0:128],
                        in_=c.Pf[0:32, 0:128]).then_inc(sm["dve"], 1)
    c.V(v3)
    inc(c, "dve")
    barrier(c)

    for grp in range(4):
        def tsel(eng, grp=grp):
            for j in range(8):
                row = (grp // 2) * 16 + (grp % 2) * 8 + j
                mmi = eng.matmul(c.Pf[0:1, j * 128:(j + 1) * 128],
                                 lhsT=c.identf[0:32, row:row + 1],
                                 rhs=c.fcc[0:32, 0:128], start=True, stop=True)
                if j == 7:
                    mmi.then_inc(sm["mm"], 1)
        c.T(tsel)
        n = inc(c, "mm")

        def vsel(eng, grp=grp, n=n):
            eng.wait_ge(sm["mm"], n)
            dst = c.browz[0:1, grp * 1024:(grp + 1) * 1024]
            eng.tensor_copy(out=dst, in_=c.Pf[0:1, 0:1024]).then_inc(sm["dve"], 1)
        c.V(vsel)
        n2 = inc(c, "dve")

        def tw(eng, n2=n2):
            eng.wait_ge(sm["dve"], n2)
        c.T(tw)

    for half in range(4):
        def tb(eng, half=half):
            for n0 in range(2):
                mmi = eng.matmul(c.Pf[:, n0 * 512:(n0 + 1) * 512],
                                 lhsT=c.ones1b[0:1, :],
                                 rhs=c.browz[0:1, half * 1024 + n0 * 512:
                                             half * 1024 + (n0 + 1) * 512],
                                 start=True, stop=True)
            mmi.then_inc(sm["mm"], 1)
        c.T(tb)
        n = inc(c, "mm")

        def vb(eng, half=half, n=n):
            eng.wait_ge(sm["mm"], n)
            eng.tensor_copy(out=c.fb[:, half * 1024:(half + 1) * 1024],
                            in_=c.Pf[:, :]).then_inc(sm["dve"], 1)
        c.V(vb)
        n2 = inc(c, "dve")

        def tw2(eng, n2=n2):
            eng.wait_ge(sm["dve"], n2)
        c.T(tw2)
    barrier(c)

    if c.kph < 3:
        return
    dve0 = ct["dve"]
    xt2 = []
    for fc in range(32):
        xt2.append(inc(c, "dXa" if fc % 2 == 0 else "dXb", 16))

    def y2(eng):
        for fc in range(32):
            if fc >= 2:
                eng.wait_ge(sm["dve"], dve0 + 2 * (fc - 2) + 2)
            buf = c.fa if fc % 2 == 0 else c.fcc
            eng.dma_start(out=buf[:, :],
                          in_=c.x_fm[fc * 128:(fc + 1) * 128, :]
                          ).then_inc(sm["dXa" if fc % 2 == 0 else "dXb"], 16)
    c.Y(y2)

    def v4(eng):
        d = ct["dve"]
        for fc in range(32):
            eng.wait_ge(sm["dXa" if fc % 2 == 0 else "dXb"], xt2[fc])
            buf = c.fa if fc % 2 == 0 else c.fcc
            eng.tensor_tensor(out=buf[:, :], in0=buf[:, :], in1=c.fb[:, 0:2048],
                              op=ALU.mult).then_inc(sm["dve"], 1)
            d += 1
            eng.wait_ge(sm["dve"], d)
            eng.tensor_tensor(out=c.big[:, fc * 2048:(fc + 1) * 2048], in0=buf[:, :],
                              in1=c.fb[:, 2048:4096], op=ALU.add).then_inc(sm["dve"], 1)
            d += 1
    c.V(v4)
    ct["dve"] += 64
    barrier(c)

    mm0 = ct["mm"]
    act0b = ct["act"]
    wsem = ["dWa", "dWb"]
    ssem = ["dSa", "dSb"]
    wt = [inc(c, wsem[s % 2], 16) for s in range(48)]
    mm_t = {(s, nq): mm0 + 4 * s + max(0, min(s - 1, 32)) + nq + 1
            for s in range(48) for nq in range(4)}
    mmq = {s: mm0 + 5 * s + 9 for s in range(32)}
    ct["mm"] = mm0 + 192 + 32 + 1
    mm_bc = ct["mm"]
    act_t = {(s, nq): act0b + 4 * s + nq + 1 for s in range(48) for nq in range(4)}
    ct["act"] = act0b + 192
    st_t = [inc(c, ssem[s % 2], 16) for s in range(48)]

    xn_dve = ct["dve"]
    sq_inc = {}
    red_done = {}
    plan = []
    for s in range(32):
        plan.append(("sq", s))
        if s >= 2:
            plan.append(("red", s - 2))
    plan += [("red", 30), ("red", 31), ("negc", 0), ("tbl", 0)]
    for op, s in plan:
        if op == "sq":
            sq_inc[s] = inc(c, "dve")
        elif op == "red":
            red_done[s] = inc(c, "dve", 2 if s >= 16 else 1)
        else:
            inc(c, "dve")
    dve_negrow = ct["dve"] - 1
    dve_tbl = ct["dve"]

    def g1(eng):
        for s in range(48):
            if s >= 2:
                eng.wait_ge(sm["mm"], mm_t[(s - 2, 3)])
            eng.dma_start(out=c.wbuf[:, (s % 2) * 4096:(s % 2 + 1) * 4096]
                          .rearrange("p (f q) -> p f q", f=32),
                          in_=c.wqkv[s].rearrange("f p q -> p f q")
                          ).then_inc(sm[wsem[s % 2]], 16)
    c.G(g1)

    def t2(eng):
        eng.wait_ge(sm["dve"], xn_dve)
        ndsts = [c.Pf[0:1, 0:512], c.Pf[0:1, 512:1024],
                 c.P1[2][0:1, :], c.P1[3][0:1, :]]
        for s in range(48):
            if 2 <= s < 34:
                sn = s - 2
                eng.wait_ge(sm["dve"], sq_inc[sn])
                if sn >= 1:
                    eng.wait_ge(sm["dve"], red_done[sn - 1])
                sqb = c.stg2 if sn % 2 == 0 else c.p4
                for j in range(4):
                    mmi = eng.matmul(ndsts[j], lhsT=c.onesb[:, 0:1],
                                     rhs=sqb[:, j * 512:(j + 1) * 512],
                                     start=True, stop=True)
                mmi.then_inc(sm["mm"], 1)
            eng.wait_ge(sm[wsem[s % 2]], wt[s])
            for nq in range(4):
                ps = c.P1[nq % 2]
                e = 4 * s + nq
                if e >= 2:
                    eng.wait_ge(sm["act"], act0b + e - 1)
                for fc in range(32):
                    mmi = eng.matmul(
                        ps[:, :],
                        lhsT=c.wbuf[:, (s % 2) * 4096 + fc * 128:
                                    (s % 2) * 4096 + (fc + 1) * 128],
                        rhs=c.big[:, fc * 2048 + nq * 512:fc * 2048 + (nq + 1) * 512],
                        start=(fc == 0), stop=(fc == 31))
                mmi.then_inc(sm["mm"], 1)
        eng.wait_ge(sm["dve"], dve_negrow)
        eng.matmul(c.P1[2][:, 0:64], lhsT=c.ones1f[0:1, :],
                   rhs=c.fcc[0:1, 96:160], start=True,
                   stop=True).then_inc(sm["mm"], 1)
    c.T(t2)

    def v_norms(eng):
        for op, s in plan:
            if op == "sq":
                eng.wait_ge(sm["act"], act_t[(s, 3)])
                if s >= 2:
                    eng.wait_ge(sm["mm"], mmq[s - 2])
                sqb = c.stg2 if s % 2 == 0 else c.p4
                src = c.stg[:, (s % 2) * 2048:(s % 2 + 1) * 2048]
                eng.tensor_tensor(out=sqb[:, 0:2048], in0=src, in1=src,
                                  op=ALU.mult).then_inc(sm["dve"], 1)
            elif op == "red":
                eng.wait_ge(sm["mm"], mmq[s])
                srcs = [c.Pf[0:1, 0:512], c.Pf[0:1, 512:1024],
                        c.P1[2][0:1, :], c.P1[3][0:1, :]]
                if s < 16:
                    for j in range(4):
                        ri = eng.tensor_reduce(
                            out=c.fcc[0:1, s * 4 + j:s * 4 + j + 1], in_=srcs[j],
                            axis=mybir.AxisListType.X, op=ALU.max)
                    ri.then_inc(sm["dve"], 1)
                else:
                    for j in range(4):
                        ri = eng.tensor_reduce(
                            out=c.fcc[0:1, 80 + j:81 + j], in_=srcs[j],
                            axis=mybir.AxisListType.X, op=ALU.max)
                    ri.then_inc(sm["dve"], 1)
                    eng.wait_ge(sm["dve"], red_done[s] - 1)
                    eng.tensor_reduce(out=c.fcc[0:1, 48 + s:49 + s],
                                      in_=c.fcc[0:1, 80:84],
                                      axis=mybir.AxisListType.X,
                                      op=ALU.max).then_inc(sm["dve"], 1)
            elif op == "negc":
                eng.wait_ge(sm["dve"], red_done[31])
                for h in range(16):
                    ti = eng.tensor_scalar(
                        out=c.fcc[0:1, 96 + h * 4:100 + h * 4],
                        in0=c.fcc[0:1, h * 4:h * 4 + 4],
                        scalar1=c.fcc[0:1, 64 + h:65 + h], scalar2=-0.5,
                        op0=ALU.add, op1=ALU.mult)
                ti.then_inc(sm["dve"], 1)
            else:
                eng.wait_ge(sm["mm"], mm_bc)
                eng.tensor_copy(out=c.fcc[:, 512:576],
                                in_=c.P1[2][:, 0:64]).then_inc(sm["dve"], 1)
    c.V(v_norms)

    def a2(eng):
        for s in range(48):
            scale = SC_H if s < 32 else 1.0
            for nq in range(4):
                eng.wait_ge(sm["mm"], mm_t[(s, nq)])
                if s >= 2 and nq == 0:
                    eng.wait_ge(sm[ssem[s % 2]], st_t[s - 2])
                eng.activation(out=c.stg[:, (s % 2) * 2048 + nq * 512:
                                         (s % 2) * 2048 + (nq + 1) * 512],
                               in_=c.P1[nq % 2][:, :], func=AF.Identity,
                               bias=c.bqkv_s[:, s:s + 1],
                               scale=scale).then_inc(sm["act"], 1)
    c.A(a2)

    def y3(eng):
        for s in range(48):
            eng.wait_ge(sm["act"], act_t[(s, 3)])
            src = c.stg[:, (s % 2) * 2048:(s % 2 + 1) * 2048]
            if s < 16:
                dst = c.qA[s]
            elif s < 32:
                dst = c.kA[s - 16]
            else:
                dst = c.vA[s - 32]
            eng.dma_start(out=dst, in_=src).then_inc(sm[ssem[s % 2]], 16)
    c.Y(y3)
    barrier(c)

    if c.kph < 5:
        return
    def ya1done(eng, _a=ct["dSa"], _b=ct["dSb"]):
        eng.wait_ge(sm["dSa"], _a)
        eng.wait_ge(sm["dSb"], _b)
    c.Y(ya1done)
    c._a2p = {}
    nh = min(16, int(os.environ.get("KPH_H", "16")))
    _a2_pre(c, 0, 0, guard=None)
    _a2_pre(c, 0, 1)
    for hh in range(nh):
        _a2_head(c, hh, nh)
    barrier(c)

    if c.kph < 6:
        return
    def gcc(eng, _ca=c.cnt["dCa"], _cb=c.cnt["dCb"]):
        eng.wait_ge(sm["dCa"], _ca)
        eng.wait_ge(sm["dCb"], _cb)
        eng.collective_compute(
            "AllGather", ALU.bypass,
            replica_groups=[[0, 1], [2, 3], [4, 5], [6, 7]],
            ins=[c.ccA_in.rearrange("h p q -> (h p) q")],
            outs=[c.ccA_out.rearrange("r h p q -> (r h p) q")],
        ).then_inc(sm["cc"], 1)
    c.G(gcc)
    ncc = inc(c, "cc")
    barrier(c)

    for lc in range(32):
        r, j = lc // 16, lc % 16
        buf = c.kbuf if lc % 2 == 0 else c.vbuf
        tmp = c.stg2[:, 0:1024] if lc % 2 == 0 else c.stg2[:, 1024:2048]
        asem = "dKs" if lc % 2 == 0 else "dVs"
        dv_prev = c.cnt["dve"]
        nA = inc(c, asem, 16)
        nB = inc(c, asem, 16)

        def ya(eng, r=r, j=j, buf=buf, lc=lc, ncc=ncc, dv_prev=dv_prev, asem=asem):
            if lc == 0:
                eng.wait_ge(sm["cc"], ncc)
            if lc >= 2:
                eng.wait_ge(sm["dve"], dv_prev - 1)
            eng.dma_start(out=buf[:, 0:1024],
                          in_=c.ccA_out[r, j][:, 0:1024]).then_inc(sm[asem], 16)
            eng.dma_start(out=buf[:, 1024:2048],
                          in_=c.ccA_out[r, j][:, 1024:2048]).then_inc(sm[asem], 16)
        c.Y(ya)

        ndsel = inc(c, "dve", 2)

        def va(eng, buf=buf, tmp=tmp, lc=lc, nB=nB, asem=asem, ndsel=ndsel):
            eng.wait_ge(sm[asem], nB)
            eng.tensor_scalar(out=tmp, in0=buf[:, 0:1024],
                              scalar1=c.g_sel0, scalar2=None,
                              op0=ALU.mult).then_inc(sm["dve"], 1)
            eng.wait_ge(sm["dve"], ndsel - 1)
            eng.scalar_tensor_tensor(
                out=c.big[:, 32768 + lc * 1024:32768 + (lc + 1) * 1024],
                in0=buf[:, 1024:2048], scalar=c.g_sel1,
                in1=tmp, op0=ALU.mult,
                op1=ALU.add).then_inc(sm["dve"], 1)
        c.V(va)
    barrier(c)

    def gob(eng):
        eng.dma_start(out=c.browz[0:1, 0:4096], in_=c.ob).then_inc(sm["dW"], 16)
    c.G(gob)
    inc(c, "dW", 16)
    barrier(c)
    _oproj_residual(c, wblocks=c.wo, nof=8, nlat=32, latbase=32768,
                    bias_row=c.browz[0:1, 0:4096], xsrc=c.x_own,
                    emit_out=(c.stage == 1), tb_count=8)


def _a2_slots(c, hh):
    if hh % 2 == 0:
        return dict(kb=c.kbuf[:, :], vb=c.vbuf[:, :], qt=c.qtl[:, :],
                    vs=c.stg[:, 0:2048], negbase=144)
    B = c.big
    return dict(kb=B[:, 36864:38912], qt=B[:, 38912:40960],
                vs=B[:, 40960:43008], vb=B[:, 43008:45056], negbase=140)


def _a2_pre(c, hh, phase, guard=None):
    sm = c.sem
    S = _a2_slots(c, hh)
    st = c._a2p.setdefault(hh, {})

    if phase == 0:
        prev = c._a2p.get(hh - 1)
        st["nK"] = inc(c, "dKs", 16)
        st["nV"] = inc(c, "dVs", 16)
        st["nQ"] = inc(c, "dQs", 16)

        def y(eng, hh=hh, S=S, guard=guard, prev=prev):
            if guard is not None:
                eng.wait_ge(sm["mm"], guard)
            if prev is not None:
                eng.wait_ge(sm["dKs"], prev["nK"])
                eng.wait_ge(sm["dVs"], prev["nV"])
                eng.wait_ge(sm["dQs"], prev["nQ"])
            eng.dma_start(out=S["kb"], in_=c.kA[hh]).then_inc(sm["dKs"], 16)
            eng.dma_start(out=S["vs"], in_=c.vA[hh]).then_inc(sm["dVs"], 16)
            eng.dma_start(out=S["qt"], in_=c.qA[hh]).then_inc(sm["dQs"], 16)
        c.Y(y)

    elif phase == 1:
        def tv(eng, S=S, nV=st["nV"]):
            eng.wait_ge(sm["dVs"], nV)
            for rnd in range(2):
                for j in range(8):
                    kb = rnd * 8 + j
                    ti = eng.transpose(
                        c.Pb[:, rnd * 1024 + j * 128:rnd * 1024 + (j + 1) * 128],
                        S["vs"][:, kb * 128:(kb + 1) * 128], c.identb)
                ti.then_inc(sm["mm"], 1)
        c.T(tv)
        mmv = [inc(c, "mm"), inc(c, "mm")]

        def vv(eng, S=S, mmv=mmv):
            for rnd in range(2):
                eng.wait_ge(sm["mm"], mmv[rnd])
                eng.tensor_copy(out=S["vb"][:, rnd * 1024:(rnd + 1) * 1024],
                                in_=c.Pb[:, rnd * 1024:(rnd + 1) * 1024]
                                ).then_inc(sm["dve"], 1)
        c.V(vv)
        inc(c, "dve")
        st["ndv_vb"] = inc(c, "dve")


def _a2_head(c, hh, nh):
    sm = c.sem
    S = _a2_slots(c, hh)
    st = c._a2p[hh]
    nxt = hh + 1 if hh + 1 < nh else None
    Sn = _a2_slots(c, nxt) if nxt is not None else None

    mmg = c.cnt["mm"]
    if nxt is not None:
        stn = c._a2p.setdefault(nxt, {})
        stn["nK"] = inc(c, "dKs", 16)
        stn["nV"] = inc(c, "dVs", 16)
        stn["nQ"] = inc(c, "dQs", 16)
    mm_sc, act_e, mm_pv = {}, {}, {}
    ndv5, ndv6, nmm_rl, cprev = {}, {}, {}, {}
    for qt in range(4):
        for kbg in range(8):
            mm_sc[(qt, kbg)] = inc(c, "mm")
            act_e[(qt, kbg)] = inc(c, "act")
            mm_pv[(qt, kbg)] = inc(c, "mm")
        ndv5[qt] = inc(c, "dve", 2)
        nmm_rl[qt] = inc(c, "mm")
        ndv6[qt] = inc(c, "dve", 2)
        csem = "dCa" if qt % 2 == 0 else "dCb"
        cprev[qt] = c.cnt[csem]
        inc(c, csem, 16)
        if qt == 0 and nxt is not None:
            stn["mmv"] = [inc(c, "mm"), inc(c, "mm")]
            inc(c, "dve")
            stn["ndv_vb"] = inc(c, "dve")

    def y_all(eng):
        if nxt is not None:
            eng.wait_ge(sm["mm"], mmg)
            eng.wait_ge(sm["dKs"], st["nK"])
            eng.wait_ge(sm["dVs"], st["nV"])
            eng.wait_ge(sm["dQs"], st["nQ"])
            eng.dma_start(out=Sn["kb"], in_=c.kA[nxt]).then_inc(sm["dKs"], 16)
            eng.dma_start(out=Sn["vs"], in_=c.vA[nxt]).then_inc(sm["dVs"], 16)
            eng.dma_start(out=Sn["qt"], in_=c.qA[nxt]).then_inc(sm["dQs"], 16)
        for qt in range(4):
            csem = "dCa" if qt % 2 == 0 else "dCb"
            eng.wait_ge(sm["dve"], ndv6[qt])
            eng.dma_start(out=c.ccA_in[hh][:, qt * 512:(qt + 1) * 512],
                          in_=c.stg2[:, (qt % 2) * 512:(qt % 2 + 1) * 512]
                          ).then_inc(sm[csem], 16)
    c.Y(y_all)

    def t_all(eng):
        for qt in range(4):
            for kbg in range(8):
                if kbg > 0:
                    eng.wait_ge(sm["act"], act_e[(qt, kbg - 1)])
                if qt == 0 and kbg == 0:
                    eng.wait_ge(sm["dKs"], st["nK"])
                    eng.wait_ge(sm["dQs"], st["nQ"])
                for j in range(2):
                    kb = kbg * 2 + j
                    mmi = eng.matmul(c.Pf[:, j * 512:(j + 1) * 512],
                                     lhsT=S["kb"][:, kb * 128:(kb + 1) * 128],
                                     rhs=S["qt"][:, qt * 512:(qt + 1) * 512],
                                     start=True, stop=True)
                mmi.then_inc(sm["mm"], 1)
                eng.wait_ge(sm["act"], act_e[(qt, kbg)])
                if qt == 0 and kbg == 0:
                    eng.wait_ge(sm["dve"], st["ndv_vb"])
                for j in range(2):
                    kb = kbg * 2 + j
                    rh = c.p4[:, (kbg % 2) * 1024 + j * 512:
                              (kbg % 2) * 1024 + (j + 1) * 512]
                    eng.matmul(c.P1[0][:, :],
                               lhsT=S["vb"][:, kb * 128:(kb + 1) * 128],
                               rhs=rh, start=(kb == 0), stop=(kb == 15))
                    mmi = eng.matmul(c.P1[1][0:1, :], lhsT=c.onesb[:, 0:1],
                                     rhs=rh, start=(kb == 0), stop=(kb == 15))
                mmi.then_inc(sm["mm"], 1)
            eng.wait_ge(sm["dve"], ndv5[qt])
            eng.matmul(c.P1[2][:, :], lhsT=c.ones1b[0:1, :],
                       rhs=c.browz[0:1, 0:512], start=True,
                       stop=True).then_inc(sm["mm"], 1)
            eng.wait_ge(sm["dve"], ndv6[qt])
            if qt == 0 and nxt is not None:
                eng.wait_ge(sm["dVs"], stn["nV"])
                for rnd in range(2):
                    for j in range(8):
                        kb = rnd * 8 + j
                        ti = eng.transpose(
                            c.Pb[:, rnd * 1024 + j * 128:rnd * 1024 + (j + 1) * 128],
                            Sn["vs"][:, kb * 128:(kb + 1) * 128], c.identb)
                    ti.then_inc(sm["mm"], 1)
    c.T(t_all)

    def a_all(eng):
        for qt in range(4):
            for kbg in range(8):
                eng.wait_ge(sm["mm"], mm_sc[(qt, kbg)])
                if kbg >= 2:
                    eng.wait_ge(sm["mm"], mm_pv[(qt, kbg - 2)])
                eng.activation(out=c.p4[:, (kbg % 2) * 1024:(kbg % 2 + 1) * 1024],
                               in_=c.Pf[:, :], func=AF.Exp,
                               bias=c.fcc[:, 512 + hh * 4 + qt:513 + hh * 4 + qt]
                               ).then_inc(sm["act"], 1)
    c.A(a_all)

    def v_all(eng):
        for qt in range(4):
            eng.wait_ge(sm["mm"], mm_pv[(qt, 7)])
            eng.reciprocal(out=c.fcc[0:1, 0:512],
                           in_=c.P1[1][0:1, :]).then_inc(sm["dve"], 1)
            eng.wait_ge(sm["dve"], ndv5[qt] - 1)
            eng.tensor_copy(out=c.browz[0:1, 0:512],
                            in_=c.fcc[0:1, 0:512]).then_inc(sm["dve"], 1)
            eng.wait_ge(sm["mm"], nmm_rl[qt])
            eng.wait_ge(sm["dCa" if qt % 2 == 0 else "dCb"], cprev[qt])
            eng.tensor_copy(out=c.stg2[:, 1024:1536],
                            in_=c.P1[2][:, :]).then_inc(sm["dve"], 1)
            eng.wait_ge(sm["dve"], ndv6[qt] - 1)
            eng.tensor_tensor(out=c.stg2[:, (qt % 2) * 512:(qt % 2 + 1) * 512],
                              in0=c.P1[0][:, :], in1=c.stg2[:, 1024:1536],
                              op=ALU.mult).then_inc(sm["dve"], 1)
            if qt == 0 and nxt is not None:
                for rnd in range(2):
                    eng.wait_ge(sm["mm"], stn["mmv"][rnd])
                    eng.tensor_copy(out=Sn["vb"][:, rnd * 1024:(rnd + 1) * 1024],
                                    in_=c.Pb[:, rnd * 1024:(rnd + 1) * 1024]
                                    ).then_inc(sm["dve"], 1)
    c.V(v_all)




def _oproj_residual(c, wblocks, nof, nlat, latbase, bias_row, xsrc, emit_out, tb_count):
    sm = c.sem
    c._op_prev_mm = None
    c._op_prev_st = None
    nlatq = nlat // 4
    for of in range(nof):
        for tbg in range(tb_count // 4):
            prev_st = getattr(c, "_op_prev_st", None)
            mm_q0 = c.cnt["mm"]
            wq_t = [inc(c, "dWa" if q % 2 == 0 else "dWb", 16) for q in range(4)]

            first_grp = (of == 0 and tbg == 0)

            def gw(eng, of=of, tbg=tbg, wq_t=wq_t, mm_q0=mm_q0, first_grp=first_grp):
                for fcq in range(4):
                    if fcq >= 2:
                        eng.wait_ge(sm["mm"], mm_q0 + fcq)
                    elif not first_grp:
                        eng.wait_ge(sm["mm"], mm_q0)
                    eng.dma_start(
                        out=c.wbuf[:, (fcq % 2) * 4096:
                                   (fcq % 2) * 4096 + nlatq * 512]
                        .rearrange("p (f q) -> p f q", f=nlatq),
                        in_=wblocks[of, fcq * nlatq:(fcq + 1) * nlatq]
                        .rearrange("f p q -> p f q")
                        ).then_inc(sm["dWa" if fcq % 2 == 0 else "dWb"], 16)
            c.G(gw)

            def tchain(eng, of=of, tbg=tbg, wq_t=wq_t, dw0=c.cnt["dW"],
                       first_grp=first_grp):
                if first_grp and bias_row is not None:
                    eng.wait_ge(sm["dW"], dw0)
                for fcq in range(4):
                    eng.wait_ge(sm["dWa" if fcq % 2 == 0 else "dWb"], wq_t[fcq])
                    for tb4 in range(4):
                        for j in range(nlatq):
                            fc = fcq * nlatq + j
                            tb = tbg * 4 + tb4
                            mmi = eng.matmul(
                                c.P1[tb4][:, :],
                                lhsT=c.big[:, latbase + fc * 1024 + tb * 128:
                                           latbase + fc * 1024 + (tb + 1) * 128],
                                rhs=c.wbuf[:, (fcq % 2) * 4096 + j * 512:
                                           (fcq % 2) * 4096 + (j + 1) * 512],
                                start=(fcq == 0 and j == 0),
                                stop=(bias_row is None and fcq == 3 and j == nlatq - 1))
                    mmi.then_inc(sm["mm"], 1)
                if bias_row is not None:
                    for tb4 in range(4):
                        mmi = eng.matmul(c.P1[tb4][:, :], lhsT=c.ones1b[0:1, :],
                                         rhs=bias_row[0:1, of * 512:(of + 1) * 512],
                                         start=False, stop=True)
                    mmi.then_inc(sm["mm"], 1)
            c.T(tchain)
            inc(c, "mm", 3 if bias_row is None else 4)
            mm_last = inc(c, "mm")

            xs_t = []
            if xsrc is not None:
                def yx(eng, of=of, tbg=tbg, xs_t=xs_t, ndvp=c.cnt["dve"]):
                    eng.wait_ge(sm["dve"], ndvp)
                    for tb4 in range(4):
                        tb = tbg * 4 + tb4
                        eng.dma_start(out=c.fa[:, tb4 * 512:(tb4 + 1) * 512],
                                      in_=xsrc[tb * 128:(tb + 1) * 128,
                                               of * 512:(of + 1) * 512]
                                      ).then_inc(sm["dQ"], 16)
                c.Y(yx)
                for tb4 in range(4):
                    xs_t.append(inc(c, "dQ", 16))

            dv0 = c.cnt["dve"]

            def vev(eng, of=of, tbg=tbg, mm_last=mm_last, xs_t=xs_t,
                    prev_st=prev_st, dv0=dv0):
                eng.wait_ge(sm["mm"], mm_last)
                if xs_t:
                    eng.wait_ge(sm["dQ"], xs_t[3])
                if prev_st is not None:
                    eng.wait_ge(sm["dO"], prev_st)
                d = dv0
                for tb4 in range(4):
                    tb = tbg * 4 + tb4
                    xin = (c.fa[:, tb4 * 512:(tb4 + 1) * 512] if xs_t else
                           c.big[:, tb * 4096 + of * 512:tb * 4096 + (of + 1) * 512])
                    eng.tensor_tensor(out=c.fcc[:, tb4 * 512:(tb4 + 1) * 512],
                                      in0=c.P1[tb4][:, :],
                                      in1=xin,
                                      op=ALU.add).then_inc(sm["dve"], 1)
                    d += 1
                    eng.wait_ge(sm["dve"], d)
                    eng.tensor_copy(
                        out=c.big[:, tb * 4096 + of * 512:tb * 4096 + (of + 1) * 512],
                        in_=c.fcc[:, tb4 * 512:(tb4 + 1) * 512]).then_inc(sm["dve"], 1)
                    d += 1
            c.V(vev)
            ndv = inc(c, "dve", 8)

            if emit_out:
                def yo(eng, of=of, tbg=tbg, ndv=ndv):
                    eng.wait_ge(sm["dve"], ndv)
                    for tb4 in range(4):
                        tb = tbg * 4 + tb4
                        eng.dma_start(out=c.out[tb * 128:(tb + 1) * 128,
                                                of * 512:(of + 1) * 512],
                                      in_=c.fcc[:, tb4 * 512:(tb4 + 1) * 512]
                                      ).then_inc(sm["dO"], 16)
                c.Y(yo)
                c._op_prev_st = inc(c, "dO", 64)
            def twv(eng, ndv=ndv):
                eng.wait_ge(sm["dve"], ndv)
            c.T(twv)
            c._op_prev_mm = mm_last
    c._op_prev_mm = None
    c._op_prev_st = None
    barrier(c)


def _load_row(c, src_ap, width):
    sm = c.sem
    n = inc(c, "dW", 16)

    def g(eng, n=n):
        eng.dma_start(out=c.browz[0:1, 0:width], in_=src_ap).then_inc(sm["dW"], 16)
    c.G(g)
    return n


def _bcast_row(c, dst, width, row_n):
    sm = c.sem
    prev_ndv = None
    for rnd in range(width // 1024):
        nmm = inc(c, "mm")

        def t(eng, rnd=rnd, nmm=nmm, row_n=row_n, prev_ndv=prev_ndv):
            eng.wait_ge(sm["dW"], row_n)
            if prev_ndv is not None:
                eng.wait_ge(sm["dve"], prev_ndv)
            for n0 in range(2):
                mmi = eng.matmul(c.Pf[:, n0 * 512:(n0 + 1) * 512],
                                 lhsT=c.ones1b[0:1, :],
                                 rhs=c.browz[0:1, rnd * 1024 + n0 * 512:
                                             rnd * 1024 + (n0 + 1) * 512],
                                 start=True, stop=True)
            mmi.then_inc(sm["mm"], 1)
        c.T(t)
        ndv = inc(c, "dve")

        def v(eng, rnd=rnd, nmm=nmm, ndv=ndv):
            eng.wait_ge(sm["mm"], nmm)
            eng.tensor_copy(out=dst[:, rnd * 1024:(rnd + 1) * 1024],
                            in_=c.Pf[:, :]).then_inc(sm["dve"], 1)
        c.V(v)
        prev_ndv = ndv
    barrier(c)



def _ln_stats(c):
    sm = c.sem
    cl = c.cols
    for tc in range(8):
        ndv = inc(c, "dve", 3)

        def vst(eng, tc=tc, ndv=ndv):
            d = ndv - 3
            for j in range(8):
                bi = eng.bn_stats(out=c.fa[0:128, j * 6:(j + 1) * 6],
                                  in_=c.big[:, tc * 4096 + j * 512:
                                            tc * 4096 + (j + 1) * 512])
            bi.then_inc(sm["dve"], 1)
            d += 1
            eng.wait_ge(sm["dve"], d)
            eng.bn_aggr(out=c.fa[:, 48:50], in_=c.fa[:, 0:48]).then_inc(sm["dve"], 1)
            d += 1
            eng.wait_ge(sm["dve"], d)
            eng.tensor_copy(out=cl[:, 64 + tc:65 + tc],
                            in_=c.fa[:, 48:49]).then_inc(sm["dve"], 1)
        c.V(vst)
        na = inc(c, "act")

        def ast(eng, tc=tc, ndv=ndv, na=na):
            eng.wait_ge(sm["dve"], ndv)
            eng.activation(out=c.fa[:, 50:51], in_=c.fa[:, 49:50], func=AF.Sqrt,
                           bias=cl[:, 148:149]).then_inc(sm["act"], 1)
        c.A(ast)
        ndv2 = inc(c, "dve", 3)

        def vst2(eng, tc=tc, na=na, ndv2=ndv2):
            d = ndv2 - 3
            eng.wait_ge(sm["act"], na)
            eng.reciprocal(out=cl[:, 0 + tc:1 + tc],
                           in_=c.fa[:, 50:51]).then_inc(sm["dve"], 1)
            d += 1
            eng.wait_ge(sm["dve"], d)
            eng.tensor_tensor(out=c.fa[:, 51:52], in0=cl[:, 64 + tc:65 + tc],
                              in1=cl[:, 0 + tc:1 + tc],
                              op=ALU.mult).then_inc(sm["dve"], 1)
            d += 1
            eng.wait_ge(sm["dve"], d)
            eng.tensor_scalar(out=cl[:, 8 + tc:9 + tc], in0=c.fa[:, 51:52],
                              scalar1=-1.0, scalar2=None,
                              op0=ALU.mult).then_inc(sm["dve"], 1)
        c.V(vst2)
    barrier(c)

def emit_ffn(c, tag, emit_out):
    sm, ct = c.sem, c.cnt
    XRES = 0
    QG = 32768
    GS = 49152
    INT = 57344
    cl = c.cols
    barrier(c)

    _ln_stats(c)

    fstop = os.environ.get("FFN_STOP", "")
    ti = 0
    for th in range(1 if fstop else 2):
        rn = _load_row(c, c.invg, 4096)
        _bcast_row(c, c.fb, 4096, rn)
        for tcl in range(4):
            tc = th * 4 + tcl
            _ffn_quant_tile(
                c, src=c.big[:, XRES + tc * 4096:XRES + (tc + 1) * 4096],
                a_col=cl[:, tc:tc + 1], b_col=cl[:, 8 + tc:9 + tc],
                inv_b=c.fb, s_col=cl[:, 16 + tc:17 + tc],
                rs_slot=cl[:, 40 + tc:41 + tc],
                width=4096, qdst=QG, tcl=tcl, use_ln=True, ti=ti)
            ti += 1
        barrier(c)
        if fstop == "quantg":
            return
        rn = _load_row(c, c.swg, 2048)
        _bcast_row(c, c.p4, 2048, rn)
        _ffn_mm(c, c.wgT, nsl=4, nk=32, qbase=QG, th=th, evac="gate", gsbase=GS)
        barrier(c)
        if fstop == "gate":
            return
        rn = _load_row(c, c.invu, 4096)
        _bcast_row(c, c.fb, 4096, rn)
        for tcl in range(4):
            tc = th * 4 + tcl
            _ffn_quant_tile(
                c, src=c.big[:, XRES + tc * 4096:XRES + (tc + 1) * 4096],
                a_col=cl[:, tc:tc + 1], b_col=cl[:, 8 + tc:9 + tc],
                inv_b=c.fb, s_col=cl[:, 24 + tc:25 + tc],
                rs_slot=cl[:, 48 + tc:49 + tc],
                width=4096, qdst=QG, tcl=tcl, use_ln=True, ti=ti)
            ti += 1
        barrier(c)
        rn = _load_row(c, c.swu, 2048)
        _bcast_row(c, c.p4, 2048, rn)
        _ffn_mm(c, c.wuT, nsl=4, nk=32, qbase=QG, th=th, evac="up",
                gsbase=GS, intbase=INT)
        barrier(c)
        if fstop == "up":
            return
        rn = _load_row(c, c.invi, 2048)
        _bcast_row(c, c.p4, 2048, rn)
        for tcl in range(4):
            tc = th * 4 + tcl
            _ffn_quant_tile(
                c, src=c.big[:, INT + tcl * 2048:INT + (tcl + 1) * 2048],
                a_col=None, b_col=None, inv_b=c.p4,
                s_col=cl[:, 32 + tc:33 + tc], rs_slot=cl[:, 56 + tc:57 + tc],
                width=2048, qdst=QG, tcl=tcl, use_ln=False, ti=ti)
            ti += 1
        barrier(c)
        if fstop == "requant":
            return
        rn = _load_row(c, c.swd, 4096)
        _bcast_row(c, c.fb, 4096, rn)
        _ffn_mm(c, c.wdT, nsl=8, nk=16, qbase=QG, th=th, evac="down",
                emit_out=emit_out)
        barrier(c)


def _ffn_quant_tile(c, src, a_col, b_col, inv_b, s_col, rs_slot, width, qdst,
                    tcl, use_ln, ti):
    sm = c.sem
    p = ti % 2
    xs = c.wbuf[:, p * 4096:p * 4096 + width]
    MAGIC = 12582912.0
    prev = getattr(c, "_fqt_prev", {})
    ndv = inc(c, "dve", 4 if use_ln else 3)

    def v(eng, ndv=ndv, p=p, use_ln=use_ln, pw=prev.get(("w", p)),
          nops=(4 if use_ln else 3)):
        d = ndv - nops
        if pw is not None:
            eng.wait_ge(sm["mm"], pw)
        if use_ln:
            eng.scalar_tensor_tensor(out=xs, in0=src, scalar=a_col, in1=inv_b[:, 0:width],
                                     op0=ALU.mult, op1=ALU.mult).then_inc(sm["dve"], 1)
            d += 1
            eng.wait_ge(sm["dve"], d)
            eng.scalar_tensor_tensor(out=xs, in0=inv_b[:, 0:width], scalar=b_col,
                                     in1=xs, op0=ALU.mult,
                                     op1=ALU.add).then_inc(sm["dve"], 1)
            d += 1
            eng.wait_ge(sm["dve"], d)
        else:
            eng.tensor_tensor(out=xs, in0=src, in1=inv_b[:, 0:width],
                              op=ALU.mult).then_inc(sm["dve"], 1)
            d += 1
            eng.wait_ge(sm["dve"], d)
        eng.tensor_reduce(out=s_col, in_=xs, axis=mybir.AxisListType.X,
                          op=ALU.max, apply_absolute_value=True).then_inc(sm["dve"], 1)
        d += 1
        eng.wait_ge(sm["dve"], d)
        eng.tensor_scalar(out=s_col, in0=s_col, scalar1=1.0 / 127.0,
                          scalar2=1e-8, op0=ALU.mult,
                          op1=ALU.max).then_inc(sm["dve"], 1)
    c.V(v)
    ndv2 = inc(c, "dve")

    def v2(eng, ndv2=ndv2):
        eng.wait_ge(sm["dve"], ndv2 - 1)
        eng.reciprocal(out=rs_slot, in_=s_col).then_inc(sm["dve"], 1)
    c.V(v2)

    for fh in range(width // 1024):
        fp = (ti * (width // 1024) + fh) % 2
        fcb = c.fcc[:, fp * 1024:(fp + 1) * 1024]
        qtm = c.stg[:, fp * 2048:fp * 2048 + 1024]
        ndq = inc(c, "dve", 3)

        def vq(eng, fh=fh, fp=fp, fcb=fcb, qtm=qtm, ndq=ndq, ndv2=ndv2,
               do0=c.cnt["dO"], pq=prev.get(("q", fp))):
            d = ndq - 3
            eng.wait_ge(sm["dve"], ndv2)
            eng.wait_ge(sm["dO"], do0)
            if pq is not None:
                eng.wait_ge(sm["mm"], pq)
            eng.tensor_scalar(out=fcb, in0=xs[:, fh * 1024:(fh + 1) * 1024],
                              scalar1=rs_slot, scalar2=None,
                              op0=ALU.mult).then_inc(sm["dve"], 1)
            d += 1
            eng.wait_ge(sm["dve"], d)
            eng.tensor_scalar(out=fcb, in0=fcb, scalar1=MAGIC,
                              scalar2=MAGIC, op0=ALU.add,
                              op1=ALU.subtract).then_inc(sm["dve"], 1)
            d += 1
            eng.wait_ge(sm["dve"], d)
            eng.tensor_scalar(out=qtm, in0=fcb,
                              scalar1=127.0, scalar2=-127.0, op0=ALU.min,
                              op1=ALU.max).then_inc(sm["dve"], 1)
        c.V(vq)
        nmm = inc(c, "mm")

        def tq(eng, fp=fp, qtm=qtm, ndq=ndq, nmm=nmm,
               pv=prev.get(("pb", fp))):
            eng.wait_ge(sm["dve"], ndq)
            if pv is not None:
                eng.wait_ge(sm["dve"], pv)
            for j in range(8):
                ti_ = eng.transpose(c.Pb[:, fp * 1024 + j * 128:fp * 1024 + (j + 1) * 128],
                                    qtm[:, j * 128:(j + 1) * 128], c.identb)
            ti_.then_inc(sm["mm"], 1)
        c.T(tq)
        ndq2 = inc(c, "dve")

        def vq2(eng, fh=fh, fp=fp, nmm=nmm, ndq2=ndq2, qdst=qdst, tcl=tcl):
            eng.wait_ge(sm["mm"], nmm)
            for j in range(8):
                fc = fh * 8 + j
                cpi = eng.tensor_copy(
                    out=c.big[:, qdst + fc * 512 + tcl * 128:
                              qdst + fc * 512 + (tcl + 1) * 128],
                    in_=c.Pb[:, fp * 1024 + j * 128:fp * 1024 + (j + 1) * 128])
            cpi.then_inc(sm["dve"], 1)
        c.V(vq2)
        prev[("q", fp)] = nmm
        prev[("pb", fp)] = ndq2
        prev[("w", p)] = nmm
    c._fqt_prev = prev


def _ffn_mm(c, wblocks, nsl, nk, qbase, th, evac, emit_out=False, gsbase=None,
            intbase=None):
    sm = c.sem
    cl = c.cols
    nld = nk // 8
    slots = [c.wbuf[:, 0:4096], c.wbuf[:, 4096:8192], c.stg[:, 0:4096]]
    ssem = ["dWa", "dWb", "dSa"]
    QB0 = c.cnt["mm"]
    ev_last = {}
    gq = 0
    for sl in range(nsl):
        wq_t = []
        loads = []
        for l in range(nld):
            g = gq + l
            wq_t.append((ssem[g % 3], inc(c, ssem[g % 3], 16)))
            loads.append((l, g))

        def gw(eng, sl=sl, loads=loads, wq_t=wq_t, QB0=QB0):
            for l, g in loads:
                if g >= 2:
                    eng.wait_ge(sm["mm"], QB0 + g - 1)
                eng.dma_start(
                    out=slots[g % 3][:, 0:4096]
                    .rearrange("p (f q) -> p f q", f=8),
                    in_=wblocks[sl, l * 8:(l + 1) * 8]
                    .rearrange("f p q -> p f q")
                    ).then_inc(sm[wq_t[l][0]], 16)
        c.G(gw)

        def tchain(eng, sl=sl, loads=loads, wq_t=wq_t, qbase=qbase,
                   evw=dict(ev_last)):
            for t4, tgt in evw.items():
                eng.wait_ge(sm[tgt[0]], tgt[1])
            for l, g in loads:
                eng.wait_ge(sm[wq_t[l][0]], wq_t[l][1])
                for tb4 in range(4):
                    for j in range(8):
                        fc = l * 8 + j
                        mmi = eng.matmul(
                            c.P1[tb4][:, :],
                            lhsT=c.big[:, qbase + fc * 512 + tb4 * 128:
                                       qbase + fc * 512 + (tb4 + 1) * 128],
                            rhs=slots[g % 3][:, j * 512:(j + 1) * 512],
                            start=(l == 0 and j == 0), stop=(l == nld - 1 and j == 7))
                mmi.then_inc(sm["mm"], 1)
        c.T(tchain)
        mm_last = inc(c, "mm", nld)
        gq += nld

        for tb4 in range(4):
            tc = th * 4 + tb4
            if evac == "gate":
                ndv = inc(c, "dve")

                def vg(eng, sl=sl, tb4=tb4, tc=tc, mm_last=mm_last, ndv=ndv,
                       pag=ev_last.get(tb4)):
                    eng.wait_ge(sm["mm"], mm_last)
                    if pag is not None:
                        eng.wait_ge(sm[pag[0]], pag[1])
                    eng.scalar_tensor_tensor(
                        out=c.fa[:, (tb4 % 4) * 512:(tb4 % 4 + 1) * 512],
                        in0=c.P1[tb4][:, :], scalar=cl[:, 16 + tc:17 + tc],
                        in1=c.p4[:, sl * 512:(sl + 1) * 512], op0=ALU.mult,
                        op1=ALU.mult).then_inc(sm["dve"], 1)
                c.V(vg)
                na = inc(c, "act")

                def ag(eng, sl=sl, tb4=tb4, ndv=ndv, na=na, gsbase=gsbase):
                    eng.wait_ge(sm["dve"], ndv)
                    eng.activation(
                        out=c.big[:, gsbase + tb4 * 2048 + sl * 512:
                                  gsbase + tb4 * 2048 + (sl + 1) * 512],
                        in_=c.fa[:, (tb4 % 4) * 512:(tb4 % 4 + 1) * 512],
                        func=AF.Silu).then_inc(sm["act"], 1)
                c.A(ag)
                ev_last[tb4] = ("act", na)
            elif evac == "up":
                ndv = inc(c, "dve", 3)

                def vu(eng, sl=sl, tb4=tb4, tc=tc, mm_last=mm_last, ndv=ndv,
                       gsbase=gsbase, intbase=intbase):
                    d = ndv - 3
                    eng.wait_ge(sm["mm"], mm_last)
                    eng.wait_ge(sm["dve"], d)
                    eng.scalar_tensor_tensor(
                        out=c.fa[:, (tb4 % 4) * 512:(tb4 % 4 + 1) * 512],
                        in0=c.P1[tb4][:, :], scalar=cl[:, 24 + tc:25 + tc],
                        in1=c.p4[:, sl * 512:(sl + 1) * 512], op0=ALU.mult,
                        op1=ALU.mult).then_inc(sm["dve"], 1)
                    d += 1
                    eng.wait_ge(sm["dve"], d)
                    eng.tensor_tensor(
                        out=c.fcc[:, 0:512],
                        in0=c.fa[:, (tb4 % 4) * 512:(tb4 % 4 + 1) * 512],
                        in1=c.big[:, gsbase + tb4 * 2048 + sl * 512:
                                  gsbase + tb4 * 2048 + (sl + 1) * 512],
                        op=ALU.mult).then_inc(sm["dve"], 1)
                    d += 1
                    eng.wait_ge(sm["dve"], d)
                    eng.tensor_scalar(
                        out=c.big[:, intbase + tb4 * 2048 + sl * 512:
                                  intbase + tb4 * 2048 + (sl + 1) * 512],
                        in0=c.fcc[:, 0:512], scalar1=10.0, scalar2=-10.0,
                        op0=ALU.min, op1=ALU.max).then_inc(sm["dve"], 1)
                c.V(vu)
                ev_last[tb4] = ("dve", ndv)
            else:
                ndv = inc(c, "dve", 2)
                prev_yd0 = getattr(c, "_ffn_prev_yd", None)

                def vd(eng, sl=sl, tb4=tb4, tc=tc, mm_last=mm_last, ndv=ndv,
                       prev_yd0=prev_yd0):
                    d = ndv - 2
                    eng.wait_ge(sm["mm"], mm_last)
                    eng.wait_ge(sm["dve"], d)
                    if prev_yd0 is not None:
                        eng.wait_ge(sm["dO"], prev_yd0)
                    eng.scalar_tensor_tensor(
                        out=c.fcc[:, 0:512],
                        in0=c.P1[tb4][:, :], scalar=cl[:, 32 + tc:33 + tc],
                        in1=c.fb[:, sl * 512:(sl + 1) * 512], op0=ALU.mult,
                        op1=ALU.mult).then_inc(sm["dve"], 1)
                    d += 1
                    eng.wait_ge(sm["dve"], d)
                    eng.tensor_tensor(
                        out=c.fcc[:, 512:1024], in0=c.fcc[:, 0:512],
                        in1=c.big[:, tc * 4096 + sl * 512:tc * 4096 + (sl + 1) * 512],
                        op=ALU.add).then_inc(sm["dve"], 1)
                c.V(vd)
                ndv2 = inc(c, "dve")

                def vd2(eng, sl=sl, tc=tc, ndv=ndv, ndv2=ndv2):
                    eng.wait_ge(sm["dve"], ndv)
                    eng.tensor_copy(
                        out=c.big[:, tc * 4096 + sl * 512:tc * 4096 + (sl + 1) * 512],
                        in_=c.fcc[:, 512:1024]).then_inc(sm["dve"], 1)
                c.V(vd2)
                ev_last[tb4] = ("dve", ndv2)
                if emit_out:
                    nO = inc(c, "dO", 16)

                    def yd(eng, sl=sl, tc=tc, ndv2=ndv2):
                        eng.wait_ge(sm["dve"], ndv2)
                        eng.dma_start(
                            out=c.out[tc * 128:(tc + 1) * 128, sl * 512:(sl + 1) * 512],
                            in_=c.fcc[:, 512:1024]).then_inc(sm["dO"], 16)
                    c.Y(yd)
                    c._ffn_prev_yd = nO
    barrier(c)


def _ln_stats(c):
    sm = c.sem
    cl = c.cols
    for tc in range(8):
        ndv = inc(c, "dve", 3)

        def vst(eng, tc=tc, ndv=ndv):
            d = ndv - 3
            for j in range(8):
                bi = eng.bn_stats(out=c.fa[0:128, j * 6:(j + 1) * 6],
                                  in_=c.big[:, tc * 4096 + j * 512:
                                            tc * 4096 + (j + 1) * 512])
            bi.then_inc(sm["dve"], 1)
            d += 1
            eng.wait_ge(sm["dve"], d)
            eng.bn_aggr(out=c.fa[:, 48:50], in_=c.fa[:, 0:48]).then_inc(sm["dve"], 1)
            d += 1
            eng.wait_ge(sm["dve"], d)
            eng.tensor_copy(out=cl[:, 64 + tc:65 + tc],
                            in_=c.fa[:, 48:49]).then_inc(sm["dve"], 1)
        c.V(vst)
        na = inc(c, "act")

        def ast(eng, tc=tc, ndv=ndv, na=na):
            eng.wait_ge(sm["dve"], ndv)
            eng.activation(out=c.fa[:, 50:51], in_=c.fa[:, 49:50], func=AF.Sqrt,
                           bias=cl[:, 148:149]).then_inc(sm["act"], 1)
        c.A(ast)
        ndv2 = inc(c, "dve", 3)

        def vst2(eng, tc=tc, na=na, ndv2=ndv2):
            d = ndv2 - 3
            eng.wait_ge(sm["act"], na)
            eng.reciprocal(out=cl[:, 0 + tc:1 + tc],
                           in_=c.fa[:, 50:51]).then_inc(sm["dve"], 1)
            d += 1
            eng.wait_ge(sm["dve"], d)
            eng.tensor_tensor(out=c.fa[:, 51:52], in0=cl[:, 64 + tc:65 + tc],
                              in1=cl[:, 0 + tc:1 + tc],
                              op=ALU.mult).then_inc(sm["dve"], 1)
            d += 1
            eng.wait_ge(sm["dve"], d)
            eng.tensor_scalar(out=cl[:, 8 + tc:9 + tc], in0=c.fa[:, 51:52],
                              scalar1=-1.0, scalar2=None,
                              op0=ALU.mult).then_inc(sm["dve"], 1)
        c.V(vst2)
    barrier(c)

def emit_ffn(c, tag, emit_out):
    sm, ct = c.sem, c.cnt
    XRES = 0
    QG = 32768
    GS = 49152
    INT = 57344
    cl = c.cols
    barrier(c)

    _ln_stats(c)

    fstop = os.environ.get("FFN_STOP", "")
    ti = 0
    for th in range(1 if fstop else 2):
        rn = _load_row(c, c.invg, 4096)
        _bcast_row(c, c.fb, 4096, rn)
        for tcl in range(4):
            tc = th * 4 + tcl
            _ffn_quant_tile(
                c, src=c.big[:, XRES + tc * 4096:XRES + (tc + 1) * 4096],
                a_col=cl[:, tc:tc + 1], b_col=cl[:, 8 + tc:9 + tc],
                inv_b=c.fb, s_col=cl[:, 16 + tc:17 + tc],
                rs_slot=cl[:, 40 + tc:41 + tc],
                width=4096, qdst=QG, tcl=tcl, use_ln=True, ti=ti)
            ti += 1
        barrier(c)
        if fstop == "quantg":
            return
        rn = _load_row(c, c.swg, 2048)
        _bcast_row(c, c.p4, 2048, rn)
        _ffn_mm(c, c.wgT, nsl=4, nk=32, qbase=QG, th=th, evac="gate", gsbase=GS)
        barrier(c)
        if fstop == "gate":
            return
        rn = _load_row(c, c.invu, 4096)
        _bcast_row(c, c.fb, 4096, rn)
        for tcl in range(4):
            tc = th * 4 + tcl
            _ffn_quant_tile(
                c, src=c.big[:, XRES + tc * 4096:XRES + (tc + 1) * 4096],
                a_col=cl[:, tc:tc + 1], b_col=cl[:, 8 + tc:9 + tc],
                inv_b=c.fb, s_col=cl[:, 24 + tc:25 + tc],
                rs_slot=cl[:, 48 + tc:49 + tc],
                width=4096, qdst=QG, tcl=tcl, use_ln=True, ti=ti)
            ti += 1
        barrier(c)
        rn = _load_row(c, c.swu, 2048)
        _bcast_row(c, c.p4, 2048, rn)
        _ffn_mm(c, c.wuT, nsl=4, nk=32, qbase=QG, th=th, evac="up",
                gsbase=GS, intbase=INT)
        barrier(c)
        if fstop == "up":
            return
        rn = _load_row(c, c.invi, 2048)
        _bcast_row(c, c.p4, 2048, rn)
        for tcl in range(4):
            tc = th * 4 + tcl
            _ffn_quant_tile(
                c, src=c.big[:, INT + tcl * 2048:INT + (tcl + 1) * 2048],
                a_col=None, b_col=None, inv_b=c.p4,
                s_col=cl[:, 32 + tc:33 + tc], rs_slot=cl[:, 56 + tc:57 + tc],
                width=2048, qdst=QG, tcl=tcl, use_ln=False, ti=ti)
            ti += 1
        barrier(c)
        if fstop == "requant":
            return
        rn = _load_row(c, c.swd, 4096)
        _bcast_row(c, c.fb, 4096, rn)
        _ffn_mm(c, c.wdT, nsl=8, nk=16, qbase=QG, th=th, evac="down",
                emit_out=emit_out)
        barrier(c)


def _ffn_quant_tile(c, src, a_col, b_col, inv_b, s_col, rs_slot, width, qdst,
                    tcl, use_ln, ti):
    sm = c.sem
    p = ti % 2
    xs = c.wbuf[:, p * 4096:p * 4096 + width]
    MAGIC = 12582912.0
    prev = getattr(c, "_fqt_prev", {})
    ndv = inc(c, "dve", 4 if use_ln else 3)

    def v(eng, ndv=ndv, p=p, use_ln=use_ln, pw=prev.get(("w", p)),
          nops=(4 if use_ln else 3)):
        d = ndv - nops
        if pw is not None:
            eng.wait_ge(sm["mm"], pw)
        if use_ln:
            eng.scalar_tensor_tensor(out=xs, in0=src, scalar=a_col, in1=inv_b[:, 0:width],
                                     op0=ALU.mult, op1=ALU.mult).then_inc(sm["dve"], 1)
            d += 1
            eng.wait_ge(sm["dve"], d)
            eng.scalar_tensor_tensor(out=xs, in0=inv_b[:, 0:width], scalar=b_col,
                                     in1=xs, op0=ALU.mult,
                                     op1=ALU.add).then_inc(sm["dve"], 1)
            d += 1
            eng.wait_ge(sm["dve"], d)
        else:
            eng.tensor_tensor(out=xs, in0=src, in1=inv_b[:, 0:width],
                              op=ALU.mult).then_inc(sm["dve"], 1)
            d += 1
            eng.wait_ge(sm["dve"], d)
        eng.tensor_reduce(out=s_col, in_=xs, axis=mybir.AxisListType.X,
                          op=ALU.max, apply_absolute_value=True).then_inc(sm["dve"], 1)
        d += 1
        eng.wait_ge(sm["dve"], d)
        eng.tensor_scalar(out=s_col, in0=s_col, scalar1=1.0 / 127.0,
                          scalar2=1e-8, op0=ALU.mult,
                          op1=ALU.max).then_inc(sm["dve"], 1)
    c.V(v)
    ndv2 = inc(c, "dve")

    def v2(eng, ndv2=ndv2):
        eng.wait_ge(sm["dve"], ndv2 - 1)
        eng.reciprocal(out=rs_slot, in_=s_col).then_inc(sm["dve"], 1)
    c.V(v2)

    for fh in range(width // 1024):
        fp = (ti * (width // 1024) + fh) % 2
        fcb = c.fcc[:, fp * 1024:(fp + 1) * 1024]
        qtm = c.stg[:, fp * 2048:fp * 2048 + 1024]
        ndq = inc(c, "dve", 3)

        def vq(eng, fh=fh, fp=fp, fcb=fcb, qtm=qtm, ndq=ndq, ndv2=ndv2,
               do0=c.cnt["dO"], pq=prev.get(("q", fp))):
            d = ndq - 3
            eng.wait_ge(sm["dve"], ndv2)
            eng.wait_ge(sm["dO"], do0)
            if pq is not None:
                eng.wait_ge(sm["mm"], pq)
            eng.tensor_scalar(out=fcb, in0=xs[:, fh * 1024:(fh + 1) * 1024],
                              scalar1=rs_slot, scalar2=None,
                              op0=ALU.mult).then_inc(sm["dve"], 1)
            d += 1
            eng.wait_ge(sm["dve"], d)
            eng.tensor_scalar(out=fcb, in0=fcb, scalar1=MAGIC,
                              scalar2=MAGIC, op0=ALU.add,
                              op1=ALU.subtract).then_inc(sm["dve"], 1)
            d += 1
            eng.wait_ge(sm["dve"], d)
            eng.tensor_scalar(out=qtm, in0=fcb,
                              scalar1=127.0, scalar2=-127.0, op0=ALU.min,
                              op1=ALU.max).then_inc(sm["dve"], 1)
        c.V(vq)
        nmm = inc(c, "mm")

        def tq(eng, fp=fp, qtm=qtm, ndq=ndq, nmm=nmm,
               pv=prev.get(("pb", fp))):
            eng.wait_ge(sm["dve"], ndq)
            if pv is not None:
                eng.wait_ge(sm["dve"], pv)
            for j in range(8):
                ti_ = eng.transpose(c.Pb[:, fp * 1024 + j * 128:fp * 1024 + (j + 1) * 128],
                                    qtm[:, j * 128:(j + 1) * 128], c.identb)
            ti_.then_inc(sm["mm"], 1)
        c.T(tq)
        ndq2 = inc(c, "dve")

        def vq2(eng, fh=fh, fp=fp, nmm=nmm, ndq2=ndq2, qdst=qdst, tcl=tcl):
            eng.wait_ge(sm["mm"], nmm)
            for j in range(8):
                fc = fh * 8 + j
                cpi = eng.tensor_copy(
                    out=c.big[:, qdst + fc * 512 + tcl * 128:
                              qdst + fc * 512 + (tcl + 1) * 128],
                    in_=c.Pb[:, fp * 1024 + j * 128:fp * 1024 + (j + 1) * 128])
            cpi.then_inc(sm["dve"], 1)
        c.V(vq2)
        prev[("q", fp)] = nmm
        prev[("pb", fp)] = ndq2
        prev[("w", p)] = nmm
    c._fqt_prev = prev


def _ffn_mm(c, wblocks, nsl, nk, qbase, th, evac, emit_out=False, gsbase=None,
            intbase=None):
    sm = c.sem
    cl = c.cols
    nld = nk // 8
    wsem = ["dWa", "dWb"]
    prev = getattr(c, "_fmm_prev", None)
    for sl in range(nsl):
        mm_q0 = c.cnt["mm"]
        first = (prev is None and sl == 0)

        def gw(eng, sl=sl, mm_q0=mm_q0, first=first):
            for l in range(nld):
                if l >= 2:
                    eng.wait_ge(sm["mm"], mm_q0 + l)
                elif not first:
                    eng.wait_ge(sm["mm"], mm_q0)
                eng.dma_start(
                    out=c.wbuf[:, (l % 2) * 4096:(l % 2 + 1) * 4096]
                    .rearrange("p (f q) -> p f q", f=8),
                    in_=wblocks[sl, l * 8:(l + 1) * 8]
                    .rearrange("f p q -> p f q")
                    ).then_inc(sm[wsem[l % 2]], 16)
        c.G(gw)
        wq_t = [inc(c, wsem[l % 2], 16) for l in range(nld)]

        def tchain(eng, sl=sl, wq_t=wq_t, qbase=qbase, th=th):
            for l in range(nld):
                eng.wait_ge(sm[wsem[l % 2]], wq_t[l])
                for tb4 in range(4):
                    for j in range(8):
                        fc = l * 8 + j
                        mmi = eng.matmul(
                            c.P1[tb4][:, :],
                            lhsT=c.big[:, qbase + fc * 512 + tb4 * 128:
                                       qbase + fc * 512 + (tb4 + 1) * 128],
                            rhs=c.wbuf[:, (l % 2) * 4096 + j * 512:
                                       (l % 2) * 4096 + (j + 1) * 512],
                            start=(l == 0 and j == 0), stop=(l == nld - 1 and j == 7))
                mmi.then_inc(sm["mm"], 1)
        c.T(tchain)
        mm_last = inc(c, "mm", nld)
        prev = mm_last
        c._fmm_prev = None

        for tb4 in range(4):
            tc = th * 4 + tb4
            if evac == "gate":
                ndv = inc(c, "dve")

                def vg(eng, sl=sl, tb4=tb4, tc=tc, mm_last=mm_last, ndv=ndv):
                    eng.wait_ge(sm["mm"], mm_last)
                    eng.scalar_tensor_tensor(
                        out=c.fa[:, (tb4 % 4) * 512:(tb4 % 4 + 1) * 512],
                        in0=c.P1[tb4][:, :], scalar=cl[:, 16 + tc:17 + tc],
                        in1=c.p4[:, sl * 512:(sl + 1) * 512], op0=ALU.mult,
                        op1=ALU.mult).then_inc(sm["dve"], 1)
                c.V(vg)
                na = inc(c, "act")

                def ag(eng, sl=sl, tb4=tb4, ndv=ndv, na=na):
                    eng.wait_ge(sm["dve"], ndv)
                    eng.activation(
                        out=c.big[:, gsbase + tb4 * 2048 + sl * 512:
                                  gsbase + tb4 * 2048 + (sl + 1) * 512],
                        in_=c.fa[:, (tb4 % 4) * 512:(tb4 % 4 + 1) * 512],
                        func=AF.Silu).then_inc(sm["act"], 1)
                c.A(ag)
            elif evac == "up":
                ndv = inc(c, "dve", 3)

                def vu(eng, sl=sl, tb4=tb4, tc=tc, mm_last=mm_last, ndv=ndv):
                    d = ndv - 3
                    eng.wait_ge(sm["mm"], mm_last)
                    eng.scalar_tensor_tensor(
                        out=c.fa[:, (tb4 % 4) * 512:(tb4 % 4 + 1) * 512],
                        in0=c.P1[tb4][:, :], scalar=cl[:, 24 + tc:25 + tc],
                        in1=c.p4[:, sl * 512:(sl + 1) * 512], op0=ALU.mult,
                        op1=ALU.mult).then_inc(sm["dve"], 1)
                    d += 1
                    eng.wait_ge(sm["dve"], d)
                    eng.tensor_tensor(
                        out=c.fcc[:, 0:512],
                        in0=c.fa[:, (tb4 % 4) * 512:(tb4 % 4 + 1) * 512],
                        in1=c.big[:, gsbase + tb4 * 2048 + sl * 512:
                                  gsbase + tb4 * 2048 + (sl + 1) * 512],
                        op=ALU.mult).then_inc(sm["dve"], 1)
                    d += 1
                    eng.wait_ge(sm["dve"], d)
                    eng.tensor_scalar(
                        out=c.big[:, intbase + tb4 * 2048 + sl * 512:
                                  intbase + tb4 * 2048 + (sl + 1) * 512],
                        in0=c.fcc[:, 0:512], scalar1=10.0, scalar2=-10.0,
                        op0=ALU.min, op1=ALU.max).then_inc(sm["dve"], 1)
                c.V(vu)
            else:
                ndv = inc(c, "dve", 2)
                prev_yd0 = getattr(c, "_ffn_prev_yd", None)

                def vd(eng, sl=sl, tb4=tb4, tc=tc, mm_last=mm_last, ndv=ndv,
                       emit_out=emit_out, prev_yd0=prev_yd0):
                    d = ndv - 2
                    eng.wait_ge(sm["mm"], mm_last)
                    eng.wait_ge(sm["dve"], d)
                    if prev_yd0 is not None:
                        eng.wait_ge(sm["dO"], prev_yd0)
                    eng.scalar_tensor_tensor(
                        out=c.fcc[:, 0:512],
                        in0=c.P1[tb4][:, :], scalar=cl[:, 32 + tc:33 + tc],
                        in1=c.fb[:, sl * 512:(sl + 1) * 512], op0=ALU.mult,
                        op1=ALU.mult).then_inc(sm["dve"], 1)
                    d += 1
                    eng.wait_ge(sm["dve"], d)
                    eng.tensor_tensor(
                        out=c.fcc[:, 512:1024], in0=c.fcc[:, 0:512],
                        in1=c.big[:, tc * 4096 + sl * 512:tc * 4096 + (sl + 1) * 512],
                        op=ALU.add).then_inc(sm["dve"], 1)
                c.V(vd)
                ndv2 = inc(c, "dve")
                prev_yd = getattr(c, "_ffn_prev_yd", None)

                def vd2(eng, sl=sl, tc=tc, ndv=ndv, ndv2=ndv2, prev_yd=prev_yd):
                    eng.wait_ge(sm["dve"], ndv)
                    if prev_yd is not None:
                        eng.wait_ge(sm["dO"], prev_yd)
                    eng.tensor_copy(
                        out=c.big[:, tc * 4096 + sl * 512:tc * 4096 + (sl + 1) * 512],
                        in_=c.fcc[:, 512:1024]).then_inc(sm["dve"], 1)
                c.V(vd2)
                if emit_out:
                    nO = inc(c, "dO", 16)

                    def yd(eng, sl=sl, tc=tc, ndv2=ndv2):
                        eng.wait_ge(sm["dve"], ndv2)
                        eng.dma_start(
                            out=c.out[tc * 128:(tc + 1) * 128, sl * 512:(sl + 1) * 512],
                            in_=c.fcc[:, 512:1024]).then_inc(sm["dO"], 16)
                    c.Y(yd)
                    c._ffn_prev_yd = nO
        barrier(c)



def _attn_b_normrope(c, buf, isk):
    sm = c.sem
    H2 = 65456
    CB1 = 64560
    MN = 63536
    ndv1 = inc(c, "dve")

    def v1(eng, ndv1=ndv1):
        eng.tensor_tensor(out=c.stg[:, 0:1024], in0=buf, in1=buf,
                          op=ALU.mult).then_inc(sm["dve"], 1)
    c.V(v1)
    nmm1 = inc(c, "mm")

    def t1(eng, ndv1=ndv1, nmm1=nmm1):
        eng.wait_ge(sm["dve"], ndv1)
        for nq in range(2):
            mmi = eng.matmul(c.Pf[0:2, nq * 512:(nq + 1) * 512],
                             lhsT=c.big[:, H2:H2 + 2],
                             rhs=c.stg[:, nq * 512:(nq + 1) * 512],
                             start=True, stop=True)
        mmi.then_inc(sm["mm"], 1)
    c.T(t1)
    na1 = inc(c, "act")

    def a1(eng, nmm1=nmm1, na1=na1):
        eng.wait_ge(sm["mm"], nmm1)
        eng.activation(out=c.fcc[0:2, 0:1024], in_=c.Pf[0:2, 0:1024],
                       func=AF.Sqrt).then_inc(sm["act"], 1)
    c.A(a1)
    ndv2 = inc(c, "dve", 3)

    def v2(eng, na1=na1, ndv2=ndv2, isk=isk):
        d = ndv2 - 3
        eng.wait_ge(sm["act"], na1)
        eng.tensor_scalar(out=c.fcc[0:2, 0:1024], in0=c.fcc[0:2, 0:1024],
                          scalar1=1e-12, scalar2=None,
                          op0=ALU.max).then_inc(sm["dve"], 1)
        d += 1
        eng.wait_ge(sm["dve"], d)
        eng.reciprocal(out=c.fcc[0:2, 1024:2048],
                       in_=c.fcc[0:2, 0:1024]).then_inc(sm["dve"], 1)
        d += 1
        eng.wait_ge(sm["dve"], d)
        if isk:
            eng.tensor_scalar(out=c.fcc[0:2, 1024:2048],
                              in0=c.fcc[0:2, 1024:2048],
                              scalar1=c.g_ktemp[0:2, 0:1], scalar2=SC_BH,
                              op0=ALU.mult, op1=ALU.mult).then_inc(sm["dve"], 1)
        else:
            eng.tensor_scalar(out=c.fcc[0:2, 1024:2048],
                              in0=c.fcc[0:2, 1024:2048],
                              scalar1=SC_BH, scalar2=None,
                              op0=ALU.mult).then_inc(sm["dve"], 1)
    c.V(v2)
    ndv2b = inc(c, "dve")

    def v2b(eng, ndv2=ndv2, ndv2b=ndv2b):
        eng.wait_ge(sm["dve"], ndv2)
        eng.tensor_copy(out=c.stg2[0:2, 1024:2048],
                        in_=c.fcc[0:2, 1024:2048]).then_inc(sm["dve"], 1)
    c.V(v2b)
    nmm2 = inc(c, "mm")

    def t2(eng, ndv2b=ndv2b, nmm2=nmm2):
        eng.wait_ge(sm["dve"], ndv2b)
        eng.matmul(c.P1[2][:, :], lhsT=c.big[0:2, CB1 + 768:CB1 + 896],
                   rhs=c.stg2[0:2, 1024:1536], start=True, stop=True)
        eng.matmul(c.P1[3][:, :], lhsT=c.big[0:2, CB1 + 768:CB1 + 896],
                   rhs=c.stg2[0:2, 1536:2048], start=True,
                   stop=True).then_inc(sm["mm"], 1)
    c.T(t2)
    ndv3 = inc(c, "dve", 4)

    def v3(eng, nmm2=nmm2, ndv3=ndv3):
        d = ndv3 - 4
        eng.wait_ge(sm["mm"], nmm2)
        eng.tensor_copy(out=c.stg[:, 1024:1536], in_=c.P1[2][:, :]).then_inc(sm["dve"], 1)
        d += 1
        eng.tensor_copy(out=c.stg[:, 1536:2048], in_=c.P1[3][:, :]).then_inc(sm["dve"], 1)
        d += 1
        eng.wait_ge(sm["dve"], d)
        eng.tensor_tensor(out=buf, in0=buf, in1=c.stg[:, 1024:2048],
                          op=ALU.mult).then_inc(sm["dve"], 1)
        d += 1
        eng.wait_ge(sm["dve"], d)
        eng.nop().then_inc(sm["dve"], 1)
    c.V(v3)
    nmm3 = inc(c, "mm")

    def t3(eng, ndv3=ndv3, nmm3=nmm3):
        eng.wait_ge(sm["dve"], ndv3)
        eng.matmul(c.P1[0][:, :], lhsT=c.big[:, MN:MN + 128],
                   rhs=buf[:, 0:512], start=True, stop=True)
        eng.matmul(c.P1[1][:, :], lhsT=c.big[:, MN:MN + 128],
                   rhs=buf[:, 512:1024], start=True,
                   stop=True).then_inc(sm["mm"], 1)
    c.T(t3)
    ndv4 = inc(c, "dve", 5)

    def v4(eng, nmm3=nmm3, ndv4=ndv4):
        d = ndv4 - 5
        eng.wait_ge(sm["mm"], nmm3)
        eng.tensor_copy(out=c.stg2[:, 0:512], in_=c.P1[0][:, :]).then_inc(sm["dve"], 1)
        d += 1
        eng.tensor_copy(out=c.stg2[:, 512:1024], in_=c.P1[1][:, :]).then_inc(sm["dve"], 1)
        d += 1
        eng.wait_ge(sm["dve"], d)
        eng.tensor_tensor(out=c.stg[:, 1024:2048], in0=buf, in1=c.vbuf[:, 0:1024],
                          op=ALU.mult).then_inc(sm["dve"], 1)
        d += 1
        eng.wait_ge(sm["dve"], d)
        eng.tensor_tensor(out=c.stg2[:, 0:1024], in0=c.stg2[:, 0:1024],
                          in1=c.vbuf[:, 1024:2048], op=ALU.mult).then_inc(sm["dve"], 1)
        d += 1
        eng.wait_ge(sm["dve"], d)
        eng.tensor_tensor(out=buf, in0=c.stg[:, 1024:2048], in1=c.stg2[:, 0:1024],
                          op=ALU.add).then_inc(sm["dve"], 1)
    c.V(v4)
    barrier(c)


def emit_attn_b(c, emit_out):
    sm, ct = c.sem, c.cnt
    cl = c.cols
    XN = 32768
    KQ = 49152
    C1Q = 32768
    QH = 40976
    C1K = KQ + 10 * 1028
    VT = 61488
    MN = 63536
    CB1 = 64560
    QATT = 49152
    VA = 57344
    CX = 32768
    barrier(c)

    ncb = inc(c, "dW", 16 * 7)

    def gc1(eng):
        eng.dma_start(out=c.big[:, CB1:CB1 + 512]
                      .rearrange("p (a b) -> p a b", a=8),
                      in_=c.hsum16.rearrange("a p b -> p a b")).then_inc(sm["dW"], 16)
        eng.dma_start(out=c.big[:, CB1 + 512:CB1 + 640]
                      .rearrange("p (a b) -> p a b", a=2),
                      in_=c.hsum4.rearrange("a p b -> p a b")).then_inc(sm["dW"], 16)
        eng.dma_start(out=c.big[0:64, CB1 + 640:CB1 + 768],
                      in_=c.bc2).then_inc(sm["dW"], 16)
        eng.dma_start(out=c.big[64:128, CB1 + 640:CB1 + 768],
                      in_=c.bc2).then_inc(sm["dW"], 16)
        eng.dma_start(out=c.big[0:2, CB1 + 768:CB1 + 896],
                      in_=c.bch2).then_inc(sm["dW"], 16)
        eng.dma_start(out=c.kbuf[:, :].rearrange("p (j q) -> p j q", j=4),
                      in_=c.masks.rearrange("j p q -> p j q")).then_inc(sm["dW"], 16)
        eng.dma_start(out=c.big[:, 65456:65458],
                      in_=c.hsum2x64).then_inc(sm["dW"], 16)
    c.G(gc1)
    c._b_ncb = c.cnt["dW"]
    _ln_stats(c)

    for th in range(2):
        for tcl in range(4):
            tc = th * 4 + tcl
            na = inc(c, "act")

            def axn(eng, tc=tc, na=na, pmm=c.cnt["mm"]):
                eng.wait_ge(sm["mm"], pmm)
                eng.activation(out=c.stg[:, 0:4096],
                               in_=c.big[:, tc * 4096:(tc + 1) * 4096],
                               func=AF.Identity, scale=cl[:, tc:tc + 1],
                               bias=cl[:, 8 + tc:9 + tc]).then_inc(sm["act"], 1)
            c.A(axn)
            for fh in range(2):
                nmm = inc(c, "mm")

                def txn(eng, fh=fh, na=na, nmm=nmm, pdv=c.cnt["dve"],
                        ncb=c._b_ncb):
                    eng.wait_ge(sm["dW"], ncb)
                    eng.wait_ge(sm["act"], na)
                    eng.wait_ge(sm["dve"], pdv)
                    for j in range(16):
                        ti = eng.transpose(c.Pb[:, j * 128:(j + 1) * 128],
                                           c.stg[:, fh * 2048 + j * 128:
                                                 fh * 2048 + (j + 1) * 128], c.identb)
                    ti.then_inc(sm["mm"], 1)
                c.T(txn)
                ndv = inc(c, "dve")

                def vxn(eng, fh=fh, tcl=tcl, nmm=nmm, ndv=ndv):
                    eng.wait_ge(sm["mm"], nmm)
                    for j in range(16):
                        fc = fh * 16 + j
                        cpi = eng.tensor_copy(
                            out=c.big[:, XN + fc * 512 + tcl * 128:
                                      XN + fc * 512 + (tcl + 1) * 128],
                            in_=c.Pb[:, j * 128:(j + 1) * 128])
                    cpi.then_inc(sm["dve"], 1)
                c.V(vxn)
        barrier(c)
        ev_t = {}
        for j in range(12):
            nw = inc(c, "dWa" if j % 2 == 0 else "dWb", 16)

            def gw(eng, j=j, nw=nw, g2=(ev_t.get(j - 2))):
                if g2 is not None:
                    eng.wait_ge(sm["act"], g2)
                eng.dma_start(out=c.wbuf[:, (j % 2) * 4096:(j % 2 + 1) * 4096]
                              .rearrange("p (f q) -> p f q", f=32),
                              in_=c.wqbT[j].rearrange("f p q -> p f q")
                              ).then_inc(sm["dWa" if j % 2 == 0 else "dWb"], 16)
            c.G(gw)
            nmm = inc(c, "mm")

            def tq(eng, j=j, nw=nw, nmm=nmm, pact=(ev_t.get(j - 2))):
                eng.wait_ge(sm["dWa" if j % 2 == 0 else "dWb"], nw)
                if pact is not None:
                    eng.wait_ge(sm["act"], pact)
                for fc in range(32):
                    mmi = eng.matmul(
                        c.P1[j % 2][:, :],
                        lhsT=c.wbuf[:, (j % 2) * 4096 + fc * 128:
                                    (j % 2) * 4096 + (fc + 1) * 128],
                        rhs=c.big[:, XN + fc * 512:XN + fc * 512 + 512],
                        start=(fc == 0), stop=(fc == 31))
                mmi.then_inc(sm["mm"], 1)
            c.T(tq)
            na = inc(c, "act")

            def aq(eng, j=j, th=th, nmm=nmm, na=na):
                eng.wait_ge(sm["mm"], nmm)
                eng.activation(out=c.big[:, KQ + j * 1028 + 4 + th * 512:
                                         KQ + j * 1028 + 4 + (th + 1) * 512],
                               in_=c.P1[j % 2][:, :],
                               func=AF.Copy).then_inc(sm["act"], 1)
            c.A(aq)
            ev_t[j] = na
        barrier(c)
    barrier(c)

    def ypk(eng):
        for jc in range(12):
            eng.dma_start(
                out=c.ccB1_in[256 // 256 * 0 + jc // 2:jc // 2 + 1,
                              (jc % 2) * 512:(jc % 2) * 512 + 512]
                .rearrange("r (p h) -> (r p) h", p=128),
                in_=c.big[:, KQ + jc * 1028 + 4 + 1020:KQ + jc * 1028 + 4 + 1024]
                ).then_inc(sm["dQ"], 16)
    c.Y(ypk)
    npk = inc(c, "dQ", 16 * 12)

    def gcc1(eng, npk=npk):
        eng.wait_ge(sm["dQ"], npk)
        eng.collective_compute(
            "AllGather", ALU.bypass,
            replica_groups=[[0, 1], [2, 3], [4, 5], [6, 7]],
            ins=[c.ccB1_in], outs=[c.ccB1_out.rearrange("r a q -> (r a) q")],
        ).then_inc(sm["cc"], 1)
    c.G(gcc1)
    ncc1 = inc(c, "cc")

    def yhl(eng, ncc1=ncc1):
        eng.wait_ge(sm["cc"], ncc1)
        for r in range(2):
            for jc in range(12):
                eng.dma_start(
                    out=c.stg2[:, r * 64 + jc * 4:r * 64 + (jc + 1) * 4],
                    in_=c.ccB1_out[r, jc // 2:jc // 2 + 1,
                                   (jc % 2) * 512:(jc % 2) * 512 + 512]
                    .rearrange("r (p h) -> (r p) h", p=128)).then_inc(sm["dQ"], 16)
    c.Y(yhl)
    nhl = inc(c, "dQ", 16 * 24)
    ndvh = inc(c, "dve", 2)

    def vhl(eng, nhl=nhl, ndvh=ndvh):
        eng.wait_ge(sm["dQ"], nhl)
        eng.tensor_scalar(out=c.stg2[:, 128:176], in0=c.stg2[:, 0:48],
                          scalar1=c.g_halo0, scalar2=None,
                          op0=ALU.mult).then_inc(sm["dve"], 1)
        eng.wait_ge(sm["dve"], ndvh - 1)
        eng.scalar_tensor_tensor(out=c.stg2[:, 176:224], in0=c.stg2[:, 64:112],
                                 scalar=c.g_halo1, in1=c.stg2[:, 128:176],
                                 op0=ALU.mult, op1=ALU.add).then_inc(sm["dve"], 1)
    c.V(vhl)

    def vhl2(eng, ndvh=ndvh, base=c.cnt["dve"]):
        eng.wait_ge(sm["dve"], ndvh)
        d = base
        for jc in range(12):
            cpi = eng.tensor_copy(
                out=c.big[:, KQ + jc * 1028:KQ + jc * 1028 + 4],
                in_=c.stg2[:, 176 + jc * 4:176 + (jc + 1) * 4])
        cpi.then_inc(sm["dve"], 1)
    c.V(vhl2)
    inc(c, "dve")
    barrier(c)

    for tb in range(8):
        nmm = inc(c, "mm")

        def tv(eng, tb=tb, nmm=nmm, pdv=c.cnt["dve"]):
            eng.wait_ge(sm["dve"], pdv)
            eng.transpose(c.Pb[:, 0:128],
                          c.big[:, KQ + 10 * 1028 + 4 + tb * 128:
                                KQ + 10 * 1028 + 4 + (tb + 1) * 128], c.identb)
            eng.transpose(c.Pb[:, 128:256],
                          c.big[:, KQ + 11 * 1028 + 3 + tb * 128:
                                KQ + 11 * 1028 + 3 + (tb + 1) * 128],
                          c.identb).then_inc(sm["mm"], 1)
        c.T(tv)
        ndv = inc(c, "dve")

        def vv(eng, tb=tb, nmm=nmm, ndv=ndv):
            eng.wait_ge(sm["mm"], nmm)
            eng.tensor_copy(out=c.big[:, VT + tb * 256:VT + (tb + 1) * 256],
                            in_=c.Pb[:, 0:256]).then_inc(sm["dve"], 1)
        c.V(vv)
    barrier(c)

    for side in range(2):
        for nq in range(2):
            nmm = inc(c, "mm")

            def tm(eng, side=side, nq=nq, nmm=nmm, pdv=c.cnt["dve"]):
                eng.wait_ge(sm["dve"], pdv)
                nch = 2 if side == 0 else 8
                cb = CB1 + 512 if side == 0 else CB1
                jb = 8 if side == 0 else 0
                for ci in range(nch):
                    mmi = eng.matmul(
                        c.Pf[0:64, nq * 512:(nq + 1) * 512],
                        lhsT=c.big[:, cb + ci * 64:cb + (ci + 1) * 64],
                        rhs=c.big[:, KQ + (jb + ci) * 1028 + 4 + nq * 512:
                                  KQ + (jb + ci) * 1028 + 4 + (nq + 1) * 512],
                        start=(ci == 0), stop=(ci == nch - 1))
                mmi.then_inc(sm["mm"], 1)
            c.T(tm)
            ndv = inc(c, "dve")

            def vm(eng, side=side, nq=nq, nmm=nmm, ndv=ndv):
                eng.wait_ge(sm["mm"], nmm)
                pb = 0 if side == 0 else 64
                eng.tensor_copy(
                    out=c.big[pb:pb + 64, MN + nq * 512:MN + (nq + 1) * 512],
                    in_=c.Pf[0:64, nq * 512:(nq + 1) * 512]).then_inc(sm["dve"], 1)
            c.V(vm)
    barrier(c)

    for path in range(2):
        nocc = 8 if path == 0 else 2
        wsrc = c.w1q if path == 0 else c.w1k
        c1base = C1Q if path == 0 else C1K
        jbase = 0 if path == 0 else 8
        bcol0 = 0 if path == 0 else 16
        woff = 0 if path == 0 else 3072
        c1sem = "dWa" if path == 0 else "dWb"
        nw = inc(c, c1sem, 16)

        def gw1(eng, wsrc=wsrc, nocc=nocc, woff=woff, c1sem=c1sem):
            eng.dma_start(out=c.stg[:, woff:woff + nocc * 384]
                          .rearrange("p (o t q) -> p o t q", o=nocc, t=3),
                          in_=wsrc.rearrange("o t p q -> p o t q")
                          ).then_inc(sm[c1sem], 16)
        c.G(gw1)
        for occ in range(nocc):
            for seg, (t0, t1) in enumerate([(-2, 510), (510, 1022), (1022, 1024)]):
                nmm = inc(c, "mm")

                def t1f(eng, occ=occ, t0=t0, t1=t1, nw=nw, nmm=nmm,
                        c1base=c1base, jbase=jbase, seg=seg, woff=woff,
                        par=len(getattr(c, "_c1ev", [])) % 2,
                        c1sem=c1sem,
                        pact=(getattr(c, "_c1ev", [])[-2:-1] or [None])[0]):
                    eng.wait_ge(sm[c1sem], nw)
                    if pact is not None:
                        eng.wait_ge(sm["act"], pact)
                    n = t1 - t0
                    ps = c.P1[par]
                    for tau in range(3):
                        mmi = eng.matmul(
                            ps[:, 0:n],
                            lhsT=c.stg[:, woff + occ * 384 + tau * 128:
                                       woff + occ * 384 + (tau + 1) * 128],
                            rhs=c.big[:, KQ + (jbase + occ) * 1028 + (t0 + 2 + tau):
                                      KQ + (jbase + occ) * 1028 + (t0 + 2 + tau) + n],
                            start=(tau == 0), stop=(tau == 2))
                    mmi.then_inc(sm["mm"], 1)
                c.T(t1f)
                na = inc(c, "act")

                def a1f(eng, occ=occ, t0=t0, t1=t1, nmm=nmm, na=na,
                        c1base=c1base, bcol0=bcol0, seg=seg,
                        par=len(getattr(c, "_c1ev", [])) % 2):
                    eng.wait_ge(sm["mm"], nmm)
                    n = t1 - t0
                    ps = c.P1[par]
                    eng.activation(out=c.big[:, c1base + occ * 1026 + (t0 + 2):
                                             c1base + occ * 1026 + (t0 + 2) + n],
                                   in_=ps[:, 0:n], func=AF.Identity,
                                   bias=c.convb[:, bcol0 + occ:bcol0 + occ + 1],
                                   scale=1.0).then_inc(sm["act"], 1)
                c.A(a1f)
                if not hasattr(c, "_c1ev"):
                    c._c1ev = []
                c._c1ev.append(na)
    barrier(c)
    c._c1ev = []

    for path in range(2):
        nocc = 8 if path == 0 else 2
        nicc = 8 if path == 0 else 2
        wsrc = c.w2q if path == 0 else c.w2k
        c1base = C1Q if path == 0 else C1K
        bcol0 = 8 if path == 0 else 18
        mnb = 0 if path == 0 else 64
        for occ in range(nocc):
            nw = inc(c, "dWa" if occ % 2 == 0 else "dWb", 16)

            def gw2(eng, wsrc=wsrc, occ=occ, nicc=nicc, nw=nw,
                    pact=(getattr(c, "_c2w", [])[-2:-1] or [None])[0]):
                if pact is not None:
                    eng.wait_ge(sm["act"], pact)
                eng.dma_start(
                    out=c.wbuf[:, (occ % 2) * 4096:(occ % 2) * 4096 + nicc * 384]
                    .rearrange("p (i t q) -> p i t q", i=nicc, t=3),
                    in_=wsrc[occ].rearrange("i t p q -> p i t q")
                    ).then_inc(sm["dWa" if occ % 2 == 0 else "dWb"], 16)
            c.G(gw2)
            for nq in range(2):
                nmm = inc(c, "mm")

                def t2f(eng, occ=occ, nq=nq, nw=nw, nmm=nmm, nicc=nicc,
                        c1base=c1base, mnb=mnb,
                        pact=(getattr(c, "_c2ev", [])[-2:-1] or [None])[0]):
                    eng.wait_ge(sm["dWa" if occ % 2 == 0 else "dWb"], nw)
                    if pact is not None:
                        eng.wait_ge(sm["act"], pact)
                    ps = c.P1[nq % 2]
                    for ic in range(nicc):
                        for tau in range(3):
                            eng.matmul(
                                ps[:, :],
                                lhsT=c.wbuf[:, (occ % 2) * 4096 + ic * 384 +
                                            tau * 128:(occ % 2) * 4096 + ic * 384 +
                                            (tau + 1) * 128],
                                rhs=c.big[:, c1base + ic * 1026 + nq * 512 + tau:
                                          c1base + ic * 1026 + nq * 512 + tau + 512],
                                start=(ic == 0 and tau == 0), stop=False)
                    mmi = eng.matmul(
                        ps[:, :],
                        lhsT=c.big[mnb:mnb + 64, CB1 + 640:CB1 + 768],
                        rhs=c.big[mnb:mnb + 64, MN + nq * 512:MN + (nq + 1) * 512],
                        start=False, stop=True)
                    mmi.then_inc(sm["mm"], 1)
                c.T(t2f)
                na = inc(c, "act")

                def a2f(eng, path=path, occ=occ, nq=nq, nmm=nmm, na=na, bcol0=bcol0):
                    eng.wait_ge(sm["mm"], nmm)
                    ps = c.P1[nq % 2]
                    if path == 0:
                        dst = c.big[:, QH + occ * 1024 + nq * 512:
                                    QH + occ * 1024 + (nq + 1) * 512]
                    else:
                        kt = c.p4 if occ == 0 else c.qtl
                        dst = kt[:, nq * 512:(nq + 1) * 512]
                    eng.activation(out=dst, in_=ps[:, :], func=AF.Identity,
                                   bias=c.convb[:, bcol0 + occ:bcol0 + occ + 1],
                                   scale=1.0).then_inc(sm["act"], 1)
                c.A(a2f)
                if not hasattr(c, "_c2ev"):
                    c._c2ev = []
                c._c2ev.append(na)
                c._c2w = getattr(c, "_c2w", [])
            c._c2w.append(c.cnt["act"])
    barrier(c)
    c._c2ev = []
    c._c2w = []

    def gc2(eng):
        eng.dma_start(out=c.big[:, MN:MN + 128], in_=c.swap).then_inc(sm["dW"], 16)
        eng.dma_start(out=c.fa[0:64, 1100:1228],
                      in_=c.rope_c).then_inc(sm["dW"], 16)
        eng.dma_start(out=c.fa[0:64, 1228:1356],
                      in_=c.rope_s).then_inc(sm["dW"], 16)
    c.G(gc2)
    ngc2 = inc(c, "dW", 48)

    def ypos(eng):
        eng.dma_start(out=c.fcc[0:1, 0:1024], in_=c.pos).then_inc(sm["dQ"], 16)
    c.Y(ypos)
    npos = inc(c, "dQ", 16)

    def gio(eng):
        eng.iota(out=c.fa[0:64, 1024:1025].bitcast(I32), pattern=[[0, 1]],
                 base=0, channel_multiplier=1).then_inc(sm["gp"], 1)
    c.G(gio)
    inc(c, "gp")
    barrier(c)

    def vw0(eng, ngc2=ngc2, npos=npos):
        eng.wait_ge(sm["dW"], ngc2)
        eng.wait_ge(sm["dQ"], npos)
        eng.nop().then_inc(sm["dve"], 1)
    c.V(vw0)
    inc(c, "dve")

    def tw0(eng, ngc2=ngc2, npos=npos):
        eng.wait_ge(sm["dW"], ngc2)
        eng.wait_ge(sm["dQ"], npos)
    c.T(tw0)

    def aw0(eng, ngc2=ngc2):
        eng.wait_ge(sm["dW"], ngc2)
    c.A(aw0)

    CHI = float(np.float32(-2.0 * LN10K / 64.0))
    CLO = float(np.float32(-2.0 * LN10K / 64.0 - CHI))

    def vinv(eng, base=c.cnt["dve"]):
        d = base
        ops = [
            lambda: eng.tensor_scalar(out=c.fa[0:64, 1025:1026].bitcast(I32),
                                      in0=c.fa[0:64, 1024:1025].bitcast(I32),
                                      scalar1=-2, scalar2=None,
                                      op0=ALU.bitwise_and),
            lambda: eng.tensor_copy(out=c.fa[0:64, 1026:1027],
                                    in_=c.fa[0:64, 1025:1026].bitcast(I32)),
            lambda: eng.tensor_scalar(out=c.fa[0:64, 1027:1028],
                                      in0=c.fa[0:64, 1026:1027], scalar1=0.5,
                                      scalar2=CHI, op0=ALU.mult, op1=ALU.mult),
            lambda: eng.tensor_scalar(out=c.fa[0:64, 1028:1029],
                                      in0=c.fa[0:64, 1026:1027], scalar1=0.5,
                                      scalar2=CLO, op0=ALU.mult, op1=ALU.mult),
            lambda: eng.tensor_tensor(out=c.fa[0:64, 1029:1030],
                                      in0=c.fa[0:64, 1027:1028],
                                      in1=c.fa[0:64, 1028:1029], op=ALU.add),
        ]
        for op in ops:
            op().then_inc(sm["dve"], 1)
            d += 1
            eng.wait_ge(sm["dve"], d)
    c.V(vinv)
    nvi = inc(c, "dve", 5)

    def ainv(eng, nvi=nvi):
        eng.wait_ge(sm["dve"], nvi)
        eng.activation(out=c.fa[0:64, 1030:1031], in_=c.fa[0:64, 1029:1030],
                       func=AF.Exp).then_inc(sm["act"], 1)
    c.A(ainv)
    inc(c, "act")
    barrier(c)

    def tiv(eng):
        eng.transpose(c.Pf[0:1, 0:64], c.fa[0:64, 1030:1031],
                      c.identf[0:64, 0:64]).then_inc(sm["mm"], 1)
    c.T(tiv)
    inc(c, "mm")
    barrier(c)

    def viv(eng):
        eng.tensor_copy(out=c.fcc[0:1, 1024:1088],
                        in_=c.Pf[0:1, 0:64]).then_inc(sm["dve"], 1)
    c.V(viv)
    inc(c, "dve")
    barrier(c)

    def tth(eng):
        for nq in range(2):
            mmi = eng.matmul(c.Pf[0:64, nq * 512:(nq + 1) * 512],
                             lhsT=c.fcc[0:1, 1024:1088],
                             rhs=c.fcc[0:1, nq * 512:(nq + 1) * 512],
                             start=True, stop=True)
        mmi.then_inc(sm["mm"], 1)
    c.T(tth)
    inc(c, "mm")
    barrier(c)

    def vth(eng):
        eng.tensor_copy(out=c.fa[0:64, 0:1024],
                        in_=c.Pf[0:64, 0:1024]).then_inc(sm["dve"], 1)
    c.V(vth)
    inc(c, "dve")
    barrier(c)

    INV2PI = float(np.float32(1.0 / (2.0 * PI)))
    PI2HI = float(np.float32(2.0 * PI))
    PI2LO = float(np.float32(2.0 * PI - PI2HI))
    MAGIC = 12582912.0
    for var in range(2):
        def vrr(eng, var=var, base=c.cnt["dve"]):
            d = base
            ops = [
                lambda: eng.tensor_scalar(out=c.fa[64:128, 0:1024],
                                          in0=c.fa[0:64, 0:1024],
                                          scalar1=(0.0 if var == 0 else PI / 2),
                                          scalar2=None, op0=ALU.add),
                lambda: eng.tensor_scalar(out=c.fcc[64:128, 0:1024],
                                          in0=c.fa[64:128, 0:1024],
                                          scalar1=INV2PI, scalar2=MAGIC,
                                          op0=ALU.mult, op1=ALU.add),
                lambda: eng.tensor_scalar(out=c.fcc[64:128, 0:1024],
                                          in0=c.fcc[64:128, 0:1024],
                                          scalar1=MAGIC, scalar2=None,
                                          op0=ALU.subtract),
                lambda: eng.scalar_tensor_tensor(out=c.fa[64:128, 0:1024],
                                                 in0=c.fcc[64:128, 0:1024],
                                                 scalar=-PI2HI,
                                                 in1=c.fa[64:128, 0:1024],
                                                 op0=ALU.mult, op1=ALU.add),
                lambda: eng.scalar_tensor_tensor(out=c.fa[64:128, 0:1024],
                                                 in0=c.fcc[64:128, 0:1024],
                                                 scalar=-PI2LO,
                                                 in1=c.fa[64:128, 0:1024],
                                                 op0=ALU.mult, op1=ALU.add),
                lambda: eng.tensor_scalar(out=c.fa[64:128, 0:1024],
                                          in0=c.fa[64:128, 0:1024],
                                          scalar1=PI, scalar2=-PI,
                                          op0=ALU.min, op1=ALU.max),
            ]
            for op in ops:
                op().then_inc(sm["dve"], 1)
                d += 1
                eng.wait_ge(sm["dve"], d)
        c.V(vrr)
        nrr = inc(c, "dve", 6)

        def asin(eng, var=var, nrr=nrr):
            eng.wait_ge(sm["dve"], nrr)
            eng.activation(out=c.fcc[0:64, var * 1024:(var + 1) * 1024],
                           in_=c.fa[64:128, 0:1024],
                           func=AF.Sin).then_inc(sm["act"], 1)
        c.A(asin)
        inc(c, "act")
        barrier(c)

    for var in range(2):
        def tex(eng, var=var):
            lh = c.fa[0:64, 1228:1356] if var == 0 else c.fa[0:64, 1100:1228]
            for nq in range(2):
                mmi = eng.matmul(c.Pf[:, nq * 512:(nq + 1) * 512],
                                 lhsT=lh,
                                 rhs=c.fcc[0:64, (0 if var == 0 else 1024) + nq * 512:
                                           (0 if var == 0 else 1024) + (nq + 1) * 512],
                                 start=True, stop=True)
            mmi.then_inc(sm["mm"], 1)
        c.T(tex)
        nex = inc(c, "mm")

        def vex(eng, var=var, nex=nex):
            eng.wait_ge(sm["mm"], nex)
            eng.tensor_copy(out=c.vbuf[:, (1 - var) * 1024:(2 - var) * 1024],
                            in_=c.Pf[:, 0:1024]).then_inc(sm["dve"], 1)
        c.V(vex)
        inc(c, "dve")
        barrier(c)

    for ch in range(10):
        if ch < 8:
            buf = c.big[:, QH + ch * 1024:QH + (ch + 1) * 1024]
            isk = False
        else:
            buf = (c.p4 if ch == 8 else c.qtl)[:, 0:1024]
            isk = True
        _attn_b_normrope(c, buf, isk)

    prev_cp = None
    for h in [14, 15] + list(range(14)):
        ndv = inc(c, "dve")

        def vcp(eng, h=h, ndv=ndv, prev_cp=prev_cp):
            if h == 0 and prev_cp is not None:
                eng.wait_ge(sm["dve"], prev_cp)
            cdst = (h % 4) + 4 * (h // 8)
            bb = ((h // 4) % 2) * 64
            sb_ = (h % 2) * 64
            eng.tensor_copy(
                out=c.big[bb:bb + 64, QATT + cdst * 1024:QATT + (cdst + 1) * 1024],
                in_=c.big[sb_:sb_ + 64, QH + (h // 2) * 1024:
                          QH + (h // 2 + 1) * 1024]).then_inc(sm["dve"], 1)
        c.V(vcp)
        if h == 15:
            prev_cp = c.cnt["dve"]
    barrier(c)

    def ypk2(eng):
        eng.dma_start(out=c.ccB2_in[0:128, :], in_=c.p4[:, 0:1024]).then_inc(sm["dQ"], 16)
        eng.dma_start(out=c.ccB2_in[128:256, :], in_=c.qtl[:, 0:1024]).then_inc(sm["dQ"], 16)
        for tb in range(8):
            eng.dma_start(
                out=c.ccB2_in[256 + tb * 32:256 + (tb + 1) * 32, :]
                .rearrange("a (b q) -> (a b) q", q=256),
                in_=c.big[:, VT + tb * 256:VT + (tb + 1) * 256]).then_inc(sm["dQ"], 16)
    c.Y(ypk2)
    npk2 = inc(c, "dQ", 160)

    def gcc2(eng, npk2=npk2):
        eng.wait_ge(sm["dQ"], npk2)
        eng.collective_compute(
            "AllGather", ALU.bypass,
            replica_groups=[[0, 1], [2, 3], [4, 5], [6, 7]],
            ins=[c.ccB2_in], outs=[c.ccB2_out.rearrange("r a q -> (r a) q")],
        ).then_inc(sm["cc"], 1)
    c.G(gcc2)
    ncc2 = inc(c, "cc")

    ndvm = inc(c, "dve")

    def vmset(eng, ndvm=ndvm):
        eng.memset(c.big[:, VA:VA + 15 * 260], 1.0).then_inc(sm["dve"], 1)
    c.V(vmset)
    for tb in range(8):
        ndv = inc(c, "dve")

        def vown(eng, tb=tb, ndvm=ndvm, ndv=ndv):
            if tb == 0:
                eng.wait_ge(sm["dve"], ndvm)
            for g in range(4):
                cpi = eng.tensor_copy(
                    out=c.big[:, VA + tb * 260 + g * 65:VA + tb * 260 + g * 65 + 64],
                    in_=c.big[:, VT + tb * 256 + g * 64:VT + tb * 256 + (g + 1) * 64])
            cpi.then_inc(sm["dve"], 1)
        c.V(vown)
    nown = c.cnt["dve"]

    ndvm2 = inc(c, "dve")

    def vmset2(eng, nown=nown, npk2=npk2, ndvm2=ndvm2):
        eng.wait_ge(sm["dve"], nown)
        eng.wait_ge(sm["dQ"], npk2)
        eng.memset(c.big[:, VA + 15 * 260:VA + 16 * 260],
                   1.0).then_inc(sm["dve"], 1)
    c.V(vmset2)
    barrier(c)

    for t in range(2):
        tile = c.p4 if t == 0 else c.qtl
        nld = inc(c, "dQ", 32)

        def ykp(eng, t=t, ncc2=ncc2):
            if t == 0:
                eng.wait_ge(sm["cc"], ncc2)
            eng.dma_start(out=c.stg[:, 0:1024],
                          in_=c.ccB2_out[0, t * 128:(t + 1) * 128, :]).then_inc(sm["dQ"], 16)
            eng.dma_start(out=c.stg[:, 1024:2048],
                          in_=c.ccB2_out[1, t * 128:(t + 1) * 128, :]).then_inc(sm["dQ"], 16)
        c.Y(ykp)
        ndv = inc(c, "dve", 2)

        def vkp(eng, tile=tile, nld=nld, ndv=ndv):
            d = ndv - 2
            eng.wait_ge(sm["dQ"], nld)
            eng.tensor_scalar(out=c.stg2[:, 0:1024], in0=c.stg[:, 0:1024],
                              scalar1=c.g_peer0, scalar2=None,
                              op0=ALU.mult).then_inc(sm["dve"], 1)
            d += 1
            eng.wait_ge(sm["dve"], d)
            eng.scalar_tensor_tensor(out=tile[:, 1024:2048], in0=c.stg[:, 1024:2048],
                                     scalar=c.g_peer1, in1=c.stg2[:, 0:1024],
                                     op0=ALU.mult, op1=ALU.add).then_inc(sm["dve"], 1)
        c.V(vkp)
        barrier(c)
    for tb in range(8):
        nld = inc(c, "dQ", 32)

        def yvp(eng, tb=tb, pdv=c.cnt["dve"]):
            eng.wait_ge(sm["dve"], pdv)
            for r in range(2):
                eng.dma_start(
                    out=c.stg[:, r * 256:(r + 1) * 256],
                    in_=c.ccB2_out[r, 256 + tb * 32:256 + (tb + 1) * 32, :]
                    .rearrange("a (b q) -> (a b) q", q=256)).then_inc(sm["dQ"], 16)
        c.Y(yvp)
        ndv = inc(c, "dve", 3)

        def vvp(eng, tb=tb, nld=nld, ndv=ndv):
            d = ndv - 3
            eng.wait_ge(sm["dQ"], nld)
            eng.tensor_scalar(out=c.stg2[:, 0:256], in0=c.stg[:, 0:256],
                              scalar1=c.g_peer0, scalar2=None,
                              op0=ALU.mult).then_inc(sm["dve"], 1)
            d += 1
            eng.wait_ge(sm["dve"], d)
            eng.scalar_tensor_tensor(out=c.stg2[:, 256:512], in0=c.stg[:, 256:512],
                                     scalar=c.g_peer1, in1=c.stg2[:, 0:256],
                                     op0=ALU.mult, op1=ALU.add).then_inc(sm["dve"], 1)
            d += 1
            eng.wait_ge(sm["dve"], d)
            slot = 8 + tb
            for g in range(4):
                cpi = eng.tensor_copy(
                    out=c.big[:, VA + slot * 260 + g * 65:VA + slot * 260 + g * 65 + 64],
                    in_=c.stg2[:, 256 + g * 64:256 + (g + 1) * 64])
            cpi.then_inc(sm["dve"], 1)
        c.V(vvp)
    barrier(c)

    ndvn = inc(c, "dve", 2)

    def vnegb(eng, ndvn=ndvn):
        d = ndvn - 2
        eng.tensor_scalar(out=cl[:, 150:151], in0=c.g_ktemp, scalar1=-SC_B,
                          scalar2=None, op0=ALU.mult).then_inc(sm["dve"], 1)
        d += 1
        eng.wait_ge(sm["dve"], d)
        eng.tensor_tensor(out=cl[:, 151:152], in0=cl[:, 150:151], in1=c.g_kill,
                          op=ALU.add).then_inc(sm["dve"], 1)
    c.V(vnegb)
    barrier(c)
    for h in range(16):
        _attn_b_head(c, h, QATT, VA, CX)
    barrier(c)

    _oproj_residual(c, wblocks=c.wobT, nof=8, nlat=8, latbase=CX,
                    bias_row=None, xsrc=None, emit_out=emit_out, tb_count=8)


def _attn_b_head(c, h, QATT, VA, CX):
    sm = c.sem
    cl = c.cols
    g = h // 4
    ktile = c.p4 if g < 2 else c.qtl
    kb = (g % 2) * 64
    chq = (h % 4) + 4 * (h // 8)
    bb = (h % 2) * 64
    gk = h // 4

    qplan = []
    for qt in range(2):
        own = list(range(0, min(qt * 4 + 4, 8)))
        slots = own + list(range(8, 16))
        nmm_sc, act_b, mm_pv, ndvm = {}, {}, {}, {}
        for bi in range(0, len(slots), 2):
            pair = slots[bi:bi + 2]
            nmm_sc[bi] = inc(c, "mm")
            act_b[bi] = inc(c, "act")
            if [sl for sl in pair if sl < 8 and sl - qt * 4 >= 0]:
                ndvm[bi] = inc(c, "dve")
            mm_pv[bi] = inc(c, "mm")
        ndv5 = inc(c, "dve", 2)
        nmm_rl = inc(c, "mm")
        ndv6 = inc(c, "dve", 2)
        qplan.append(dict(slots=slots, nmm_sc=nmm_sc, act_b=act_b,
                          mm_pv=mm_pv, ndvm=ndvm, ndv5=ndv5,
                          nmm_rl=nmm_rl, ndv6=ndv6))

    def t_all(eng):
        for qt in range(2):
            P = qplan[qt]
            slots = P["slots"]
            first_pv = True
            for bi in range(0, len(slots), 2):
                pair = slots[bi:bi + 2]
                if bi >= 2:
                    eng.wait_ge(sm["act"], P["act_b"][bi - 2])
                for j, sl in enumerate(pair):
                    mmi = eng.matmul(
                        c.Pf[:, j * 512:(j + 1) * 512],
                        lhsT=ktile[kb:kb + 64, sl * 128:(sl + 1) * 128],
                        rhs=c.big[kb:kb + 64, QATT + chq * 1024 + qt * 512:
                                  QATT + chq * 1024 + (qt + 1) * 512],
                        start=True, stop=True)
                mmi.then_inc(sm["mm"], 1)
                eng.wait_ge(sm["act"], P["act_b"][bi])
                if bi in P["ndvm"]:
                    eng.wait_ge(sm["dve"], P["ndvm"][bi])
                last = bi + 2 >= len(slots)
                for j, sl in enumerate(pair):
                    mmi = eng.matmul(
                        c.P1[0][0:65, :],
                        lhsT=c.big[:, VA + sl * 260 + gk * 65:
                                   VA + sl * 260 + gk * 65 + 65],
                        rhs=c.stg[:, (bi % 4) // 2 * 1024 + j * 512:
                                  (bi % 4) // 2 * 1024 + (j + 1) * 512],
                        start=(first_pv and j == 0),
                        stop=(last and j == len(pair) - 1))
                mmi.then_inc(sm["mm"], 1)
                first_pv = False
            eng.wait_ge(sm["dve"], P["ndv5"])
            eng.matmul(c.P1[2][0:64, :], lhsT=c.ones1b[0:1, 0:64],
                       rhs=c.browz[0:1, 0:512], start=True,
                       stop=True).then_inc(sm["mm"], 1)
            eng.wait_ge(sm["dve"], P["ndv6"])
    c.T(t_all)

    def a_all(eng):
        for qt in range(2):
            P = qplan[qt]
            slots = P["slots"]
            for bi in range(0, len(slots), 2):
                pair = slots[bi:bi + 2]
                eng.wait_ge(sm["mm"], P["nmm_sc"][bi])
                if bi >= 4:
                    eng.wait_ge(sm["mm"], P["mm_pv"][bi - 4])
                bias = cl[:, 150:151] if pair[0] < 8 else cl[:, 151:152]
                eng.activation(out=c.stg[:, (bi % 4) // 2 * 1024:
                                         ((bi % 4) // 2 + 1) * 1024],
                               in_=c.Pf[:, :], func=AF.Exp,
                               bias=bias).then_inc(sm["act"], 1)
    c.A(a_all)

    def v_all(eng):
        for qt in range(2):
            P = qplan[qt]
            slots = P["slots"]
            for bi in range(0, len(slots), 2):
                if bi not in P["ndvm"]:
                    continue
                pair = slots[bi:bi + 2]
                eng.wait_ge(sm["act"], P["act_b"][bi])
                for j, sl in enumerate(pair):
                    j0 = sl - qt * 4
                    if sl < 8 and j0 >= 0:
                        cpi = eng.tensor_tensor(
                            out=c.stg[:, (bi % 4) // 2 * 1024 + j * 512:
                                      (bi % 4) // 2 * 1024 + (j + 1) * 512],
                            in0=c.stg[:, (bi % 4) // 2 * 1024 + j * 512:
                                      (bi % 4) // 2 * 1024 + (j + 1) * 512],
                            in1=c.kbuf[:, j0 * 512:(j0 + 1) * 512],
                            op=ALU.mult)
                cpi.then_inc(sm["dve"], 1)
            eng.wait_ge(sm["mm"], P["mm_pv"][len(slots) - 2])
            eng.reciprocal(out=c.fcc[0:1, 0:512],
                           in_=c.P1[0][64:65, 0:512]).then_inc(sm["dve"], 1)
            eng.wait_ge(sm["dve"], P["ndv5"] - 1)
            eng.tensor_copy(out=c.browz[0:1, 0:512],
                            in_=c.fcc[0:1, 0:512]).then_inc(sm["dve"], 1)
            eng.wait_ge(sm["mm"], P["nmm_rl"])
            eng.tensor_copy(out=c.stg2[0:64, 1024:1536],
                            in_=c.P1[2][0:64, :]).then_inc(sm["dve"], 1)
            eng.wait_ge(sm["dve"], P["ndv6"] - 1)
            eng.tensor_tensor(
                out=c.big[bb:bb + 64, CX + (h // 2) * 1024 + qt * 512:
                          CX + (h // 2) * 1024 + (qt + 1) * 512],
                in0=c.P1[0][0:64, :], in1=c.stg2[0:64, 1024:1536],
                op=ALU.mult).then_inc(sm["dve"], 1)
    c.V(v_all)


def emit_finish(c):
    sm = c.sem
    tot = c.cnt["dO"]

    def y(eng, tot=tot):
        eng.wait_ge(sm["dO"], tot)
    c.Y(y)
    barrier(c)



def kernel(**inputs):
    in_maps = make_in_maps(inputs)
    nc = build_nc(STAGE)
    res = run_bass_kernel_spmd(nc, in_maps, core_ids=list(range(8)))
    out = np.zeros((4, S, H), np.float32)
    for core in range(8):
        b, half = core // 2, core % 2
        out[b, half * ROWS:(half + 1) * ROWS, :] = res.results[core]["out"]
    return out


# version 07 (final)
# speedup vs baseline: 1.0609x; 1.0609x over previous
import contextlib
import os
import numpy as np
import concourse.bass as bass
import concourse.mybir as mybir
from concourse.bass_utils import run_bass_kernel_spmd

F32 = mybir.dt.float32
BF16 = mybir.dt.bfloat16
I32 = mybir.dt.int32
AF = mybir.ActivationFunctionType
ALU = mybir.AluOpType

H = 4096
S = 2048
ROWS = 1024
SC_A = float(1.0 / np.sqrt(128.0))
SC_H = float(SC_A ** 0.5)
SC_B = 0.125
SC_BH = float(SC_B ** 0.5)
LN10K = float(np.log(10000.0))
PI = float(np.pi)
KILL = -30000.0
STAGE = 4


class Ctx:
    pass



def _blk(w, kchunks, mslices, mwidth):
    K, M = w.shape
    assert K == kchunks * 128 and M == mslices * mwidth
    return np.ascontiguousarray(
        w.reshape(kchunks, 128, mslices, mwidth).transpose(2, 0, 1, 3))


def make_in_maps(inputs):
    x = np.asarray(inputs["L0_attn_a_x"], dtype=np.float32)
    qkvw = np.asarray(inputs["L0_attn_a_attn_qkvw"], np.float32)
    qkvb = np.asarray(inputs["L0_attn_a_attn_qkvb"], np.float32)
    ow = np.asarray(inputs["L0_attn_a_attn_ow"], np.float32)
    ob = np.asarray(inputs["L0_attn_a_attn_ob"], np.float32)
    wqv = np.asarray(inputs["L0_attn_b_w_qkv"], np.float32)
    wo_b = np.asarray(inputs["L0_attn_b_w_o"], np.float32)
    qc1w = np.asarray(inputs["L0_attn_b_qc1_w"], np.float32)
    qc1b = np.asarray(inputs["L0_attn_b_qc1_b"], np.float32)
    qc2w = np.asarray(inputs["L0_attn_b_qc2_w"], np.float32)
    qc2b = np.asarray(inputs["L0_attn_b_qc2_b"], np.float32)
    kc1w = np.asarray(inputs["L0_attn_b_kc1_w"], np.float32)
    kc1b = np.asarray(inputs["L0_attn_b_kc1_b"], np.float32)
    kc2w = np.asarray(inputs["L0_attn_b_kc2_w"], np.float32)
    kc2b = np.asarray(inputs["L0_attn_b_kc2_b"], np.float32)
    ktemp = np.asarray(inputs["L0_attn_b_key_temp"], np.float32)
    wg = np.asarray(inputs["L0_ffn_w_gate"], np.float32)
    swg = np.asarray(inputs["L0_ffn_s_wgate"], np.float32)
    wu = np.asarray(inputs["L0_ffn_w_up"], np.float32)
    swu = np.asarray(inputs["L0_ffn_s_wup"], np.float32)
    wd = np.asarray(inputs["L0_ffn_w_down"], np.float32)
    swd = np.asarray(inputs["L0_ffn_s_wdown"], np.float32)
    invg = np.asarray(inputs["L0_ffn_inv_gate"], np.float32)
    invu = np.asarray(inputs["L0_ffn_inv_up"], np.float32)
    invi = np.asarray(inputs["L0_ffn_inv_inter"], np.float32)

    ident = np.eye(128, dtype=np.float32)
    w1q = np.zeros((8, 3, 128, 128), np.float32)
    for occ in range(8):
        for tau in range(3):
            for gl in range(2):
                g = occ * 2 + gl
                w1q[occ, tau, gl * 64:(gl + 1) * 64, gl * 64:(gl + 1) * 64] = \
                    qc1w[g * 64:(g + 1) * 64, :, tau].T
    w1k = np.zeros((2, 3, 128, 128), np.float32)
    for occ in range(2):
        for tau in range(3):
            for gl in range(2):
                g = occ * 2 + gl
                w1k[occ, tau, gl * 64:(gl + 1) * 64, gl * 64:(gl + 1) * 64] = \
                    kc1w[g * 64:(g + 1) * 64, :, tau].T
    w2q = np.ascontiguousarray(
        qc2w.transpose(1, 2, 0).reshape(8, 128, 3, 8, 128).transpose(0, 2, 3, 1, 4))
    w2k = np.ascontiguousarray(
        kc2w.transpose(1, 2, 0).reshape(2, 128, 3, 2, 128).transpose(0, 2, 3, 1, 4))
    hsum16 = np.zeros((8, 128, 64), np.float32)
    for cc in range(8):
        for hl in range(2):
            hsum16[cc, hl * 64:(hl + 1) * 64, :] = np.eye(64, dtype=np.float32) / 16.0
    hsum4 = np.zeros((2, 128, 64), np.float32)
    for cc in range(2):
        for hl in range(2):
            hsum4[cc, hl * 64:(hl + 1) * 64, :] = np.eye(64, dtype=np.float32) / 4.0
    bc2 = np.zeros((64, 128), np.float32)
    bc2[:, 0:64] = np.eye(64, dtype=np.float32) * 0.5
    bc2[:, 64:128] = np.eye(64, dtype=np.float32) * 0.5
    hsum2x64 = np.zeros((128, 2), np.float32)
    hsum2x64[0:64, 0] = 1.0
    hsum2x64[64:128, 1] = 1.0
    bch2 = np.zeros((2, 128), np.float32)
    bch2[0, 0:64] = 1.0
    bch2[1, 64:128] = 1.0
    swap = np.zeros((128, 128), np.float32)
    for p in range(128):
        swap[p ^ 1, p] = 1.0
    rope_c = np.zeros((64, 128), np.float32)
    rope_s = np.zeros((64, 128), np.float32)
    for p in range(128):
        i = (p % 64) // 2
        rope_c[i, p] = 1.0
        rope_s[i, p] = -1.0 if (p % 2 == 0) else 1.0
    tri = np.triu(np.ones((128, 128), np.float32))
    masks = np.zeros((4, 128, 512), np.float32)
    for j in range(4):
        masks[j, :, j * 128:(j + 1) * 128] = tri
        masks[j, :, (j + 1) * 128:] = 1.0

    in_maps = []
    for core in range(8):
        b, half = core // 2, core % 2
        xb = x[b]
        roff = half * ROWS
        wq = qkvw[:, 2048 * half:2048 * (half + 1)]
        wk = qkvw[:, 4096 + 2048 * half:4096 + 2048 * (half + 1)]
        wv = qkvw[:, 8192 + 2048 * half:8192 + 2048 * (half + 1)]
        wqkv = np.concatenate([_blk(wq, 32, 16, 128), _blk(wk, 32, 16, 128),
                               _blk(wv, 32, 16, 128)], axis=0)
        bqkv = np.concatenate([
            qkvb[2048 * half:2048 * (half + 1)],
            qkvb[4096 + 2048 * half:4096 + 2048 * (half + 1)],
            qkvb[8192 + 2048 * half:8192 + 2048 * (half + 1)],
        ]).reshape(48, 128).T
        gates = np.zeros((128, 8), np.float32)
        gates[:, 0] = 1.0 - half
        gates[:, 1] = float(half)
        gates[:, 2] = float(half)
        gates[:, 3] = 1.0 - half
        gates[:, 4] = float(half)
        gates[:, 5] = 0.0
        gates[:, 6] = 0.0 if half else KILL
        gates[:, 7] = ktemp[0]
        m = {
            "x_tm": np.ascontiguousarray(xb),
            "x_fm": np.ascontiguousarray(xb.T),
            "x_own": np.ascontiguousarray(xb[roff:roff + ROWS]),
            "wqkv": wqkv,
            "bqkv": np.ascontiguousarray(bqkv),
            "wo": _blk(ow, 32, 8, 512),
            "ob": ob.reshape(1, 4096),
            "wgT": _blk(np.ascontiguousarray(wg.T), 32, 4, 512),
            "wuT": _blk(np.ascontiguousarray(wu.T), 32, 4, 512),
            "wdT": _blk(np.ascontiguousarray(wd.T), 16, 8, 512),
            "swg": swg.reshape(1, 2048), "swu": swu.reshape(1, 2048),
            "swd": swd.reshape(1, 4096),
            "invg": invg.reshape(1, 4096), "invu": invu.reshape(1, 4096),
            "invi": invi.reshape(1, 2048),
            "wqbT": _blk(np.ascontiguousarray(wqv.T), 32, 12, 128),
            "wobT": _blk(np.ascontiguousarray(wo_b.T), 8, 8, 512),
            "w1q": w1q, "w1k": w1k, "w2q": w2q, "w2k": w2k,
            "cb": np.ascontiguousarray(
                np.concatenate([qc1b, qc2b, kc1b, kc2b]).reshape(20, 128).T),
            "pos": np.arange(roff, roff + ROWS, dtype=np.float32).reshape(1, ROWS),
            "masks": masks,
            "ident": ident,
            "hsum16": hsum16, "hsum4": hsum4, "bc2": bc2,
            "hsum2x64": hsum2x64, "bch2": bch2, "swap": swap,
            "rope_c": rope_c, "rope_s": rope_s,
            "gates": gates,
        }
        in_maps.append(m)
    return in_maps



def inc(c, name, k=1):
    c.cnt[name] += k
    return c.cnt[name]


def barrier(c):
    h = c.sem["gp"]
    tgt = c.cnt["gp"] + 5
    c.cnt["gp"] = tgt
    for reg, own in ((c.T, "mm"), (c.V, "dve"), (c.A, "act"), (c.Y, None), (c.G, None)):
        tgt_own = c.cnt[own] if own else 0
        def f(eng, tgt=tgt, h=h, own=own, tgt_own=tgt_own):
            if own:
                eng.wait_ge(c.sem[own], tgt_own)
            eng.nop().then_inc(h, 1)
            eng.wait_ge(h, tgt)
        reg(f)


def build_nc(stage=STAGE):
    nc = bass.Bass("TRN2", target_bir_lowering=False)
    c = Ctx()
    c.nc = nc
    c.stage = stage
    c.kph = int(os.environ.get("KPH", "99"))

    def din(name, shape, dt=F32):
        setattr(c, name, nc.dram_tensor(name, list(shape), dt, kind="ExternalInput").ap())

    din("x_tm", [S, H]); din("x_fm", [H, S]); din("x_own", [ROWS, H])
    din("wqkv", [48, 32, 128, 128]); din("bqkv", [128, 48])
    din("wo", [8, 32, 128, 512]); din("ob", [1, 4096])
    din("wgT", [4, 32, 128, 512]); din("wuT", [4, 32, 128, 512])
    din("wdT", [8, 16, 128, 512])
    din("swg", [1, 2048]); din("swu", [1, 2048]); din("swd", [1, 4096])
    din("invg", [1, 4096]); din("invu", [1, 4096]); din("invi", [1, 2048])
    din("wqbT", [12, 32, 128, 128]); din("wobT", [8, 8, 128, 512])
    din("w1q", [8, 3, 128, 128]); din("w1k", [2, 3, 128, 128])
    din("w2q", [8, 3, 8, 128, 128]); din("w2k", [2, 3, 2, 128, 128])
    din("cb", [128, 20]); din("pos", [1, ROWS]); din("masks", [4, 128, 512])
    din("ident", [128, 128])
    din("hsum16", [8, 128, 64]); din("hsum4", [2, 128, 64]); din("bc2", [64, 128])
    din("hsum2x64", [128, 2]); din("bch2", [2, 128]); din("swap", [128, 128])
    din("rope_c", [64, 128]); din("rope_s", [64, 128])
    din("gates", [128, 8])

    c.out = nc.dram_tensor("out", [ROWS, H], F32, kind="ExternalOutput").ap()

    c.qA = nc.dram_tensor("qA", [16, 128, 2048], BF16, kind="Internal").ap()
    c.kA = nc.dram_tensor("kA", [16, 128, 2048], BF16, kind="Internal").ap()
    c.vA = nc.dram_tensor("vA", [16, 128, 2048], BF16, kind="Internal").ap()
    c.ccA_in = nc.dram_tensor("ccA_in", [16, 128, 2048], BF16, kind="Internal").ap()
    c.ccA_out = nc.dram_tensor("ccA_out", [2, 16, 128, 2048], BF16, kind="Internal").ap()
    c.ccB1_in = nc.dram_tensor("ccB1_in", [6, 1024], BF16, kind="Internal").ap()
    c.ccB1_out = nc.dram_tensor("ccB1_out", [2, 6, 1024], BF16, kind="Internal").ap()
    c.ccB2_in = nc.dram_tensor("ccB2_in", [512, 1024], BF16, kind="Internal").ap()
    c.ccB2_out = nc.dram_tensor("ccB2_out", [2, 512, 1024], BF16, kind="Internal").ap()

    es = contextlib.ExitStack()

    def sb(name, shape, dt):
        setattr(c, name, es.enter_context(nc.sbuf_tensor(name, list(shape), dt)))

    sb("big", [128, 65536], BF16)
    sb("wbuf", [128, 8192], BF16)
    sb("stg", [128, 4096], BF16)
    sb("stg2", [128, 2048], BF16)
    sb("kbuf", [128, 2048], BF16)
    sb("vbuf", [128, 2048], BF16)
    sb("qtl", [128, 2048], BF16)
    sb("p4", [128, 2048], BF16)
    sb("fa", [128, 2048], F32)
    sb("fb", [128, 4096], BF16)
    sb("fcc", [128, 2048], F32)
    sb("cols", [128, 160], F32)
    sb("rowf", [1, 128], F32)
    sb("browz", [1, 4096], BF16)
    sb("cb16", [128, 128], BF16)
    sb("cf32", [128, 256], F32)
    sb("onesb", [128, 8], BF16)
    sb("ones1b", [1, 128], BF16)
    sb("ones1f", [1, 128], F32)

    c.P1 = [es.enter_context(nc.psum_tensor(f"p1_{i}", [128, 512], F32)) for i in range(4)]
    c.Pf = es.enter_context(nc.psum_tensor("pf", [128, 1024], F32))
    c.Pb = es.enter_context(nc.psum_tensor("pb", [128, 2048], BF16))

    c.sem = {}
    c.cnt = {}
    for name in ["dW", "dX", "dQ", "dO", "mm", "act", "dve", "gp", "cc",
                 "dWa", "dWb", "dXa", "dXb", "dSa", "dSb", "dKs", "dVs", "dQs",
                 "dCa", "dCb"]:
        c.sem[name] = es.enter_context(nc.semaphore(name))
        c.cnt[name] = 0

    c.block = es.enter_context(nc.Block())
    c.T = lambda f: c.block.tensor(f)
    c.V = lambda f: c.block.vector(f)
    c.A = lambda f: c.block.scalar(f)
    c.Y = lambda f: c.block.sync(f)
    c.G = lambda f: c.block.gpsimd(f)

    emit_setup(c)
    emit_attn_a(c)
    if c.stage >= 2 and c.kph >= 7:
        emit_ffn(c, tag="f1", emit_out=(c.stage == 2))
    if c.stage >= 3:
        emit_attn_b(c, emit_out=(c.stage == 3))
    if c.stage >= 4:
        emit_ffn(c, tag="f2", emit_out=True)
    emit_finish(c)

    es.close()
    return nc



def emit_setup(c):
    h = c.sem["dW"]

    def g(eng):
        eng.dma_start(out=c.cf32[:, 0:128], in_=c.ident).then_inc(h, 16)
        eng.dma_start(out=c.cf32[:, 128:136], in_=c.gates).then_inc(h, 16)
        eng.dma_start(out=c.cf32[:, 136:156], in_=c.cb).then_inc(h, 16)
        eng.dma_start(out=c.cf32[:, 156:204], in_=c.bqkv).then_inc(h, 16)
        eng.dma_start(out=c.cb16[:, 0:128], in_=c.ident).then_inc(h, 16)
    c.G(g)
    n = inc(c, "dW", 16 * 5)

    def v(eng, n=n):
        eng.wait_ge(c.sem["dW"], n)
        eng.memset(c.onesb[:, :], 1.0)
        eng.memset(c.ones1b[:, :], 1.0)
        eng.memset(c.ones1f[:, :], 1.0)
        eng.memset(c.cols[:, 148:149], 1e-5)
        eng.memset(c.cols[:, 149:150], 0.0)
        eng.tensor_scalar(out=c.cols[:, 96:128], in0=c.cf32[:, 156:188],
                          scalar1=SC_H, scalar2=None, op0=ALU.mult)
        eng.tensor_copy(out=c.cols[:, 128:144],
                        in_=c.cf32[:, 188:204]).then_inc(c.sem["dve"], 1)
    c.V(v)
    inc(c, "dve")
    barrier(c)
    c.identb = c.cb16[:, 0:128]
    c.identf = c.cf32[:, 0:128]
    c.g_sel0 = c.cf32[:, 128:129]
    c.g_sel1 = c.cf32[:, 129:130]
    c.g_peer0 = c.cf32[:, 130:131]
    c.g_peer1 = c.cf32[:, 131:132]
    c.g_halo0 = c.cf32[:, 132:133]
    c.g_halo1 = c.cf32[:, 133:134]
    c.g_kill = c.cf32[:, 134:135]
    c.g_ktemp = c.cf32[:, 135:136]
    c.convb = c.cf32[:, 136:156]
    c.bqkv_s = c.cols[:, 96:144]



def emit_attn_a(c):
    sm, ct = c.sem, c.cnt

    act0 = ct["act"]
    xt = []
    for hc in range(32):
        xt.append(inc(c, "dXa" if hc % 2 == 0 else "dXb", 16))

    def y(eng):
        for hc in range(32):
            if hc >= 2:
                eng.wait_ge(sm["act"], act0 + 2 * (hc - 2) + 2)
            buf = c.fa if hc % 2 == 0 else c.fcc
            tc, hf = hc // 2, hc % 2
            eng.dma_start(out=buf[:, :],
                          in_=c.x_tm[tc * 128:(tc + 1) * 128, hf * 2048:(hf + 1) * 2048]
                          ).then_inc(sm["dXa" if hc % 2 == 0 else "dXb"], 16)
    c.Y(y)

    def a(eng):
        for hc in range(32):
            eng.wait_ge(sm["dXa" if hc % 2 == 0 else "dXb"], xt[hc])
            buf = c.fa if hc % 2 == 0 else c.fcc
            tc, hf = hc // 2, hc % 2
            if hc > 0:
                eng.wait_ge(sm["act"], act0 + 2 * hc - 1)
            eng.activation(out=c.stg[:, 0:2048], in_=buf[:, :], func=AF.Copy,
                           accum_out=c.cols[:, 16 * hf + tc:16 * hf + tc + 1]
                           ).then_inc(sm["act"], 1)
            if hc > 0:
                eng.wait_ge(sm["act"], act0 + 2 * hc)
            eng.activation(out=c.stg[:, 2048:4096], in_=buf[:, :], func=AF.Square,
                           accum_out=c.cols[:, 32 + 16 * hf + tc:32 + 16 * hf + tc + 1]
                           ).then_inc(sm["act"], 1)
    c.A(a)
    ct["act"] += 64
    barrier(c)

    if c.kph < 2:
        return
    cl = c.cols

    def v1(eng):
        d = c.cnt["dve"]
        ops = [
            lambda: eng.tensor_tensor(out=cl[:, 0:16], in0=cl[:, 0:16],
                                      in1=cl[:, 16:32], op=ALU.add),
            lambda: eng.tensor_scalar(out=cl[:, 0:16], in0=cl[:, 0:16],
                                      scalar1=1.0 / 4096.0, scalar2=None,
                                      op0=ALU.mult),
            lambda: eng.tensor_tensor(out=cl[:, 32:48], in0=cl[:, 32:48],
                                      in1=cl[:, 48:64], op=ALU.add),
            lambda: eng.tensor_scalar(out=cl[:, 32:48], in0=cl[:, 32:48],
                                      scalar1=1.0 / 4096.0, scalar2=None,
                                      op0=ALU.mult),
            lambda: eng.tensor_tensor(out=cl[:, 16:32], in0=cl[:, 0:16],
                                      in1=cl[:, 0:16], op=ALU.mult),
            lambda: eng.tensor_tensor(out=cl[:, 48:64], in0=cl[:, 32:48],
                                      in1=cl[:, 16:32], op=ALU.subtract),
        ]
        for i, op in enumerate(ops):
            op().then_inc(sm["dve"], 1)
            d += 1
            eng.wait_ge(sm["dve"], d)
    c.V(v1)
    inc(c, "dve", 6)
    barrier(c)

    def a1(eng):
        eng.activation(out=cl[:, 48:64], in_=cl[:, 48:64], func=AF.Sqrt,
                       bias=cl[:, 148:149]).then_inc(sm["act"], 1)
    c.A(a1)
    inc(c, "act")
    barrier(c)

    def v2(eng):
        d = c.cnt["dve"]
        ops = [
            lambda: eng.reciprocal(out=cl[:, 64:80], in_=cl[:, 48:64]),
            lambda: eng.tensor_tensor(out=cl[:, 16:32], in0=cl[:, 0:16],
                                      in1=cl[:, 64:80], op=ALU.mult),
            lambda: eng.tensor_scalar(out=cl[:, 80:96], in0=cl[:, 16:32],
                                      scalar1=-1.0, scalar2=None, op0=ALU.mult),
        ]
        for op in ops:
            op().then_inc(sm["dve"], 1)
            d += 1
            eng.wait_ge(sm["dve"], d)
    c.V(v2)
    inc(c, "dve", 3)
    barrier(c)

    def t1(eng):
        eng.transpose(c.Pf[0:32, 0:128], cl[:, 64:96],
                      c.identf).then_inc(sm["mm"], 1)
    c.T(t1)
    inc(c, "mm")
    barrier(c)

    def v3(eng):
        eng.tensor_copy(out=c.fcc[0:32, 0:128],
                        in_=c.Pf[0:32, 0:128]).then_inc(sm["dve"], 1)
    c.V(v3)
    inc(c, "dve")
    barrier(c)

    for grp in range(4):
        def tsel(eng, grp=grp):
            for j in range(8):
                row = (grp // 2) * 16 + (grp % 2) * 8 + j
                mmi = eng.matmul(c.Pf[0:1, j * 128:(j + 1) * 128],
                                 lhsT=c.identf[0:32, row:row + 1],
                                 rhs=c.fcc[0:32, 0:128], start=True, stop=True)
                if j == 7:
                    mmi.then_inc(sm["mm"], 1)
        c.T(tsel)
        n = inc(c, "mm")

        def vsel(eng, grp=grp, n=n):
            eng.wait_ge(sm["mm"], n)
            dst = c.browz[0:1, grp * 1024:(grp + 1) * 1024]
            eng.tensor_copy(out=dst, in_=c.Pf[0:1, 0:1024]).then_inc(sm["dve"], 1)
        c.V(vsel)
        n2 = inc(c, "dve")

        def tw(eng, n2=n2):
            eng.wait_ge(sm["dve"], n2)
        c.T(tw)

    for half in range(4):
        def tb(eng, half=half):
            for n0 in range(2):
                mmi = eng.matmul(c.Pf[:, n0 * 512:(n0 + 1) * 512],
                                 lhsT=c.ones1b[0:1, :],
                                 rhs=c.browz[0:1, half * 1024 + n0 * 512:
                                             half * 1024 + (n0 + 1) * 512],
                                 start=True, stop=True)
            mmi.then_inc(sm["mm"], 1)
        c.T(tb)
        n = inc(c, "mm")

        def vb(eng, half=half, n=n):
            eng.wait_ge(sm["mm"], n)
            eng.tensor_copy(out=c.fb[:, half * 1024:(half + 1) * 1024],
                            in_=c.Pf[:, :]).then_inc(sm["dve"], 1)
        c.V(vb)
        n2 = inc(c, "dve")

        def tw2(eng, n2=n2):
            eng.wait_ge(sm["dve"], n2)
        c.T(tw2)
    barrier(c)

    if c.kph < 3:
        return
    dve0 = ct["dve"]
    xt2 = []
    for fc in range(32):
        xt2.append(inc(c, "dXa" if fc % 2 == 0 else "dXb", 16))

    def y2(eng):
        for fc in range(32):
            if fc >= 2:
                eng.wait_ge(sm["dve"], dve0 + 2 * (fc - 2) + 2)
            buf = c.fa if fc % 2 == 0 else c.fcc
            eng.dma_start(out=buf[:, :],
                          in_=c.x_fm[fc * 128:(fc + 1) * 128, :]
                          ).then_inc(sm["dXa" if fc % 2 == 0 else "dXb"], 16)
    c.Y(y2)

    def v4(eng):
        d = ct["dve"]
        for fc in range(32):
            eng.wait_ge(sm["dXa" if fc % 2 == 0 else "dXb"], xt2[fc])
            buf = c.fa if fc % 2 == 0 else c.fcc
            eng.tensor_tensor(out=buf[:, :], in0=buf[:, :], in1=c.fb[:, 0:2048],
                              op=ALU.mult).then_inc(sm["dve"], 1)
            d += 1
            eng.wait_ge(sm["dve"], d)
            eng.tensor_tensor(out=c.big[:, fc * 2048:(fc + 1) * 2048], in0=buf[:, :],
                              in1=c.fb[:, 2048:4096], op=ALU.add).then_inc(sm["dve"], 1)
            d += 1
    c.V(v4)
    ct["dve"] += 64
    barrier(c)

    mm0 = ct["mm"]
    act0b = ct["act"]
    wsem = ["dWa", "dWb"]
    ssem = ["dSa", "dSb"]
    wt = [inc(c, wsem[s % 2], 16) for s in range(48)]
    mm_t = {(s, nq): mm0 + 4 * s + max(0, min(s - 1, 32)) + nq + 1
            for s in range(48) for nq in range(4)}
    mmq = {s: mm0 + 5 * s + 9 for s in range(32)}
    ct["mm"] = mm0 + 192 + 32 + 1
    mm_bc = ct["mm"]
    act_t = {(s, nq): act0b + 4 * s + nq + 1 for s in range(48) for nq in range(4)}
    ct["act"] = act0b + 192
    st_t = [inc(c, ssem[s % 2], 16) for s in range(48)]

    xn_dve = ct["dve"]
    sq_inc = {}
    red_done = {}
    plan = []
    for s in range(32):
        plan.append(("sq", s))
        if s >= 2:
            plan.append(("red", s - 2))
    plan += [("red", 30), ("red", 31), ("negc", 0), ("tbl", 0)]
    for op, s in plan:
        if op == "sq":
            sq_inc[s] = inc(c, "dve")
        elif op == "red":
            red_done[s] = inc(c, "dve", 2 if s >= 16 else 1)
        else:
            inc(c, "dve")
    dve_negrow = ct["dve"] - 1
    dve_tbl = ct["dve"]

    def g1(eng):
        for s in range(48):
            if s >= 2:
                eng.wait_ge(sm["mm"], mm_t[(s - 2, 3)])
            eng.dma_start(out=c.wbuf[:, (s % 2) * 4096:(s % 2 + 1) * 4096]
                          .rearrange("p (f q) -> p f q", f=32),
                          in_=c.wqkv[s].rearrange("f p q -> p f q")
                          ).then_inc(sm[wsem[s % 2]], 16)
    c.G(g1)

    def t2(eng):
        eng.wait_ge(sm["dve"], xn_dve)
        ndsts = [c.Pf[0:1, 0:512], c.Pf[0:1, 512:1024],
                 c.P1[2][0:1, :], c.P1[3][0:1, :]]
        for s in range(48):
            if 2 <= s < 34:
                sn = s - 2
                eng.wait_ge(sm["dve"], sq_inc[sn])
                if sn >= 1:
                    eng.wait_ge(sm["dve"], red_done[sn - 1])
                sqb = c.stg2 if sn % 2 == 0 else c.p4
                for j in range(4):
                    mmi = eng.matmul(ndsts[j], lhsT=c.onesb[:, 0:1],
                                     rhs=sqb[:, j * 512:(j + 1) * 512],
                                     start=True, stop=True)
                mmi.then_inc(sm["mm"], 1)
            eng.wait_ge(sm[wsem[s % 2]], wt[s])
            for nq in range(4):
                ps = c.P1[nq % 2]
                e = 4 * s + nq
                if e >= 2:
                    eng.wait_ge(sm["act"], act0b + e - 1)
                for fc in range(32):
                    mmi = eng.matmul(
                        ps[:, :],
                        lhsT=c.wbuf[:, (s % 2) * 4096 + fc * 128:
                                    (s % 2) * 4096 + (fc + 1) * 128],
                        rhs=c.big[:, fc * 2048 + nq * 512:fc * 2048 + (nq + 1) * 512],
                        start=(fc == 0), stop=(fc == 31))
                mmi.then_inc(sm["mm"], 1)
        eng.wait_ge(sm["dve"], dve_negrow)
        eng.matmul(c.P1[2][:, 0:64], lhsT=c.ones1f[0:1, :],
                   rhs=c.fcc[0:1, 96:160], start=True,
                   stop=True).then_inc(sm["mm"], 1)
    c.T(t2)

    def v_norms(eng):
        for op, s in plan:
            if op == "sq":
                eng.wait_ge(sm["act"], act_t[(s, 3)])
                if s >= 2:
                    eng.wait_ge(sm["mm"], mmq[s - 2])
                sqb = c.stg2 if s % 2 == 0 else c.p4
                src = c.stg[:, (s % 2) * 2048:(s % 2 + 1) * 2048]
                eng.tensor_tensor(out=sqb[:, 0:2048], in0=src, in1=src,
                                  op=ALU.mult).then_inc(sm["dve"], 1)
            elif op == "red":
                eng.wait_ge(sm["mm"], mmq[s])
                srcs = [c.Pf[0:1, 0:512], c.Pf[0:1, 512:1024],
                        c.P1[2][0:1, :], c.P1[3][0:1, :]]
                if s < 16:
                    for j in range(4):
                        ri = eng.tensor_reduce(
                            out=c.fcc[0:1, s * 4 + j:s * 4 + j + 1], in_=srcs[j],
                            axis=mybir.AxisListType.X, op=ALU.max)
                    ri.then_inc(sm["dve"], 1)
                else:
                    for j in range(4):
                        ri = eng.tensor_reduce(
                            out=c.fcc[0:1, 80 + j:81 + j], in_=srcs[j],
                            axis=mybir.AxisListType.X, op=ALU.max)
                    ri.then_inc(sm["dve"], 1)
                    eng.wait_ge(sm["dve"], red_done[s] - 1)
                    eng.tensor_reduce(out=c.fcc[0:1, 48 + s:49 + s],
                                      in_=c.fcc[0:1, 80:84],
                                      axis=mybir.AxisListType.X,
                                      op=ALU.max).then_inc(sm["dve"], 1)
            elif op == "negc":
                eng.wait_ge(sm["dve"], red_done[31])
                for h in range(16):
                    ti = eng.tensor_scalar(
                        out=c.fcc[0:1, 96 + h * 4:100 + h * 4],
                        in0=c.fcc[0:1, h * 4:h * 4 + 4],
                        scalar1=c.fcc[0:1, 64 + h:65 + h], scalar2=-0.5,
                        op0=ALU.add, op1=ALU.mult)
                ti.then_inc(sm["dve"], 1)
            else:
                eng.wait_ge(sm["mm"], mm_bc)
                eng.tensor_copy(out=c.fcc[:, 512:576],
                                in_=c.P1[2][:, 0:64]).then_inc(sm["dve"], 1)
    c.V(v_norms)

    def a2(eng):
        for s in range(48):
            scale = SC_H if s < 32 else 1.0
            for nq in range(4):
                eng.wait_ge(sm["mm"], mm_t[(s, nq)])
                if s >= 2 and nq == 0:
                    eng.wait_ge(sm[ssem[s % 2]], st_t[s - 2])
                eng.activation(out=c.stg[:, (s % 2) * 2048 + nq * 512:
                                         (s % 2) * 2048 + (nq + 1) * 512],
                               in_=c.P1[nq % 2][:, :], func=AF.Identity,
                               bias=c.bqkv_s[:, s:s + 1],
                               scale=scale).then_inc(sm["act"], 1)
    c.A(a2)

    def y3(eng):
        for s in range(48):
            eng.wait_ge(sm["act"], act_t[(s, 3)])
            src = c.stg[:, (s % 2) * 2048:(s % 2 + 1) * 2048]
            if s < 16:
                dst = c.qA[s]
            elif s < 32:
                dst = c.kA[s - 16]
            else:
                dst = c.vA[s - 32]
            eng.dma_start(out=dst, in_=src).then_inc(sm[ssem[s % 2]], 16)
    c.Y(y3)
    barrier(c)

    if c.kph < 5:
        return
    def ya1done(eng, _a=ct["dSa"], _b=ct["dSb"]):
        eng.wait_ge(sm["dSa"], _a)
        eng.wait_ge(sm["dSb"], _b)
    c.Y(ya1done)
    c._a2p = {}
    nh = min(16, int(os.environ.get("KPH_H", "16")))
    _a2_pre(c, 0, 0, guard=None)
    _a2_pre(c, 0, 1)
    for hh in range(nh):
        _a2_head(c, hh, nh)
    barrier(c)

    if c.kph < 6:
        return
    def gcc(eng, _ca=c.cnt["dCa"], _cb=c.cnt["dCb"]):
        eng.wait_ge(sm["dCa"], _ca)
        eng.wait_ge(sm["dCb"], _cb)
        eng.collective_compute(
            "AllGather", ALU.bypass,
            replica_groups=[[0, 1], [2, 3], [4, 5], [6, 7]],
            ins=[c.ccA_in.rearrange("h p q -> (h p) q")],
            outs=[c.ccA_out.rearrange("r h p q -> (r h p) q")],
        ).then_inc(sm["cc"], 1)
    c.G(gcc)
    ncc = inc(c, "cc")
    barrier(c)

    for lc in range(32):
        r, j = lc // 16, lc % 16
        buf = c.kbuf if lc % 2 == 0 else c.vbuf
        tmp = c.stg2[:, 0:1024] if lc % 2 == 0 else c.stg2[:, 1024:2048]
        asem = "dKs" if lc % 2 == 0 else "dVs"
        dv_prev = c.cnt["dve"]
        nA = inc(c, asem, 16)
        nB = inc(c, asem, 16)

        def ya(eng, r=r, j=j, buf=buf, lc=lc, ncc=ncc, dv_prev=dv_prev, asem=asem):
            if lc == 0:
                eng.wait_ge(sm["cc"], ncc)
            if lc >= 2:
                eng.wait_ge(sm["dve"], dv_prev - 1)
            eng.dma_start(out=buf[:, 0:1024],
                          in_=c.ccA_out[r, j][:, 0:1024]).then_inc(sm[asem], 16)
            eng.dma_start(out=buf[:, 1024:2048],
                          in_=c.ccA_out[r, j][:, 1024:2048]).then_inc(sm[asem], 16)
        c.Y(ya)

        ndsel = inc(c, "dve", 2)

        def va(eng, buf=buf, tmp=tmp, lc=lc, nB=nB, asem=asem, ndsel=ndsel):
            eng.wait_ge(sm[asem], nB)
            eng.tensor_scalar(out=tmp, in0=buf[:, 0:1024],
                              scalar1=c.g_sel0, scalar2=None,
                              op0=ALU.mult).then_inc(sm["dve"], 1)
            eng.wait_ge(sm["dve"], ndsel - 1)
            eng.scalar_tensor_tensor(
                out=c.big[:, 32768 + lc * 1024:32768 + (lc + 1) * 1024],
                in0=buf[:, 1024:2048], scalar=c.g_sel1,
                in1=tmp, op0=ALU.mult,
                op1=ALU.add).then_inc(sm["dve"], 1)
        c.V(va)
    barrier(c)

    def gob(eng):
        eng.dma_start(out=c.browz[0:1, 0:4096], in_=c.ob).then_inc(sm["dW"], 16)
    c.G(gob)
    inc(c, "dW", 16)
    barrier(c)
    _oproj_residual(c, wblocks=c.wo, nof=8, nlat=32, latbase=32768,
                    bias_row=c.browz[0:1, 0:4096], xsrc=c.x_own,
                    emit_out=(c.stage == 1), tb_count=8)


def _a2_slots(c, hh):
    if hh % 2 == 0:
        return dict(kb=c.kbuf[:, :], vb=c.vbuf[:, :], qt=c.qtl[:, :],
                    vs=c.stg[:, 0:2048], negbase=144)
    B = c.big
    return dict(kb=B[:, 36864:38912], qt=B[:, 38912:40960],
                vs=B[:, 40960:43008], vb=B[:, 43008:45056], negbase=140)


def _a2_pre(c, hh, phase, guard=None):
    sm = c.sem
    S = _a2_slots(c, hh)
    st = c._a2p.setdefault(hh, {})

    if phase == 0:
        prev = c._a2p.get(hh - 1)
        st["nK"] = inc(c, "dKs", 16)
        st["nV"] = inc(c, "dVs", 16)
        st["nQ"] = inc(c, "dQs", 16)

        def y(eng, hh=hh, S=S, guard=guard, prev=prev):
            if guard is not None:
                eng.wait_ge(sm["mm"], guard)
            if prev is not None:
                eng.wait_ge(sm["dKs"], prev["nK"])
                eng.wait_ge(sm["dVs"], prev["nV"])
                eng.wait_ge(sm["dQs"], prev["nQ"])
            eng.dma_start(out=S["kb"], in_=c.kA[hh]).then_inc(sm["dKs"], 16)
            eng.dma_start(out=S["vs"], in_=c.vA[hh]).then_inc(sm["dVs"], 16)
            eng.dma_start(out=S["qt"], in_=c.qA[hh]).then_inc(sm["dQs"], 16)
        c.Y(y)

    elif phase == 1:
        def tv(eng, S=S, nV=st["nV"]):
            eng.wait_ge(sm["dVs"], nV)
            for rnd in range(2):
                for j in range(8):
                    kb = rnd * 8 + j
                    ti = eng.transpose(
                        c.Pb[:, rnd * 1024 + j * 128:rnd * 1024 + (j + 1) * 128],
                        S["vs"][:, kb * 128:(kb + 1) * 128], c.identb)
                ti.then_inc(sm["mm"], 1)
        c.T(tv)
        mmv = [inc(c, "mm"), inc(c, "mm")]

        def vv(eng, S=S, mmv=mmv):
            for rnd in range(2):
                eng.wait_ge(sm["mm"], mmv[rnd])
                eng.tensor_copy(out=S["vb"][:, rnd * 1024:(rnd + 1) * 1024],
                                in_=c.Pb[:, rnd * 1024:(rnd + 1) * 1024]
                                ).then_inc(sm["dve"], 1)
        c.V(vv)
        inc(c, "dve")
        st["ndv_vb"] = inc(c, "dve")


def _a2_head(c, hh, nh):
    sm = c.sem
    S = _a2_slots(c, hh)
    st = c._a2p[hh]
    nxt = hh + 1 if hh + 1 < nh else None
    Sn = _a2_slots(c, nxt) if nxt is not None else None

    mmg = c.cnt["mm"]
    if nxt is not None:
        stn = c._a2p.setdefault(nxt, {})
        stn["nK"] = inc(c, "dKs", 16)
        stn["nV"] = inc(c, "dVs", 16)
        stn["nQ"] = inc(c, "dQs", 16)
    mm_sc, act_e, mm_pv = {}, {}, {}
    ndv5, ndv6, nmm_rl, cprev = {}, {}, {}, {}
    for qt in range(4):
        for kbg in range(8):
            mm_sc[(qt, kbg)] = inc(c, "mm")
            act_e[(qt, kbg)] = inc(c, "act", 1 if kbg % 2 == 0 else 2)
            mm_pv[(qt, kbg)] = inc(c, "mm")
        ndv5[qt] = inc(c, "dve", 2)
        nmm_rl[qt] = inc(c, "mm")
        ndv6[qt] = inc(c, "dve", 2)
        csem = "dCa" if qt % 2 == 0 else "dCb"
        cprev[qt] = c.cnt[csem]
        inc(c, csem, 16)
        if qt == 0 and nxt is not None:
            stn["mmv"] = [inc(c, "mm"), inc(c, "mm")]
            inc(c, "dve")
            stn["ndv_vb"] = inc(c, "dve")

    def y_all(eng):
        if nxt is not None:
            eng.wait_ge(sm["mm"], mmg)
            eng.wait_ge(sm["dKs"], st["nK"])
            eng.wait_ge(sm["dVs"], st["nV"])
            eng.wait_ge(sm["dQs"], st["nQ"])
            eng.dma_start(out=Sn["kb"], in_=c.kA[nxt]).then_inc(sm["dKs"], 16)
            eng.dma_start(out=Sn["vs"], in_=c.vA[nxt]).then_inc(sm["dVs"], 16)
            eng.dma_start(out=Sn["qt"], in_=c.qA[nxt]).then_inc(sm["dQs"], 16)
        for qt in range(4):
            csem = "dCa" if qt % 2 == 0 else "dCb"
            eng.wait_ge(sm["dve"], ndv6[qt])
            eng.dma_start(out=c.ccA_in[hh][:, qt * 512:(qt + 1) * 512],
                          in_=c.stg2[:, (qt % 2) * 512:(qt % 2 + 1) * 512]
                          ).then_inc(sm[csem], 16)
    c.Y(y_all)

    def t_all(eng):
        for qt in range(4):
            for kbg in range(8):
                if kbg >= 2:
                    eng.wait_ge(sm["act"], act_e[(qt, kbg - 2)])
                if qt == 0 and kbg == 0:
                    eng.wait_ge(sm["dKs"], st["nK"])
                    eng.wait_ge(sm["dQs"], st["nQ"])
                sdst = ([c.Pf[:, 0:512], c.Pf[:, 512:1024]] if kbg % 2 == 0
                        else [c.P1[3][:, :], c.P1[2][:, :]])
                for j in range(2):
                    kb = kbg * 2 + j
                    mmi = eng.matmul(sdst[j],
                                     lhsT=S["kb"][:, kb * 128:(kb + 1) * 128],
                                     rhs=S["qt"][:, qt * 512:(qt + 1) * 512],
                                     start=True, stop=True)
                mmi.then_inc(sm["mm"], 1)
                eng.wait_ge(sm["act"], act_e[(qt, kbg)])
                if qt == 0 and kbg == 0:
                    eng.wait_ge(sm["dve"], st["ndv_vb"])
                for j in range(2):
                    kb = kbg * 2 + j
                    rh = c.p4[:, (kbg % 2) * 1024 + j * 512:
                              (kbg % 2) * 1024 + (j + 1) * 512]
                    eng.matmul(c.P1[0][:, :],
                               lhsT=S["vb"][:, kb * 128:(kb + 1) * 128],
                               rhs=rh, start=(kb == 0), stop=(kb == 15))
                    mmi = eng.matmul(c.P1[1][0:1, :], lhsT=c.onesb[:, 0:1],
                                     rhs=rh, start=(kb == 0), stop=(kb == 15))
                mmi.then_inc(sm["mm"], 1)
            eng.wait_ge(sm["dve"], ndv5[qt])
            eng.matmul(c.P1[2][:, :], lhsT=c.ones1b[0:1, :],
                       rhs=c.browz[0:1, 0:512], start=True,
                       stop=True).then_inc(sm["mm"], 1)
            eng.wait_ge(sm["dve"], ndv6[qt])
            if qt == 0 and nxt is not None:
                eng.wait_ge(sm["dVs"], stn["nV"])
                for rnd in range(2):
                    for j in range(8):
                        kb = rnd * 8 + j
                        ti = eng.transpose(
                            c.Pb[:, rnd * 1024 + j * 128:rnd * 1024 + (j + 1) * 128],
                            Sn["vs"][:, kb * 128:(kb + 1) * 128], c.identb)
                    ti.then_inc(sm["mm"], 1)
    c.T(t_all)

    def a_all(eng):
        bias = None
        for qt in range(4):
            bias = c.fcc[:, 512 + hh * 4 + qt:513 + hh * 4 + qt]
            for kbg in range(8):
                eng.wait_ge(sm["mm"], mm_sc[(qt, kbg)])
                if kbg >= 2:
                    eng.wait_ge(sm["mm"], mm_pv[(qt, kbg - 2)])
                po = (kbg % 2) * 1024
                if kbg % 2 == 0:
                    eng.activation(out=c.p4[:, po:po + 1024],
                                   in_=c.Pf[:, :], func=AF.Exp,
                                   bias=bias).then_inc(sm["act"], 1)
                else:
                    eng.activation(out=c.p4[:, po:po + 512],
                                   in_=c.P1[3][:, :], func=AF.Exp,
                                   bias=bias).then_inc(sm["act"], 1)
                    eng.activation(out=c.p4[:, po + 512:po + 1024],
                                   in_=c.P1[2][:, :], func=AF.Exp,
                                   bias=bias).then_inc(sm["act"], 1)
    c.A(a_all)

    def v_all(eng):
        for qt in range(4):
            eng.wait_ge(sm["mm"], mm_pv[(qt, 7)])
            eng.reciprocal(out=c.fcc[0:1, 0:512],
                           in_=c.P1[1][0:1, :]).then_inc(sm["dve"], 1)
            eng.wait_ge(sm["dve"], ndv5[qt] - 1)
            eng.tensor_copy(out=c.browz[0:1, 0:512],
                            in_=c.fcc[0:1, 0:512]).then_inc(sm["dve"], 1)
            eng.wait_ge(sm["mm"], nmm_rl[qt])
            eng.wait_ge(sm["dCa" if qt % 2 == 0 else "dCb"], cprev[qt])
            eng.tensor_copy(out=c.stg2[:, 1024:1536],
                            in_=c.P1[2][:, :]).then_inc(sm["dve"], 1)
            eng.wait_ge(sm["dve"], ndv6[qt] - 1)
            eng.tensor_tensor(out=c.stg2[:, (qt % 2) * 512:(qt % 2 + 1) * 512],
                              in0=c.P1[0][:, :], in1=c.stg2[:, 1024:1536],
                              op=ALU.mult).then_inc(sm["dve"], 1)
            if qt == 0 and nxt is not None:
                for rnd in range(2):
                    eng.wait_ge(sm["mm"], stn["mmv"][rnd])
                    eng.tensor_copy(out=Sn["vb"][:, rnd * 1024:(rnd + 1) * 1024],
                                    in_=c.Pb[:, rnd * 1024:(rnd + 1) * 1024]
                                    ).then_inc(sm["dve"], 1)
    c.V(v_all)




def _oproj_residual(c, wblocks, nof, nlat, latbase, bias_row, xsrc, emit_out, tb_count):
    sm = c.sem
    c._op_prev_mm = None
    c._op_prev_st = None
    nlatq = nlat // 4
    for of in range(nof):
        for tbg in range(tb_count // 4):
            prev_st = getattr(c, "_op_prev_st", None)
            mm_q0 = c.cnt["mm"]
            wq_t = [inc(c, "dWa" if q % 2 == 0 else "dWb", 16) for q in range(4)]

            first_grp = (of == 0 and tbg == 0)

            def gw(eng, of=of, tbg=tbg, wq_t=wq_t, mm_q0=mm_q0, first_grp=first_grp):
                for fcq in range(4):
                    if fcq >= 2:
                        eng.wait_ge(sm["mm"], mm_q0 + fcq)
                    elif not first_grp:
                        eng.wait_ge(sm["mm"], mm_q0)
                    eng.dma_start(
                        out=c.wbuf[:, (fcq % 2) * 4096:
                                   (fcq % 2) * 4096 + nlatq * 512]
                        .rearrange("p (f q) -> p f q", f=nlatq),
                        in_=wblocks[of, fcq * nlatq:(fcq + 1) * nlatq]
                        .rearrange("f p q -> p f q")
                        ).then_inc(sm["dWa" if fcq % 2 == 0 else "dWb"], 16)
            c.G(gw)

            def tchain(eng, of=of, tbg=tbg, wq_t=wq_t, dw0=c.cnt["dW"],
                       first_grp=first_grp):
                if first_grp and bias_row is not None:
                    eng.wait_ge(sm["dW"], dw0)
                for fcq in range(4):
                    eng.wait_ge(sm["dWa" if fcq % 2 == 0 else "dWb"], wq_t[fcq])
                    for tb4 in range(4):
                        for j in range(nlatq):
                            fc = fcq * nlatq + j
                            tb = tbg * 4 + tb4
                            mmi = eng.matmul(
                                c.P1[tb4][:, :],
                                lhsT=c.big[:, latbase + fc * 1024 + tb * 128:
                                           latbase + fc * 1024 + (tb + 1) * 128],
                                rhs=c.wbuf[:, (fcq % 2) * 4096 + j * 512:
                                           (fcq % 2) * 4096 + (j + 1) * 512],
                                start=(fcq == 0 and j == 0),
                                stop=(bias_row is None and fcq == 3 and j == nlatq - 1))
                    mmi.then_inc(sm["mm"], 1)
                if bias_row is not None:
                    for tb4 in range(4):
                        mmi = eng.matmul(c.P1[tb4][:, :], lhsT=c.ones1b[0:1, :],
                                         rhs=bias_row[0:1, of * 512:(of + 1) * 512],
                                         start=False, stop=True)
                    mmi.then_inc(sm["mm"], 1)
            c.T(tchain)
            inc(c, "mm", 3 if bias_row is None else 4)
            mm_last = inc(c, "mm")

            xs_t = []
            if xsrc is not None:
                def yx(eng, of=of, tbg=tbg, xs_t=xs_t, ndvp=c.cnt["dve"]):
                    eng.wait_ge(sm["dve"], ndvp)
                    for tb4 in range(4):
                        tb = tbg * 4 + tb4
                        eng.dma_start(out=c.fa[:, tb4 * 512:(tb4 + 1) * 512],
                                      in_=xsrc[tb * 128:(tb + 1) * 128,
                                               of * 512:(of + 1) * 512]
                                      ).then_inc(sm["dQ"], 16)
                c.Y(yx)
                for tb4 in range(4):
                    xs_t.append(inc(c, "dQ", 16))

            dv0 = c.cnt["dve"]

            def vev(eng, of=of, tbg=tbg, mm_last=mm_last, xs_t=xs_t,
                    prev_st=prev_st, dv0=dv0):
                eng.wait_ge(sm["mm"], mm_last)
                if xs_t:
                    eng.wait_ge(sm["dQ"], xs_t[3])
                if prev_st is not None:
                    eng.wait_ge(sm["dO"], prev_st)
                d = dv0
                for tb4 in range(4):
                    tb = tbg * 4 + tb4
                    xin = (c.fa[:, tb4 * 512:(tb4 + 1) * 512] if xs_t else
                           c.big[:, tb * 4096 + of * 512:tb * 4096 + (of + 1) * 512])
                    eng.tensor_tensor(out=c.fcc[:, tb4 * 512:(tb4 + 1) * 512],
                                      in0=c.P1[tb4][:, :],
                                      in1=xin,
                                      op=ALU.add).then_inc(sm["dve"], 1)
                    d += 1
                    eng.wait_ge(sm["dve"], d)
                    eng.tensor_copy(
                        out=c.big[:, tb * 4096 + of * 512:tb * 4096 + (of + 1) * 512],
                        in_=c.fcc[:, tb4 * 512:(tb4 + 1) * 512]).then_inc(sm["dve"], 1)
                    d += 1
            c.V(vev)
            ndv = inc(c, "dve", 8)

            if emit_out:
                def yo(eng, of=of, tbg=tbg, ndv=ndv):
                    eng.wait_ge(sm["dve"], ndv)
                    for tb4 in range(4):
                        tb = tbg * 4 + tb4
                        eng.dma_start(out=c.out[tb * 128:(tb + 1) * 128,
                                                of * 512:(of + 1) * 512],
                                      in_=c.fcc[:, tb4 * 512:(tb4 + 1) * 512]
                                      ).then_inc(sm["dO"], 16)
                c.Y(yo)
                c._op_prev_st = inc(c, "dO", 64)
            def twv(eng, ndv=ndv):
                eng.wait_ge(sm["dve"], ndv)
            c.T(twv)
            c._op_prev_mm = mm_last
    c._op_prev_mm = None
    c._op_prev_st = None
    barrier(c)


def _load_row(c, src_ap, width):
    sm = c.sem
    n = inc(c, "dW", 16)

    def g(eng, n=n):
        eng.dma_start(out=c.browz[0:1, 0:width], in_=src_ap).then_inc(sm["dW"], 16)
    c.G(g)
    return n


def _bcast_row(c, dst, width, row_n):
    sm = c.sem
    prev_ndv = None
    for rnd in range(width // 1024):
        nmm = inc(c, "mm")

        def t(eng, rnd=rnd, nmm=nmm, row_n=row_n, prev_ndv=prev_ndv):
            eng.wait_ge(sm["dW"], row_n)
            if prev_ndv is not None:
                eng.wait_ge(sm["dve"], prev_ndv)
            for n0 in range(2):
                mmi = eng.matmul(c.Pf[:, n0 * 512:(n0 + 1) * 512],
                                 lhsT=c.ones1b[0:1, :],
                                 rhs=c.browz[0:1, rnd * 1024 + n0 * 512:
                                             rnd * 1024 + (n0 + 1) * 512],
                                 start=True, stop=True)
            mmi.then_inc(sm["mm"], 1)
        c.T(t)
        ndv = inc(c, "dve")

        def v(eng, rnd=rnd, nmm=nmm, ndv=ndv):
            eng.wait_ge(sm["mm"], nmm)
            eng.tensor_copy(out=dst[:, rnd * 1024:(rnd + 1) * 1024],
                            in_=c.Pf[:, :]).then_inc(sm["dve"], 1)
        c.V(v)
        prev_ndv = ndv
    barrier(c)



def _ln_stats(c):
    sm = c.sem
    cl = c.cols
    for tc in range(8):
        ndv = inc(c, "dve", 3)

        def vst(eng, tc=tc, ndv=ndv):
            d = ndv - 3
            for j in range(8):
                bi = eng.bn_stats(out=c.fa[0:128, j * 6:(j + 1) * 6],
                                  in_=c.big[:, tc * 4096 + j * 512:
                                            tc * 4096 + (j + 1) * 512])
            bi.then_inc(sm["dve"], 1)
            d += 1
            eng.wait_ge(sm["dve"], d)
            eng.bn_aggr(out=c.fa[:, 48:50], in_=c.fa[:, 0:48]).then_inc(sm["dve"], 1)
            d += 1
            eng.wait_ge(sm["dve"], d)
            eng.tensor_copy(out=cl[:, 64 + tc:65 + tc],
                            in_=c.fa[:, 48:49]).then_inc(sm["dve"], 1)
        c.V(vst)
        na = inc(c, "act")

        def ast(eng, tc=tc, ndv=ndv, na=na):
            eng.wait_ge(sm["dve"], ndv)
            eng.activation(out=c.fa[:, 50:51], in_=c.fa[:, 49:50], func=AF.Sqrt,
                           bias=cl[:, 148:149]).then_inc(sm["act"], 1)
        c.A(ast)
        ndv2 = inc(c, "dve", 3)

        def vst2(eng, tc=tc, na=na, ndv2=ndv2):
            d = ndv2 - 3
            eng.wait_ge(sm["act"], na)
            eng.reciprocal(out=cl[:, 0 + tc:1 + tc],
                           in_=c.fa[:, 50:51]).then_inc(sm["dve"], 1)
            d += 1
            eng.wait_ge(sm["dve"], d)
            eng.tensor_tensor(out=c.fa[:, 51:52], in0=cl[:, 64 + tc:65 + tc],
                              in1=cl[:, 0 + tc:1 + tc],
                              op=ALU.mult).then_inc(sm["dve"], 1)
            d += 1
            eng.wait_ge(sm["dve"], d)
            eng.tensor_scalar(out=cl[:, 8 + tc:9 + tc], in0=c.fa[:, 51:52],
                              scalar1=-1.0, scalar2=None,
                              op0=ALU.mult).then_inc(sm["dve"], 1)
        c.V(vst2)
    barrier(c)

def emit_ffn(c, tag, emit_out):
    sm, ct = c.sem, c.cnt
    XRES = 0
    QG = 32768
    GS = 49152
    INT = 57344
    cl = c.cols
    barrier(c)

    _ln_stats(c)

    fstop = os.environ.get("FFN_STOP", "")
    ti = 0
    for th in range(1 if fstop else 2):
        rn = _load_row(c, c.invg, 4096)
        _bcast_row(c, c.fb, 4096, rn)
        for tcl in range(4):
            tc = th * 4 + tcl
            _ffn_quant_tile(
                c, src=c.big[:, XRES + tc * 4096:XRES + (tc + 1) * 4096],
                a_col=cl[:, tc:tc + 1], b_col=cl[:, 8 + tc:9 + tc],
                inv_b=c.fb, s_col=cl[:, 16 + tc:17 + tc],
                rs_slot=cl[:, 40 + tc:41 + tc],
                width=4096, qdst=QG, tcl=tcl, use_ln=True, ti=ti)
            ti += 1
        barrier(c)
        if fstop == "quantg":
            return
        rn = _load_row(c, c.swg, 2048)
        _bcast_row(c, c.p4, 2048, rn)
        _ffn_mm(c, c.wgT, nsl=4, nk=32, qbase=QG, th=th, evac="gate", gsbase=GS)
        barrier(c)
        if fstop == "gate":
            return
        rn = _load_row(c, c.invu, 4096)
        _bcast_row(c, c.fb, 4096, rn)
        for tcl in range(4):
            tc = th * 4 + tcl
            _ffn_quant_tile(
                c, src=c.big[:, XRES + tc * 4096:XRES + (tc + 1) * 4096],
                a_col=cl[:, tc:tc + 1], b_col=cl[:, 8 + tc:9 + tc],
                inv_b=c.fb, s_col=cl[:, 24 + tc:25 + tc],
                rs_slot=cl[:, 48 + tc:49 + tc],
                width=4096, qdst=QG, tcl=tcl, use_ln=True, ti=ti)
            ti += 1
        barrier(c)
        rn = _load_row(c, c.swu, 2048)
        _bcast_row(c, c.p4, 2048, rn)
        _ffn_mm(c, c.wuT, nsl=4, nk=32, qbase=QG, th=th, evac="up",
                gsbase=GS, intbase=INT)
        barrier(c)
        if fstop == "up":
            return
        rn = _load_row(c, c.invi, 2048)
        _bcast_row(c, c.p4, 2048, rn)
        for tcl in range(4):
            tc = th * 4 + tcl
            _ffn_quant_tile(
                c, src=c.big[:, INT + tcl * 2048:INT + (tcl + 1) * 2048],
                a_col=None, b_col=None, inv_b=c.p4,
                s_col=cl[:, 32 + tc:33 + tc], rs_slot=cl[:, 56 + tc:57 + tc],
                width=2048, qdst=QG, tcl=tcl, use_ln=False, ti=ti)
            ti += 1
        barrier(c)
        if fstop == "requant":
            return
        rn = _load_row(c, c.swd, 4096)
        _bcast_row(c, c.fb, 4096, rn)
        _ffn_mm(c, c.wdT, nsl=8, nk=16, qbase=QG, th=th, evac="down",
                emit_out=emit_out)
        barrier(c)


def _ffn_quant_tile(c, src, a_col, b_col, inv_b, s_col, rs_slot, width, qdst,
                    tcl, use_ln, ti):
    sm = c.sem
    p = ti % 2
    xs = c.wbuf[:, p * 4096:p * 4096 + width]
    MAGIC = 12582912.0
    prev = getattr(c, "_fqt_prev", {})
    ndv = inc(c, "dve", 4 if use_ln else 3)

    def v(eng, ndv=ndv, p=p, use_ln=use_ln, pw=prev.get(("w", p)),
          nops=(4 if use_ln else 3)):
        d = ndv - nops
        if pw is not None:
            eng.wait_ge(sm["mm"], pw)
        if use_ln:
            eng.scalar_tensor_tensor(out=xs, in0=src, scalar=a_col, in1=inv_b[:, 0:width],
                                     op0=ALU.mult, op1=ALU.mult).then_inc(sm["dve"], 1)
            d += 1
            eng.wait_ge(sm["dve"], d)
            eng.scalar_tensor_tensor(out=xs, in0=inv_b[:, 0:width], scalar=b_col,
                                     in1=xs, op0=ALU.mult,
                                     op1=ALU.add).then_inc(sm["dve"], 1)
            d += 1
            eng.wait_ge(sm["dve"], d)
        else:
            eng.tensor_tensor(out=xs, in0=src, in1=inv_b[:, 0:width],
                              op=ALU.mult).then_inc(sm["dve"], 1)
            d += 1
            eng.wait_ge(sm["dve"], d)
        eng.tensor_reduce(out=s_col, in_=xs, axis=mybir.AxisListType.X,
                          op=ALU.max, apply_absolute_value=True).then_inc(sm["dve"], 1)
        d += 1
        eng.wait_ge(sm["dve"], d)
        eng.tensor_scalar(out=s_col, in0=s_col, scalar1=1.0 / 127.0,
                          scalar2=1e-8, op0=ALU.mult,
                          op1=ALU.max).then_inc(sm["dve"], 1)
    c.V(v)
    ndv2 = inc(c, "dve")

    def v2(eng, ndv2=ndv2):
        eng.wait_ge(sm["dve"], ndv2 - 1)
        eng.reciprocal(out=rs_slot, in_=s_col).then_inc(sm["dve"], 1)
    c.V(v2)

    for fh in range(width // 1024):
        fp = (ti * (width // 1024) + fh) % 2
        fcb = c.fcc[:, fp * 1024:(fp + 1) * 1024]
        qtm = c.stg[:, fp * 2048:fp * 2048 + 1024]
        ndq = inc(c, "dve", 3)

        def vq(eng, fh=fh, fp=fp, fcb=fcb, qtm=qtm, ndq=ndq, ndv2=ndv2,
               do0=c.cnt["dO"], pq=prev.get(("q", fp))):
            d = ndq - 3
            eng.wait_ge(sm["dve"], ndv2)
            eng.wait_ge(sm["dO"], do0)
            if pq is not None:
                eng.wait_ge(sm["mm"], pq)
            eng.tensor_scalar(out=fcb, in0=xs[:, fh * 1024:(fh + 1) * 1024],
                              scalar1=rs_slot, scalar2=None,
                              op0=ALU.mult).then_inc(sm["dve"], 1)
            d += 1
            eng.wait_ge(sm["dve"], d)
            eng.tensor_scalar(out=fcb, in0=fcb, scalar1=MAGIC,
                              scalar2=MAGIC, op0=ALU.add,
                              op1=ALU.subtract).then_inc(sm["dve"], 1)
            d += 1
            eng.wait_ge(sm["dve"], d)
            eng.tensor_scalar(out=qtm, in0=fcb,
                              scalar1=127.0, scalar2=-127.0, op0=ALU.min,
                              op1=ALU.max).then_inc(sm["dve"], 1)
        c.V(vq)
        nmm = inc(c, "mm")

        def tq(eng, fp=fp, qtm=qtm, ndq=ndq, nmm=nmm,
               pv=prev.get(("pb", fp))):
            eng.wait_ge(sm["dve"], ndq)
            if pv is not None:
                eng.wait_ge(sm["dve"], pv)
            for j in range(8):
                ti_ = eng.transpose(c.Pb[:, fp * 1024 + j * 128:fp * 1024 + (j + 1) * 128],
                                    qtm[:, j * 128:(j + 1) * 128], c.identb)
            ti_.then_inc(sm["mm"], 1)
        c.T(tq)
        ndq2 = inc(c, "dve")

        def vq2(eng, fh=fh, fp=fp, nmm=nmm, ndq2=ndq2, qdst=qdst, tcl=tcl):
            eng.wait_ge(sm["mm"], nmm)
            for j in range(8):
                fc = fh * 8 + j
                cpi = eng.tensor_copy(
                    out=c.big[:, qdst + fc * 512 + tcl * 128:
                              qdst + fc * 512 + (tcl + 1) * 128],
                    in_=c.Pb[:, fp * 1024 + j * 128:fp * 1024 + (j + 1) * 128])
            cpi.then_inc(sm["dve"], 1)
        c.V(vq2)
        prev[("q", fp)] = nmm
        prev[("pb", fp)] = ndq2
        prev[("w", p)] = nmm
    c._fqt_prev = prev


def _ffn_mm(c, wblocks, nsl, nk, qbase, th, evac, emit_out=False, gsbase=None,
            intbase=None):
    sm = c.sem
    cl = c.cols
    nld = nk // 8
    slots = [c.wbuf[:, 0:4096], c.wbuf[:, 4096:8192], c.stg[:, 0:4096]]
    ssem = ["dWa", "dWb", "dSa"]
    QB0 = c.cnt["mm"]
    ev_last = {}
    gq = 0
    for sl in range(nsl):
        wq_t = []
        loads = []
        for l in range(nld):
            g = gq + l
            wq_t.append((ssem[g % 3], inc(c, ssem[g % 3], 16)))
            loads.append((l, g))

        def gw(eng, sl=sl, loads=loads, wq_t=wq_t, QB0=QB0):
            for l, g in loads:
                if g >= 2:
                    eng.wait_ge(sm["mm"], QB0 + g - 1)
                eng.dma_start(
                    out=slots[g % 3][:, 0:4096]
                    .rearrange("p (f q) -> p f q", f=8),
                    in_=wblocks[sl, l * 8:(l + 1) * 8]
                    .rearrange("f p q -> p f q")
                    ).then_inc(sm[wq_t[l][0]], 16)
        c.G(gw)

        def tchain(eng, sl=sl, loads=loads, wq_t=wq_t, qbase=qbase,
                   evw=dict(ev_last)):
            for t4, tgt in evw.items():
                eng.wait_ge(sm[tgt[0]], tgt[1])
            for l, g in loads:
                eng.wait_ge(sm[wq_t[l][0]], wq_t[l][1])
                for tb4 in range(4):
                    for j in range(8):
                        fc = l * 8 + j
                        mmi = eng.matmul(
                            c.P1[tb4][:, :],
                            lhsT=c.big[:, qbase + fc * 512 + tb4 * 128:
                                       qbase + fc * 512 + (tb4 + 1) * 128],
                            rhs=slots[g % 3][:, j * 512:(j + 1) * 512],
                            start=(l == 0 and j == 0), stop=(l == nld - 1 and j == 7))
                mmi.then_inc(sm["mm"], 1)
        c.T(tchain)
        mm_last = inc(c, "mm", nld)
        gq += nld

        for tb4 in range(4):
            tc = th * 4 + tb4
            if evac == "gate":
                ndv = inc(c, "dve")

                def vg(eng, sl=sl, tb4=tb4, tc=tc, mm_last=mm_last, ndv=ndv,
                       pag=ev_last.get(tb4)):
                    eng.wait_ge(sm["mm"], mm_last)
                    if pag is not None:
                        eng.wait_ge(sm[pag[0]], pag[1])
                    eng.scalar_tensor_tensor(
                        out=c.fa[:, (tb4 % 4) * 512:(tb4 % 4 + 1) * 512],
                        in0=c.P1[tb4][:, :], scalar=cl[:, 16 + tc:17 + tc],
                        in1=c.p4[:, sl * 512:(sl + 1) * 512], op0=ALU.mult,
                        op1=ALU.mult).then_inc(sm["dve"], 1)
                c.V(vg)
                na = inc(c, "act")

                def ag(eng, sl=sl, tb4=tb4, ndv=ndv, na=na, gsbase=gsbase):
                    eng.wait_ge(sm["dve"], ndv)
                    eng.activation(
                        out=c.big[:, gsbase + tb4 * 2048 + sl * 512:
                                  gsbase + tb4 * 2048 + (sl + 1) * 512],
                        in_=c.fa[:, (tb4 % 4) * 512:(tb4 % 4 + 1) * 512],
                        func=AF.Silu).then_inc(sm["act"], 1)
                c.A(ag)
                ev_last[tb4] = ("act", na)
            elif evac == "up":
                ndv = inc(c, "dve", 3)

                def vu(eng, sl=sl, tb4=tb4, tc=tc, mm_last=mm_last, ndv=ndv,
                       gsbase=gsbase, intbase=intbase):
                    d = ndv - 3
                    eng.wait_ge(sm["mm"], mm_last)
                    eng.wait_ge(sm["dve"], d)
                    eng.scalar_tensor_tensor(
                        out=c.fa[:, (tb4 % 4) * 512:(tb4 % 4 + 1) * 512],
                        in0=c.P1[tb4][:, :], scalar=cl[:, 24 + tc:25 + tc],
                        in1=c.p4[:, sl * 512:(sl + 1) * 512], op0=ALU.mult,
                        op1=ALU.mult).then_inc(sm["dve"], 1)
                    d += 1
                    eng.wait_ge(sm["dve"], d)
                    eng.tensor_tensor(
                        out=c.fcc[:, 0:512],
                        in0=c.fa[:, (tb4 % 4) * 512:(tb4 % 4 + 1) * 512],
                        in1=c.big[:, gsbase + tb4 * 2048 + sl * 512:
                                  gsbase + tb4 * 2048 + (sl + 1) * 512],
                        op=ALU.mult).then_inc(sm["dve"], 1)
                    d += 1
                    eng.wait_ge(sm["dve"], d)
                    eng.tensor_scalar(
                        out=c.big[:, intbase + tb4 * 2048 + sl * 512:
                                  intbase + tb4 * 2048 + (sl + 1) * 512],
                        in0=c.fcc[:, 0:512], scalar1=10.0, scalar2=-10.0,
                        op0=ALU.min, op1=ALU.max).then_inc(sm["dve"], 1)
                c.V(vu)
                ev_last[tb4] = ("dve", ndv)
            else:
                ndv = inc(c, "dve", 2)
                prev_yd0 = getattr(c, "_ffn_prev_yd", None)

                def vd(eng, sl=sl, tb4=tb4, tc=tc, mm_last=mm_last, ndv=ndv,
                       prev_yd0=prev_yd0):
                    d = ndv - 2
                    eng.wait_ge(sm["mm"], mm_last)
                    eng.wait_ge(sm["dve"], d)
                    if prev_yd0 is not None:
                        eng.wait_ge(sm["dO"], prev_yd0)
                    eng.scalar_tensor_tensor(
                        out=c.fcc[:, 0:512],
                        in0=c.P1[tb4][:, :], scalar=cl[:, 32 + tc:33 + tc],
                        in1=c.fb[:, sl * 512:(sl + 1) * 512], op0=ALU.mult,
                        op1=ALU.mult).then_inc(sm["dve"], 1)
                    d += 1
                    eng.wait_ge(sm["dve"], d)
                    eng.tensor_tensor(
                        out=c.fcc[:, 512:1024], in0=c.fcc[:, 0:512],
                        in1=c.big[:, tc * 4096 + sl * 512:tc * 4096 + (sl + 1) * 512],
                        op=ALU.add).then_inc(sm["dve"], 1)
                c.V(vd)
                ndv2 = inc(c, "dve")

                def vd2(eng, sl=sl, tc=tc, ndv=ndv, ndv2=ndv2):
                    eng.wait_ge(sm["dve"], ndv)
                    eng.tensor_copy(
                        out=c.big[:, tc * 4096 + sl * 512:tc * 4096 + (sl + 1) * 512],
                        in_=c.fcc[:, 512:1024]).then_inc(sm["dve"], 1)
                c.V(vd2)
                ev_last[tb4] = ("dve", ndv2)
                if emit_out:
                    nO = inc(c, "dO", 16)

                    def yd(eng, sl=sl, tc=tc, ndv2=ndv2):
                        eng.wait_ge(sm["dve"], ndv2)
                        eng.dma_start(
                            out=c.out[tc * 128:(tc + 1) * 128, sl * 512:(sl + 1) * 512],
                            in_=c.fcc[:, 512:1024]).then_inc(sm["dO"], 16)
                    c.Y(yd)
                    c._ffn_prev_yd = nO
    barrier(c)


def _ln_stats(c):
    sm = c.sem
    cl = c.cols
    for tc in range(8):
        ndv = inc(c, "dve", 3)

        def vst(eng, tc=tc, ndv=ndv):
            d = ndv - 3
            for j in range(8):
                bi = eng.bn_stats(out=c.fa[0:128, j * 6:(j + 1) * 6],
                                  in_=c.big[:, tc * 4096 + j * 512:
                                            tc * 4096 + (j + 1) * 512])
            bi.then_inc(sm["dve"], 1)
            d += 1
            eng.wait_ge(sm["dve"], d)
            eng.bn_aggr(out=c.fa[:, 48:50], in_=c.fa[:, 0:48]).then_inc(sm["dve"], 1)
            d += 1
            eng.wait_ge(sm["dve"], d)
            eng.tensor_copy(out=cl[:, 64 + tc:65 + tc],
                            in_=c.fa[:, 48:49]).then_inc(sm["dve"], 1)
        c.V(vst)
        na = inc(c, "act")

        def ast(eng, tc=tc, ndv=ndv, na=na):
            eng.wait_ge(sm["dve"], ndv)
            eng.activation(out=c.fa[:, 50:51], in_=c.fa[:, 49:50], func=AF.Sqrt,
                           bias=cl[:, 148:149]).then_inc(sm["act"], 1)
        c.A(ast)
        ndv2 = inc(c, "dve", 3)

        def vst2(eng, tc=tc, na=na, ndv2=ndv2):
            d = ndv2 - 3
            eng.wait_ge(sm["act"], na)
            eng.reciprocal(out=cl[:, 0 + tc:1 + tc],
                           in_=c.fa[:, 50:51]).then_inc(sm["dve"], 1)
            d += 1
            eng.wait_ge(sm["dve"], d)
            eng.tensor_tensor(out=c.fa[:, 51:52], in0=cl[:, 64 + tc:65 + tc],
                              in1=cl[:, 0 + tc:1 + tc],
                              op=ALU.mult).then_inc(sm["dve"], 1)
            d += 1
            eng.wait_ge(sm["dve"], d)
            eng.tensor_scalar(out=cl[:, 8 + tc:9 + tc], in0=c.fa[:, 51:52],
                              scalar1=-1.0, scalar2=None,
                              op0=ALU.mult).then_inc(sm["dve"], 1)
        c.V(vst2)
    barrier(c)

def emit_ffn(c, tag, emit_out):
    sm, ct = c.sem, c.cnt
    XRES = 0
    QG = 32768
    GS = 49152
    INT = 57344
    cl = c.cols
    barrier(c)

    _ln_stats(c)

    fstop = os.environ.get("FFN_STOP", "")
    ti = 0
    for th in range(1 if fstop else 2):
        rn = _load_row(c, c.invg, 4096)
        _bcast_row(c, c.fb, 4096, rn)
        for tcl in range(4):
            tc = th * 4 + tcl
            _ffn_quant_tile(
                c, src=c.big[:, XRES + tc * 4096:XRES + (tc + 1) * 4096],
                a_col=cl[:, tc:tc + 1], b_col=cl[:, 8 + tc:9 + tc],
                inv_b=c.fb, s_col=cl[:, 16 + tc:17 + tc],
                rs_slot=cl[:, 40 + tc:41 + tc],
                width=4096, qdst=QG, tcl=tcl, use_ln=True, ti=ti)
            ti += 1
        barrier(c)
        if fstop == "quantg":
            return
        rn = _load_row(c, c.swg, 2048)
        _bcast_row(c, c.p4, 2048, rn)
        _ffn_mm(c, c.wgT, nsl=4, nk=32, qbase=QG, th=th, evac="gate", gsbase=GS)
        barrier(c)
        if fstop == "gate":
            return
        rn = _load_row(c, c.invu, 4096)
        _bcast_row(c, c.fb, 4096, rn)
        for tcl in range(4):
            tc = th * 4 + tcl
            _ffn_quant_tile(
                c, src=c.big[:, XRES + tc * 4096:XRES + (tc + 1) * 4096],
                a_col=cl[:, tc:tc + 1], b_col=cl[:, 8 + tc:9 + tc],
                inv_b=c.fb, s_col=cl[:, 24 + tc:25 + tc],
                rs_slot=cl[:, 48 + tc:49 + tc],
                width=4096, qdst=QG, tcl=tcl, use_ln=True, ti=ti)
            ti += 1
        barrier(c)
        rn = _load_row(c, c.swu, 2048)
        _bcast_row(c, c.p4, 2048, rn)
        _ffn_mm(c, c.wuT, nsl=4, nk=32, qbase=QG, th=th, evac="up",
                gsbase=GS, intbase=INT)
        barrier(c)
        if fstop == "up":
            return
        rn = _load_row(c, c.invi, 2048)
        _bcast_row(c, c.p4, 2048, rn)
        for tcl in range(4):
            tc = th * 4 + tcl
            _ffn_quant_tile(
                c, src=c.big[:, INT + tcl * 2048:INT + (tcl + 1) * 2048],
                a_col=None, b_col=None, inv_b=c.p4,
                s_col=cl[:, 32 + tc:33 + tc], rs_slot=cl[:, 56 + tc:57 + tc],
                width=2048, qdst=QG, tcl=tcl, use_ln=False, ti=ti)
            ti += 1
        barrier(c)
        if fstop == "requant":
            return
        rn = _load_row(c, c.swd, 4096)
        _bcast_row(c, c.fb, 4096, rn)
        _ffn_mm(c, c.wdT, nsl=8, nk=16, qbase=QG, th=th, evac="down",
                emit_out=emit_out)
        barrier(c)


def _ffn_quant_tile(c, src, a_col, b_col, inv_b, s_col, rs_slot, width, qdst,
                    tcl, use_ln, ti):
    sm = c.sem
    p = ti % 2
    xs = c.wbuf[:, p * 4096:p * 4096 + width]
    MAGIC = 12582912.0
    prev = getattr(c, "_fqt_prev", {})
    ndv = inc(c, "dve", 4 if use_ln else 3)

    def v(eng, ndv=ndv, p=p, use_ln=use_ln, pw=prev.get(("w", p)),
          nops=(4 if use_ln else 3)):
        d = ndv - nops
        if pw is not None:
            eng.wait_ge(sm["mm"], pw)
        if use_ln:
            eng.scalar_tensor_tensor(out=xs, in0=src, scalar=a_col, in1=inv_b[:, 0:width],
                                     op0=ALU.mult, op1=ALU.mult).then_inc(sm["dve"], 1)
            d += 1
            eng.wait_ge(sm["dve"], d)
            eng.scalar_tensor_tensor(out=xs, in0=inv_b[:, 0:width], scalar=b_col,
                                     in1=xs, op0=ALU.mult,
                                     op1=ALU.add).then_inc(sm["dve"], 1)
            d += 1
            eng.wait_ge(sm["dve"], d)
        else:
            eng.tensor_tensor(out=xs, in0=src, in1=inv_b[:, 0:width],
                              op=ALU.mult).then_inc(sm["dve"], 1)
            d += 1
            eng.wait_ge(sm["dve"], d)
        eng.tensor_reduce(out=s_col, in_=xs, axis=mybir.AxisListType.X,
                          op=ALU.max, apply_absolute_value=True).then_inc(sm["dve"], 1)
        d += 1
        eng.wait_ge(sm["dve"], d)
        eng.tensor_scalar(out=s_col, in0=s_col, scalar1=1.0 / 127.0,
                          scalar2=1e-8, op0=ALU.mult,
                          op1=ALU.max).then_inc(sm["dve"], 1)
    c.V(v)
    ndv2 = inc(c, "dve")

    def v2(eng, ndv2=ndv2):
        eng.wait_ge(sm["dve"], ndv2 - 1)
        eng.reciprocal(out=rs_slot, in_=s_col).then_inc(sm["dve"], 1)
    c.V(v2)

    for fh in range(width // 1024):
        fp = (ti * (width // 1024) + fh) % 2
        fcb = c.fcc[:, fp * 1024:(fp + 1) * 1024]
        qtm = c.stg[:, fp * 2048:fp * 2048 + 1024]
        ndq = inc(c, "dve", 3)

        def vq(eng, fh=fh, fp=fp, fcb=fcb, qtm=qtm, ndq=ndq, ndv2=ndv2,
               do0=c.cnt["dO"], pq=prev.get(("q", fp))):
            d = ndq - 3
            eng.wait_ge(sm["dve"], ndv2)
            eng.wait_ge(sm["dO"], do0)
            if pq is not None:
                eng.wait_ge(sm["mm"], pq)
            eng.tensor_scalar(out=fcb, in0=xs[:, fh * 1024:(fh + 1) * 1024],
                              scalar1=rs_slot, scalar2=None,
                              op0=ALU.mult).then_inc(sm["dve"], 1)
            d += 1
            eng.wait_ge(sm["dve"], d)
            eng.tensor_scalar(out=fcb, in0=fcb, scalar1=MAGIC,
                              scalar2=MAGIC, op0=ALU.add,
                              op1=ALU.subtract).then_inc(sm["dve"], 1)
            d += 1
            eng.wait_ge(sm["dve"], d)
            eng.tensor_scalar(out=qtm, in0=fcb,
                              scalar1=127.0, scalar2=-127.0, op0=ALU.min,
                              op1=ALU.max).then_inc(sm["dve"], 1)
        c.V(vq)
        nmm = inc(c, "mm")

        def tq(eng, fp=fp, qtm=qtm, ndq=ndq, nmm=nmm,
               pv=prev.get(("pb", fp))):
            eng.wait_ge(sm["dve"], ndq)
            if pv is not None:
                eng.wait_ge(sm["dve"], pv)
            for j in range(8):
                ti_ = eng.transpose(c.Pb[:, fp * 1024 + j * 128:fp * 1024 + (j + 1) * 128],
                                    qtm[:, j * 128:(j + 1) * 128], c.identb)
            ti_.then_inc(sm["mm"], 1)
        c.T(tq)
        ndq2 = inc(c, "dve")

        def vq2(eng, fh=fh, fp=fp, nmm=nmm, ndq2=ndq2, qdst=qdst, tcl=tcl):
            eng.wait_ge(sm["mm"], nmm)
            for j in range(8):
                fc = fh * 8 + j
                cpi = eng.tensor_copy(
                    out=c.big[:, qdst + fc * 512 + tcl * 128:
                              qdst + fc * 512 + (tcl + 1) * 128],
                    in_=c.Pb[:, fp * 1024 + j * 128:fp * 1024 + (j + 1) * 128])
            cpi.then_inc(sm["dve"], 1)
        c.V(vq2)
        prev[("q", fp)] = nmm
        prev[("pb", fp)] = ndq2
        prev[("w", p)] = nmm
    c._fqt_prev = prev


def _ffn_mm(c, wblocks, nsl, nk, qbase, th, evac, emit_out=False, gsbase=None,
            intbase=None):
    sm = c.sem
    cl = c.cols
    nld = nk // 8
    wsem = ["dWa", "dWb"]
    prev = getattr(c, "_fmm_prev", None)
    for sl in range(nsl):
        mm_q0 = c.cnt["mm"]
        first = (prev is None and sl == 0)

        def gw(eng, sl=sl, mm_q0=mm_q0, first=first):
            for l in range(nld):
                if l >= 2:
                    eng.wait_ge(sm["mm"], mm_q0 + l)
                elif not first:
                    eng.wait_ge(sm["mm"], mm_q0)
                eng.dma_start(
                    out=c.wbuf[:, (l % 2) * 4096:(l % 2 + 1) * 4096]
                    .rearrange("p (f q) -> p f q", f=8),
                    in_=wblocks[sl, l * 8:(l + 1) * 8]
                    .rearrange("f p q -> p f q")
                    ).then_inc(sm[wsem[l % 2]], 16)
        c.G(gw)
        wq_t = [inc(c, wsem[l % 2], 16) for l in range(nld)]

        def tchain(eng, sl=sl, wq_t=wq_t, qbase=qbase, th=th):
            for l in range(nld):
                eng.wait_ge(sm[wsem[l % 2]], wq_t[l])
                for tb4 in range(4):
                    for j in range(8):
                        fc = l * 8 + j
                        mmi = eng.matmul(
                            c.P1[tb4][:, :],
                            lhsT=c.big[:, qbase + fc * 512 + tb4 * 128:
                                       qbase + fc * 512 + (tb4 + 1) * 128],
                            rhs=c.wbuf[:, (l % 2) * 4096 + j * 512:
                                       (l % 2) * 4096 + (j + 1) * 512],
                            start=(l == 0 and j == 0), stop=(l == nld - 1 and j == 7))
                mmi.then_inc(sm["mm"], 1)
        c.T(tchain)
        mm_last = inc(c, "mm", nld)
        prev = mm_last
        c._fmm_prev = None

        for tb4 in range(4):
            tc = th * 4 + tb4
            if evac == "gate":
                ndv = inc(c, "dve")

                def vg(eng, sl=sl, tb4=tb4, tc=tc, mm_last=mm_last, ndv=ndv):
                    eng.wait_ge(sm["mm"], mm_last)
                    eng.scalar_tensor_tensor(
                        out=c.fa[:, (tb4 % 4) * 512:(tb4 % 4 + 1) * 512],
                        in0=c.P1[tb4][:, :], scalar=cl[:, 16 + tc:17 + tc],
                        in1=c.p4[:, sl * 512:(sl + 1) * 512], op0=ALU.mult,
                        op1=ALU.mult).then_inc(sm["dve"], 1)
                c.V(vg)
                na = inc(c, "act")

                def ag(eng, sl=sl, tb4=tb4, ndv=ndv, na=na):
                    eng.wait_ge(sm["dve"], ndv)
                    eng.activation(
                        out=c.big[:, gsbase + tb4 * 2048 + sl * 512:
                                  gsbase + tb4 * 2048 + (sl + 1) * 512],
                        in_=c.fa[:, (tb4 % 4) * 512:(tb4 % 4 + 1) * 512],
                        func=AF.Silu).then_inc(sm["act"], 1)
                c.A(ag)
            elif evac == "up":
                ndv = inc(c, "dve", 3)

                def vu(eng, sl=sl, tb4=tb4, tc=tc, mm_last=mm_last, ndv=ndv):
                    d = ndv - 3
                    eng.wait_ge(sm["mm"], mm_last)
                    eng.scalar_tensor_tensor(
                        out=c.fa[:, (tb4 % 4) * 512:(tb4 % 4 + 1) * 512],
                        in0=c.P1[tb4][:, :], scalar=cl[:, 24 + tc:25 + tc],
                        in1=c.p4[:, sl * 512:(sl + 1) * 512], op0=ALU.mult,
                        op1=ALU.mult).then_inc(sm["dve"], 1)
                    d += 1
                    eng.wait_ge(sm["dve"], d)
                    eng.tensor_tensor(
                        out=c.fcc[:, 0:512],
                        in0=c.fa[:, (tb4 % 4) * 512:(tb4 % 4 + 1) * 512],
                        in1=c.big[:, gsbase + tb4 * 2048 + sl * 512:
                                  gsbase + tb4 * 2048 + (sl + 1) * 512],
                        op=ALU.mult).then_inc(sm["dve"], 1)
                    d += 1
                    eng.wait_ge(sm["dve"], d)
                    eng.tensor_scalar(
                        out=c.big[:, intbase + tb4 * 2048 + sl * 512:
                                  intbase + tb4 * 2048 + (sl + 1) * 512],
                        in0=c.fcc[:, 0:512], scalar1=10.0, scalar2=-10.0,
                        op0=ALU.min, op1=ALU.max).then_inc(sm["dve"], 1)
                c.V(vu)
            else:
                ndv = inc(c, "dve", 2)
                prev_yd0 = getattr(c, "_ffn_prev_yd", None)

                def vd(eng, sl=sl, tb4=tb4, tc=tc, mm_last=mm_last, ndv=ndv,
                       emit_out=emit_out, prev_yd0=prev_yd0):
                    d = ndv - 2
                    eng.wait_ge(sm["mm"], mm_last)
                    eng.wait_ge(sm["dve"], d)
                    if prev_yd0 is not None:
                        eng.wait_ge(sm["dO"], prev_yd0)
                    eng.scalar_tensor_tensor(
                        out=c.fcc[:, 0:512],
                        in0=c.P1[tb4][:, :], scalar=cl[:, 32 + tc:33 + tc],
                        in1=c.fb[:, sl * 512:(sl + 1) * 512], op0=ALU.mult,
                        op1=ALU.mult).then_inc(sm["dve"], 1)
                    d += 1
                    eng.wait_ge(sm["dve"], d)
                    eng.tensor_tensor(
                        out=c.fcc[:, 512:1024], in0=c.fcc[:, 0:512],
                        in1=c.big[:, tc * 4096 + sl * 512:tc * 4096 + (sl + 1) * 512],
                        op=ALU.add).then_inc(sm["dve"], 1)
                c.V(vd)
                ndv2 = inc(c, "dve")
                prev_yd = getattr(c, "_ffn_prev_yd", None)

                def vd2(eng, sl=sl, tc=tc, ndv=ndv, ndv2=ndv2, prev_yd=prev_yd):
                    eng.wait_ge(sm["dve"], ndv)
                    if prev_yd is not None:
                        eng.wait_ge(sm["dO"], prev_yd)
                    eng.tensor_copy(
                        out=c.big[:, tc * 4096 + sl * 512:tc * 4096 + (sl + 1) * 512],
                        in_=c.fcc[:, 512:1024]).then_inc(sm["dve"], 1)
                c.V(vd2)
                if emit_out:
                    nO = inc(c, "dO", 16)

                    def yd(eng, sl=sl, tc=tc, ndv2=ndv2):
                        eng.wait_ge(sm["dve"], ndv2)
                        eng.dma_start(
                            out=c.out[tc * 128:(tc + 1) * 128, sl * 512:(sl + 1) * 512],
                            in_=c.fcc[:, 512:1024]).then_inc(sm["dO"], 16)
                    c.Y(yd)
                    c._ffn_prev_yd = nO
        barrier(c)



def _attn_b_normrope(c, buf, isk):
    sm = c.sem
    H2 = 65456
    CB1 = 64560
    MN = 63536
    ndv1 = inc(c, "dve")

    def v1(eng, ndv1=ndv1):
        eng.tensor_tensor(out=c.stg[:, 0:1024], in0=buf, in1=buf,
                          op=ALU.mult).then_inc(sm["dve"], 1)
    c.V(v1)
    nmm1 = inc(c, "mm")

    def t1(eng, ndv1=ndv1, nmm1=nmm1):
        eng.wait_ge(sm["dve"], ndv1)
        for nq in range(2):
            mmi = eng.matmul(c.Pf[0:2, nq * 512:(nq + 1) * 512],
                             lhsT=c.big[:, H2:H2 + 2],
                             rhs=c.stg[:, nq * 512:(nq + 1) * 512],
                             start=True, stop=True)
        mmi.then_inc(sm["mm"], 1)
    c.T(t1)
    na1 = inc(c, "act")

    def a1(eng, nmm1=nmm1, na1=na1):
        eng.wait_ge(sm["mm"], nmm1)
        eng.activation(out=c.fcc[0:2, 0:1024], in_=c.Pf[0:2, 0:1024],
                       func=AF.Sqrt).then_inc(sm["act"], 1)
    c.A(a1)
    ndv2 = inc(c, "dve", 3)

    def v2(eng, na1=na1, ndv2=ndv2, isk=isk):
        d = ndv2 - 3
        eng.wait_ge(sm["act"], na1)
        eng.tensor_scalar(out=c.fcc[0:2, 0:1024], in0=c.fcc[0:2, 0:1024],
                          scalar1=1e-12, scalar2=None,
                          op0=ALU.max).then_inc(sm["dve"], 1)
        d += 1
        eng.wait_ge(sm["dve"], d)
        eng.reciprocal(out=c.fcc[0:2, 1024:2048],
                       in_=c.fcc[0:2, 0:1024]).then_inc(sm["dve"], 1)
        d += 1
        eng.wait_ge(sm["dve"], d)
        if isk:
            eng.tensor_scalar(out=c.fcc[0:2, 1024:2048],
                              in0=c.fcc[0:2, 1024:2048],
                              scalar1=c.g_ktemp[0:2, 0:1], scalar2=SC_BH,
                              op0=ALU.mult, op1=ALU.mult).then_inc(sm["dve"], 1)
        else:
            eng.tensor_scalar(out=c.fcc[0:2, 1024:2048],
                              in0=c.fcc[0:2, 1024:2048],
                              scalar1=SC_BH, scalar2=None,
                              op0=ALU.mult).then_inc(sm["dve"], 1)
    c.V(v2)
    ndv2b = inc(c, "dve")

    def v2b(eng, ndv2=ndv2, ndv2b=ndv2b):
        eng.wait_ge(sm["dve"], ndv2)
        eng.tensor_copy(out=c.stg2[0:2, 1024:2048],
                        in_=c.fcc[0:2, 1024:2048]).then_inc(sm["dve"], 1)
    c.V(v2b)
    nmm2 = inc(c, "mm")

    def t2(eng, ndv2b=ndv2b, nmm2=nmm2):
        eng.wait_ge(sm["dve"], ndv2b)
        eng.matmul(c.P1[2][:, :], lhsT=c.big[0:2, CB1 + 768:CB1 + 896],
                   rhs=c.stg2[0:2, 1024:1536], start=True, stop=True)
        eng.matmul(c.P1[3][:, :], lhsT=c.big[0:2, CB1 + 768:CB1 + 896],
                   rhs=c.stg2[0:2, 1536:2048], start=True,
                   stop=True).then_inc(sm["mm"], 1)
    c.T(t2)
    ndv3 = inc(c, "dve", 4)

    def v3(eng, nmm2=nmm2, ndv3=ndv3):
        d = ndv3 - 4
        eng.wait_ge(sm["mm"], nmm2)
        eng.tensor_copy(out=c.stg[:, 1024:1536], in_=c.P1[2][:, :]).then_inc(sm["dve"], 1)
        d += 1
        eng.tensor_copy(out=c.stg[:, 1536:2048], in_=c.P1[3][:, :]).then_inc(sm["dve"], 1)
        d += 1
        eng.wait_ge(sm["dve"], d)
        eng.tensor_tensor(out=buf, in0=buf, in1=c.stg[:, 1024:2048],
                          op=ALU.mult).then_inc(sm["dve"], 1)
        d += 1
        eng.wait_ge(sm["dve"], d)
        eng.nop().then_inc(sm["dve"], 1)
    c.V(v3)
    nmm3 = inc(c, "mm")

    def t3(eng, ndv3=ndv3, nmm3=nmm3):
        eng.wait_ge(sm["dve"], ndv3)
        eng.matmul(c.P1[0][:, :], lhsT=c.big[:, MN:MN + 128],
                   rhs=buf[:, 0:512], start=True, stop=True)
        eng.matmul(c.P1[1][:, :], lhsT=c.big[:, MN:MN + 128],
                   rhs=buf[:, 512:1024], start=True,
                   stop=True).then_inc(sm["mm"], 1)
    c.T(t3)
    ndv4 = inc(c, "dve", 5)

    def v4(eng, nmm3=nmm3, ndv4=ndv4):
        d = ndv4 - 5
        eng.wait_ge(sm["mm"], nmm3)
        eng.tensor_copy(out=c.stg2[:, 0:512], in_=c.P1[0][:, :]).then_inc(sm["dve"], 1)
        d += 1
        eng.tensor_copy(out=c.stg2[:, 512:1024], in_=c.P1[1][:, :]).then_inc(sm["dve"], 1)
        d += 1
        eng.wait_ge(sm["dve"], d)
        eng.tensor_tensor(out=c.stg[:, 1024:2048], in0=buf, in1=c.vbuf[:, 0:1024],
                          op=ALU.mult).then_inc(sm["dve"], 1)
        d += 1
        eng.wait_ge(sm["dve"], d)
        eng.tensor_tensor(out=c.stg2[:, 0:1024], in0=c.stg2[:, 0:1024],
                          in1=c.vbuf[:, 1024:2048], op=ALU.mult).then_inc(sm["dve"], 1)
        d += 1
        eng.wait_ge(sm["dve"], d)
        eng.tensor_tensor(out=buf, in0=c.stg[:, 1024:2048], in1=c.stg2[:, 0:1024],
                          op=ALU.add).then_inc(sm["dve"], 1)
    c.V(v4)
    barrier(c)


def emit_attn_b(c, emit_out):
    sm, ct = c.sem, c.cnt
    cl = c.cols
    XN = 32768
    KQ = 49152
    C1Q = 32768
    QH = 40976
    C1K = KQ + 10 * 1028
    VT = 61488
    MN = 63536
    CB1 = 64560
    QATT = 49152
    VA = 57344
    CX = 32768
    barrier(c)

    ncb = inc(c, "dW", 16 * 7)

    def gc1(eng):
        eng.dma_start(out=c.big[:, CB1:CB1 + 512]
                      .rearrange("p (a b) -> p a b", a=8),
                      in_=c.hsum16.rearrange("a p b -> p a b")).then_inc(sm["dW"], 16)
        eng.dma_start(out=c.big[:, CB1 + 512:CB1 + 640]
                      .rearrange("p (a b) -> p a b", a=2),
                      in_=c.hsum4.rearrange("a p b -> p a b")).then_inc(sm["dW"], 16)
        eng.dma_start(out=c.big[0:64, CB1 + 640:CB1 + 768],
                      in_=c.bc2).then_inc(sm["dW"], 16)
        eng.dma_start(out=c.big[64:128, CB1 + 640:CB1 + 768],
                      in_=c.bc2).then_inc(sm["dW"], 16)
        eng.dma_start(out=c.big[0:2, CB1 + 768:CB1 + 896],
                      in_=c.bch2).then_inc(sm["dW"], 16)
        eng.dma_start(out=c.kbuf[:, :].rearrange("p (j q) -> p j q", j=4),
                      in_=c.masks.rearrange("j p q -> p j q")).then_inc(sm["dW"], 16)
        eng.dma_start(out=c.big[:, 65456:65458],
                      in_=c.hsum2x64).then_inc(sm["dW"], 16)
    c.G(gc1)
    c._b_ncb = c.cnt["dW"]
    _ln_stats(c)

    for th in range(2):
        for tcl in range(4):
            tc = th * 4 + tcl
            na = inc(c, "act")

            def axn(eng, tc=tc, na=na, pmm=c.cnt["mm"]):
                eng.wait_ge(sm["mm"], pmm)
                eng.activation(out=c.stg[:, 0:4096],
                               in_=c.big[:, tc * 4096:(tc + 1) * 4096],
                               func=AF.Identity, scale=cl[:, tc:tc + 1],
                               bias=cl[:, 8 + tc:9 + tc]).then_inc(sm["act"], 1)
            c.A(axn)
            for fh in range(2):
                nmm = inc(c, "mm")

                def txn(eng, fh=fh, na=na, nmm=nmm, pdv=c.cnt["dve"],
                        ncb=c._b_ncb):
                    eng.wait_ge(sm["dW"], ncb)
                    eng.wait_ge(sm["act"], na)
                    eng.wait_ge(sm["dve"], pdv)
                    for j in range(16):
                        ti = eng.transpose(c.Pb[:, j * 128:(j + 1) * 128],
                                           c.stg[:, fh * 2048 + j * 128:
                                                 fh * 2048 + (j + 1) * 128], c.identb)
                    ti.then_inc(sm["mm"], 1)
                c.T(txn)
                ndv = inc(c, "dve")

                def vxn(eng, fh=fh, tcl=tcl, nmm=nmm, ndv=ndv):
                    eng.wait_ge(sm["mm"], nmm)
                    for j in range(16):
                        fc = fh * 16 + j
                        cpi = eng.tensor_copy(
                            out=c.big[:, XN + fc * 512 + tcl * 128:
                                      XN + fc * 512 + (tcl + 1) * 128],
                            in_=c.Pb[:, j * 128:(j + 1) * 128])
                    cpi.then_inc(sm["dve"], 1)
                c.V(vxn)
        barrier(c)
        ev_t = {}
        for j in range(12):
            nw = inc(c, "dWa" if j % 2 == 0 else "dWb", 16)

            def gw(eng, j=j, nw=nw, g2=(ev_t.get(j - 2))):
                if g2 is not None:
                    eng.wait_ge(sm["act"], g2)
                eng.dma_start(out=c.wbuf[:, (j % 2) * 4096:(j % 2 + 1) * 4096]
                              .rearrange("p (f q) -> p f q", f=32),
                              in_=c.wqbT[j].rearrange("f p q -> p f q")
                              ).then_inc(sm["dWa" if j % 2 == 0 else "dWb"], 16)
            c.G(gw)
            nmm = inc(c, "mm")

            def tq(eng, j=j, nw=nw, nmm=nmm, pact=(ev_t.get(j - 2))):
                eng.wait_ge(sm["dWa" if j % 2 == 0 else "dWb"], nw)
                if pact is not None:
                    eng.wait_ge(sm["act"], pact)
                for fc in range(32):
                    mmi = eng.matmul(
                        c.P1[j % 2][:, :],
                        lhsT=c.wbuf[:, (j % 2) * 4096 + fc * 128:
                                    (j % 2) * 4096 + (fc + 1) * 128],
                        rhs=c.big[:, XN + fc * 512:XN + fc * 512 + 512],
                        start=(fc == 0), stop=(fc == 31))
                mmi.then_inc(sm["mm"], 1)
            c.T(tq)
            na = inc(c, "act")

            def aq(eng, j=j, th=th, nmm=nmm, na=na):
                eng.wait_ge(sm["mm"], nmm)
                eng.activation(out=c.big[:, KQ + j * 1028 + 4 + th * 512:
                                         KQ + j * 1028 + 4 + (th + 1) * 512],
                               in_=c.P1[j % 2][:, :],
                               func=AF.Copy).then_inc(sm["act"], 1)
            c.A(aq)
            ev_t[j] = na
        barrier(c)
    barrier(c)

    def ypk(eng):
        for jc in range(12):
            eng.dma_start(
                out=c.ccB1_in[256 // 256 * 0 + jc // 2:jc // 2 + 1,
                              (jc % 2) * 512:(jc % 2) * 512 + 512]
                .rearrange("r (p h) -> (r p) h", p=128),
                in_=c.big[:, KQ + jc * 1028 + 4 + 1020:KQ + jc * 1028 + 4 + 1024]
                ).then_inc(sm["dQ"], 16)
    c.Y(ypk)
    npk = inc(c, "dQ", 16 * 12)

    def gcc1(eng, npk=npk):
        eng.wait_ge(sm["dQ"], npk)
        eng.collective_compute(
            "AllGather", ALU.bypass,
            replica_groups=[[0, 1], [2, 3], [4, 5], [6, 7]],
            ins=[c.ccB1_in], outs=[c.ccB1_out.rearrange("r a q -> (r a) q")],
        ).then_inc(sm["cc"], 1)
    c.G(gcc1)
    ncc1 = inc(c, "cc")

    def yhl(eng, ncc1=ncc1):
        eng.wait_ge(sm["cc"], ncc1)
        for r in range(2):
            for jc in range(12):
                eng.dma_start(
                    out=c.stg2[:, r * 64 + jc * 4:r * 64 + (jc + 1) * 4],
                    in_=c.ccB1_out[r, jc // 2:jc // 2 + 1,
                                   (jc % 2) * 512:(jc % 2) * 512 + 512]
                    .rearrange("r (p h) -> (r p) h", p=128)).then_inc(sm["dQ"], 16)
    c.Y(yhl)
    nhl = inc(c, "dQ", 16 * 24)
    ndvh = inc(c, "dve", 2)

    def vhl(eng, nhl=nhl, ndvh=ndvh):
        eng.wait_ge(sm["dQ"], nhl)
        eng.tensor_scalar(out=c.stg2[:, 128:176], in0=c.stg2[:, 0:48],
                          scalar1=c.g_halo0, scalar2=None,
                          op0=ALU.mult).then_inc(sm["dve"], 1)
        eng.wait_ge(sm["dve"], ndvh - 1)
        eng.scalar_tensor_tensor(out=c.stg2[:, 176:224], in0=c.stg2[:, 64:112],
                                 scalar=c.g_halo1, in1=c.stg2[:, 128:176],
                                 op0=ALU.mult, op1=ALU.add).then_inc(sm["dve"], 1)
    c.V(vhl)

    def vhl2(eng, ndvh=ndvh, base=c.cnt["dve"]):
        eng.wait_ge(sm["dve"], ndvh)
        d = base
        for jc in range(12):
            cpi = eng.tensor_copy(
                out=c.big[:, KQ + jc * 1028:KQ + jc * 1028 + 4],
                in_=c.stg2[:, 176 + jc * 4:176 + (jc + 1) * 4])
        cpi.then_inc(sm["dve"], 1)
    c.V(vhl2)
    inc(c, "dve")
    barrier(c)

    for tb in range(8):
        nmm = inc(c, "mm")

        def tv(eng, tb=tb, nmm=nmm, pdv=c.cnt["dve"]):
            eng.wait_ge(sm["dve"], pdv)
            eng.transpose(c.Pb[:, 0:128],
                          c.big[:, KQ + 10 * 1028 + 4 + tb * 128:
                                KQ + 10 * 1028 + 4 + (tb + 1) * 128], c.identb)
            eng.transpose(c.Pb[:, 128:256],
                          c.big[:, KQ + 11 * 1028 + 3 + tb * 128:
                                KQ + 11 * 1028 + 3 + (tb + 1) * 128],
                          c.identb).then_inc(sm["mm"], 1)
        c.T(tv)
        ndv = inc(c, "dve")

        def vv(eng, tb=tb, nmm=nmm, ndv=ndv):
            eng.wait_ge(sm["mm"], nmm)
            eng.tensor_copy(out=c.big[:, VT + tb * 256:VT + (tb + 1) * 256],
                            in_=c.Pb[:, 0:256]).then_inc(sm["dve"], 1)
        c.V(vv)
    barrier(c)

    for side in range(2):
        for nq in range(2):
            nmm = inc(c, "mm")

            def tm(eng, side=side, nq=nq, nmm=nmm, pdv=c.cnt["dve"]):
                eng.wait_ge(sm["dve"], pdv)
                nch = 2 if side == 0 else 8
                cb = CB1 + 512 if side == 0 else CB1
                jb = 8 if side == 0 else 0
                for ci in range(nch):
                    mmi = eng.matmul(
                        c.Pf[0:64, nq * 512:(nq + 1) * 512],
                        lhsT=c.big[:, cb + ci * 64:cb + (ci + 1) * 64],
                        rhs=c.big[:, KQ + (jb + ci) * 1028 + 4 + nq * 512:
                                  KQ + (jb + ci) * 1028 + 4 + (nq + 1) * 512],
                        start=(ci == 0), stop=(ci == nch - 1))
                mmi.then_inc(sm["mm"], 1)
            c.T(tm)
            ndv = inc(c, "dve")

            def vm(eng, side=side, nq=nq, nmm=nmm, ndv=ndv):
                eng.wait_ge(sm["mm"], nmm)
                pb = 0 if side == 0 else 64
                eng.tensor_copy(
                    out=c.big[pb:pb + 64, MN + nq * 512:MN + (nq + 1) * 512],
                    in_=c.Pf[0:64, nq * 512:(nq + 1) * 512]).then_inc(sm["dve"], 1)
            c.V(vm)
    barrier(c)

    for path in range(2):
        nocc = 8 if path == 0 else 2
        wsrc = c.w1q if path == 0 else c.w1k
        c1base = C1Q if path == 0 else C1K
        jbase = 0 if path == 0 else 8
        bcol0 = 0 if path == 0 else 16
        woff = 0 if path == 0 else 3072
        c1sem = "dWa" if path == 0 else "dWb"
        nw = inc(c, c1sem, 16)

        def gw1(eng, wsrc=wsrc, nocc=nocc, woff=woff, c1sem=c1sem):
            eng.dma_start(out=c.stg[:, woff:woff + nocc * 384]
                          .rearrange("p (o t q) -> p o t q", o=nocc, t=3),
                          in_=wsrc.rearrange("o t p q -> p o t q")
                          ).then_inc(sm[c1sem], 16)
        c.G(gw1)
        for occ in range(nocc):
            for seg, (t0, t1) in enumerate([(-2, 510), (510, 1022), (1022, 1024)]):
                nmm = inc(c, "mm")

                def t1f(eng, occ=occ, t0=t0, t1=t1, nw=nw, nmm=nmm,
                        c1base=c1base, jbase=jbase, seg=seg, woff=woff,
                        par=len(getattr(c, "_c1ev", [])) % 2,
                        c1sem=c1sem,
                        pact=(getattr(c, "_c1ev", [])[-2:-1] or [None])[0]):
                    eng.wait_ge(sm[c1sem], nw)
                    if pact is not None:
                        eng.wait_ge(sm["act"], pact)
                    n = t1 - t0
                    ps = c.P1[par]
                    for tau in range(3):
                        mmi = eng.matmul(
                            ps[:, 0:n],
                            lhsT=c.stg[:, woff + occ * 384 + tau * 128:
                                       woff + occ * 384 + (tau + 1) * 128],
                            rhs=c.big[:, KQ + (jbase + occ) * 1028 + (t0 + 2 + tau):
                                      KQ + (jbase + occ) * 1028 + (t0 + 2 + tau) + n],
                            start=(tau == 0), stop=(tau == 2))
                    mmi.then_inc(sm["mm"], 1)
                c.T(t1f)
                na = inc(c, "act")

                def a1f(eng, occ=occ, t0=t0, t1=t1, nmm=nmm, na=na,
                        c1base=c1base, bcol0=bcol0, seg=seg,
                        par=len(getattr(c, "_c1ev", [])) % 2):
                    eng.wait_ge(sm["mm"], nmm)
                    n = t1 - t0
                    ps = c.P1[par]
                    eng.activation(out=c.big[:, c1base + occ * 1026 + (t0 + 2):
                                             c1base + occ * 1026 + (t0 + 2) + n],
                                   in_=ps[:, 0:n], func=AF.Identity,
                                   bias=c.convb[:, bcol0 + occ:bcol0 + occ + 1],
                                   scale=1.0).then_inc(sm["act"], 1)
                c.A(a1f)
                if not hasattr(c, "_c1ev"):
                    c._c1ev = []
                c._c1ev.append(na)
    barrier(c)
    c._c1ev = []

    for path in range(2):
        nocc = 8 if path == 0 else 2
        nicc = 8 if path == 0 else 2
        wsrc = c.w2q if path == 0 else c.w2k
        c1base = C1Q if path == 0 else C1K
        bcol0 = 8 if path == 0 else 18
        mnb = 0 if path == 0 else 64
        for occ in range(nocc):
            nw = inc(c, "dWa" if occ % 2 == 0 else "dWb", 16)

            def gw2(eng, wsrc=wsrc, occ=occ, nicc=nicc, nw=nw,
                    pact=(getattr(c, "_c2w", [])[-2:-1] or [None])[0]):
                if pact is not None:
                    eng.wait_ge(sm["act"], pact)
                eng.dma_start(
                    out=c.wbuf[:, (occ % 2) * 4096:(occ % 2) * 4096 + nicc * 384]
                    .rearrange("p (i t q) -> p i t q", i=nicc, t=3),
                    in_=wsrc[occ].rearrange("i t p q -> p i t q")
                    ).then_inc(sm["dWa" if occ % 2 == 0 else "dWb"], 16)
            c.G(gw2)
            for nq in range(2):
                nmm = inc(c, "mm")

                def t2f(eng, occ=occ, nq=nq, nw=nw, nmm=nmm, nicc=nicc,
                        c1base=c1base, mnb=mnb,
                        pact=(getattr(c, "_c2ev", [])[-2:-1] or [None])[0]):
                    eng.wait_ge(sm["dWa" if occ % 2 == 0 else "dWb"], nw)
                    if pact is not None:
                        eng.wait_ge(sm["act"], pact)
                    ps = c.P1[nq % 2]
                    for ic in range(nicc):
                        for tau in range(3):
                            eng.matmul(
                                ps[:, :],
                                lhsT=c.wbuf[:, (occ % 2) * 4096 + ic * 384 +
                                            tau * 128:(occ % 2) * 4096 + ic * 384 +
                                            (tau + 1) * 128],
                                rhs=c.big[:, c1base + ic * 1026 + nq * 512 + tau:
                                          c1base + ic * 1026 + nq * 512 + tau + 512],
                                start=(ic == 0 and tau == 0), stop=False)
                    mmi = eng.matmul(
                        ps[:, :],
                        lhsT=c.big[mnb:mnb + 64, CB1 + 640:CB1 + 768],
                        rhs=c.big[mnb:mnb + 64, MN + nq * 512:MN + (nq + 1) * 512],
                        start=False, stop=True)
                    mmi.then_inc(sm["mm"], 1)
                c.T(t2f)
                na = inc(c, "act")

                def a2f(eng, path=path, occ=occ, nq=nq, nmm=nmm, na=na, bcol0=bcol0):
                    eng.wait_ge(sm["mm"], nmm)
                    ps = c.P1[nq % 2]
                    if path == 0:
                        dst = c.big[:, QH + occ * 1024 + nq * 512:
                                    QH + occ * 1024 + (nq + 1) * 512]
                    else:
                        kt = c.p4 if occ == 0 else c.qtl
                        dst = kt[:, nq * 512:(nq + 1) * 512]
                    eng.activation(out=dst, in_=ps[:, :], func=AF.Identity,
                                   bias=c.convb[:, bcol0 + occ:bcol0 + occ + 1],
                                   scale=1.0).then_inc(sm["act"], 1)
                c.A(a2f)
                if not hasattr(c, "_c2ev"):
                    c._c2ev = []
                c._c2ev.append(na)
                c._c2w = getattr(c, "_c2w", [])
            c._c2w.append(c.cnt["act"])
    barrier(c)
    c._c2ev = []
    c._c2w = []

    def gc2(eng):
        eng.dma_start(out=c.big[:, MN:MN + 128], in_=c.swap).then_inc(sm["dW"], 16)
        eng.dma_start(out=c.fa[0:64, 1100:1228],
                      in_=c.rope_c).then_inc(sm["dW"], 16)
        eng.dma_start(out=c.fa[0:64, 1228:1356],
                      in_=c.rope_s).then_inc(sm["dW"], 16)
    c.G(gc2)
    ngc2 = inc(c, "dW", 48)

    def ypos(eng):
        eng.dma_start(out=c.fcc[0:1, 0:1024], in_=c.pos).then_inc(sm["dQ"], 16)
    c.Y(ypos)
    npos = inc(c, "dQ", 16)

    def gio(eng):
        eng.iota(out=c.fa[0:64, 1024:1025].bitcast(I32), pattern=[[0, 1]],
                 base=0, channel_multiplier=1).then_inc(sm["gp"], 1)
    c.G(gio)
    inc(c, "gp")
    barrier(c)

    def vw0(eng, ngc2=ngc2, npos=npos):
        eng.wait_ge(sm["dW"], ngc2)
        eng.wait_ge(sm["dQ"], npos)
        eng.nop().then_inc(sm["dve"], 1)
    c.V(vw0)
    inc(c, "dve")

    def tw0(eng, ngc2=ngc2, npos=npos):
        eng.wait_ge(sm["dW"], ngc2)
        eng.wait_ge(sm["dQ"], npos)
    c.T(tw0)

    def aw0(eng, ngc2=ngc2):
        eng.wait_ge(sm["dW"], ngc2)
    c.A(aw0)

    CHI = float(np.float32(-2.0 * LN10K / 64.0))
    CLO = float(np.float32(-2.0 * LN10K / 64.0 - CHI))

    def vinv(eng, base=c.cnt["dve"]):
        d = base
        ops = [
            lambda: eng.tensor_scalar(out=c.fa[0:64, 1025:1026].bitcast(I32),
                                      in0=c.fa[0:64, 1024:1025].bitcast(I32),
                                      scalar1=-2, scalar2=None,
                                      op0=ALU.bitwise_and),
            lambda: eng.tensor_copy(out=c.fa[0:64, 1026:1027],
                                    in_=c.fa[0:64, 1025:1026].bitcast(I32)),
            lambda: eng.tensor_scalar(out=c.fa[0:64, 1027:1028],
                                      in0=c.fa[0:64, 1026:1027], scalar1=0.5,
                                      scalar2=CHI, op0=ALU.mult, op1=ALU.mult),
            lambda: eng.tensor_scalar(out=c.fa[0:64, 1028:1029],
                                      in0=c.fa[0:64, 1026:1027], scalar1=0.5,
                                      scalar2=CLO, op0=ALU.mult, op1=ALU.mult),
            lambda: eng.tensor_tensor(out=c.fa[0:64, 1029:1030],
                                      in0=c.fa[0:64, 1027:1028],
                                      in1=c.fa[0:64, 1028:1029], op=ALU.add),
        ]
        for op in ops:
            op().then_inc(sm["dve"], 1)
            d += 1
            eng.wait_ge(sm["dve"], d)
    c.V(vinv)
    nvi = inc(c, "dve", 5)

    def ainv(eng, nvi=nvi):
        eng.wait_ge(sm["dve"], nvi)
        eng.activation(out=c.fa[0:64, 1030:1031], in_=c.fa[0:64, 1029:1030],
                       func=AF.Exp).then_inc(sm["act"], 1)
    c.A(ainv)
    inc(c, "act")
    barrier(c)

    def tiv(eng):
        eng.transpose(c.Pf[0:1, 0:64], c.fa[0:64, 1030:1031],
                      c.identf[0:64, 0:64]).then_inc(sm["mm"], 1)
    c.T(tiv)
    inc(c, "mm")
    barrier(c)

    def viv(eng):
        eng.tensor_copy(out=c.fcc[0:1, 1024:1088],
                        in_=c.Pf[0:1, 0:64]).then_inc(sm["dve"], 1)
    c.V(viv)
    inc(c, "dve")
    barrier(c)

    def tth(eng):
        for nq in range(2):
            mmi = eng.matmul(c.Pf[0:64, nq * 512:(nq + 1) * 512],
                             lhsT=c.fcc[0:1, 1024:1088],
                             rhs=c.fcc[0:1, nq * 512:(nq + 1) * 512],
                             start=True, stop=True)
        mmi.then_inc(sm["mm"], 1)
    c.T(tth)
    inc(c, "mm")
    barrier(c)

    def vth(eng):
        eng.tensor_copy(out=c.fa[0:64, 0:1024],
                        in_=c.Pf[0:64, 0:1024]).then_inc(sm["dve"], 1)
    c.V(vth)
    inc(c, "dve")
    barrier(c)

    INV2PI = float(np.float32(1.0 / (2.0 * PI)))
    PI2HI = float(np.float32(2.0 * PI))
    PI2LO = float(np.float32(2.0 * PI - PI2HI))
    MAGIC = 12582912.0
    for var in range(2):
        def vrr(eng, var=var, base=c.cnt["dve"]):
            d = base
            ops = [
                lambda: eng.tensor_scalar(out=c.fa[64:128, 0:1024],
                                          in0=c.fa[0:64, 0:1024],
                                          scalar1=(0.0 if var == 0 else PI / 2),
                                          scalar2=None, op0=ALU.add),
                lambda: eng.tensor_scalar(out=c.fcc[64:128, 0:1024],
                                          in0=c.fa[64:128, 0:1024],
                                          scalar1=INV2PI, scalar2=MAGIC,
                                          op0=ALU.mult, op1=ALU.add),
                lambda: eng.tensor_scalar(out=c.fcc[64:128, 0:1024],
                                          in0=c.fcc[64:128, 0:1024],
                                          scalar1=MAGIC, scalar2=None,
                                          op0=ALU.subtract),
                lambda: eng.scalar_tensor_tensor(out=c.fa[64:128, 0:1024],
                                                 in0=c.fcc[64:128, 0:1024],
                                                 scalar=-PI2HI,
                                                 in1=c.fa[64:128, 0:1024],
                                                 op0=ALU.mult, op1=ALU.add),
                lambda: eng.scalar_tensor_tensor(out=c.fa[64:128, 0:1024],
                                                 in0=c.fcc[64:128, 0:1024],
                                                 scalar=-PI2LO,
                                                 in1=c.fa[64:128, 0:1024],
                                                 op0=ALU.mult, op1=ALU.add),
                lambda: eng.tensor_scalar(out=c.fa[64:128, 0:1024],
                                          in0=c.fa[64:128, 0:1024],
                                          scalar1=PI, scalar2=-PI,
                                          op0=ALU.min, op1=ALU.max),
            ]
            for op in ops:
                op().then_inc(sm["dve"], 1)
                d += 1
                eng.wait_ge(sm["dve"], d)
        c.V(vrr)
        nrr = inc(c, "dve", 6)

        def asin(eng, var=var, nrr=nrr):
            eng.wait_ge(sm["dve"], nrr)
            eng.activation(out=c.fcc[0:64, var * 1024:(var + 1) * 1024],
                           in_=c.fa[64:128, 0:1024],
                           func=AF.Sin).then_inc(sm["act"], 1)
        c.A(asin)
        inc(c, "act")
        barrier(c)

    for var in range(2):
        def tex(eng, var=var):
            lh = c.fa[0:64, 1228:1356] if var == 0 else c.fa[0:64, 1100:1228]
            for nq in range(2):
                mmi = eng.matmul(c.Pf[:, nq * 512:(nq + 1) * 512],
                                 lhsT=lh,
                                 rhs=c.fcc[0:64, (0 if var == 0 else 1024) + nq * 512:
                                           (0 if var == 0 else 1024) + (nq + 1) * 512],
                                 start=True, stop=True)
            mmi.then_inc(sm["mm"], 1)
        c.T(tex)
        nex = inc(c, "mm")

        def vex(eng, var=var, nex=nex):
            eng.wait_ge(sm["mm"], nex)
            eng.tensor_copy(out=c.vbuf[:, (1 - var) * 1024:(2 - var) * 1024],
                            in_=c.Pf[:, 0:1024]).then_inc(sm["dve"], 1)
        c.V(vex)
        inc(c, "dve")
        barrier(c)

    for ch in range(10):
        if ch < 8:
            buf = c.big[:, QH + ch * 1024:QH + (ch + 1) * 1024]
            isk = False
        else:
            buf = (c.p4 if ch == 8 else c.qtl)[:, 0:1024]
            isk = True
        _attn_b_normrope(c, buf, isk)

    prev_cp = None
    for h in [14, 15] + list(range(14)):
        ndv = inc(c, "dve")

        def vcp(eng, h=h, ndv=ndv, prev_cp=prev_cp):
            if h == 0 and prev_cp is not None:
                eng.wait_ge(sm["dve"], prev_cp)
            cdst = (h % 4) + 4 * (h // 8)
            bb = ((h // 4) % 2) * 64
            sb_ = (h % 2) * 64
            eng.tensor_copy(
                out=c.big[bb:bb + 64, QATT + cdst * 1024:QATT + (cdst + 1) * 1024],
                in_=c.big[sb_:sb_ + 64, QH + (h // 2) * 1024:
                          QH + (h // 2 + 1) * 1024]).then_inc(sm["dve"], 1)
        c.V(vcp)
        if h == 15:
            prev_cp = c.cnt["dve"]
    barrier(c)

    def ypk2(eng):
        eng.dma_start(out=c.ccB2_in[0:128, :], in_=c.p4[:, 0:1024]).then_inc(sm["dQ"], 16)
        eng.dma_start(out=c.ccB2_in[128:256, :], in_=c.qtl[:, 0:1024]).then_inc(sm["dQ"], 16)
        for tb in range(8):
            eng.dma_start(
                out=c.ccB2_in[256 + tb * 32:256 + (tb + 1) * 32, :]
                .rearrange("a (b q) -> (a b) q", q=256),
                in_=c.big[:, VT + tb * 256:VT + (tb + 1) * 256]).then_inc(sm["dQ"], 16)
    c.Y(ypk2)
    npk2 = inc(c, "dQ", 160)

    def gcc2(eng, npk2=npk2):
        eng.wait_ge(sm["dQ"], npk2)
        eng.collective_compute(
            "AllGather", ALU.bypass,
            replica_groups=[[0, 1], [2, 3], [4, 5], [6, 7]],
            ins=[c.ccB2_in], outs=[c.ccB2_out.rearrange("r a q -> (r a) q")],
        ).then_inc(sm["cc"], 1)
    c.G(gcc2)
    ncc2 = inc(c, "cc")

    ndvm = inc(c, "dve")

    def vmset(eng, ndvm=ndvm):
        eng.memset(c.big[:, VA:VA + 15 * 260], 1.0).then_inc(sm["dve"], 1)
    c.V(vmset)
    for tb in range(8):
        ndv = inc(c, "dve")

        def vown(eng, tb=tb, ndvm=ndvm, ndv=ndv):
            if tb == 0:
                eng.wait_ge(sm["dve"], ndvm)
            for g in range(4):
                cpi = eng.tensor_copy(
                    out=c.big[:, VA + tb * 260 + g * 65:VA + tb * 260 + g * 65 + 64],
                    in_=c.big[:, VT + tb * 256 + g * 64:VT + tb * 256 + (g + 1) * 64])
            cpi.then_inc(sm["dve"], 1)
        c.V(vown)
    nown = c.cnt["dve"]

    ndvm2 = inc(c, "dve")

    def vmset2(eng, nown=nown, npk2=npk2, ndvm2=ndvm2):
        eng.wait_ge(sm["dve"], nown)
        eng.wait_ge(sm["dQ"], npk2)
        eng.memset(c.big[:, VA + 15 * 260:VA + 16 * 260],
                   1.0).then_inc(sm["dve"], 1)
    c.V(vmset2)
    barrier(c)

    for t in range(2):
        tile = c.p4 if t == 0 else c.qtl
        nld = inc(c, "dQ", 32)

        def ykp(eng, t=t, ncc2=ncc2):
            if t == 0:
                eng.wait_ge(sm["cc"], ncc2)
            eng.dma_start(out=c.stg[:, 0:1024],
                          in_=c.ccB2_out[0, t * 128:(t + 1) * 128, :]).then_inc(sm["dQ"], 16)
            eng.dma_start(out=c.stg[:, 1024:2048],
                          in_=c.ccB2_out[1, t * 128:(t + 1) * 128, :]).then_inc(sm["dQ"], 16)
        c.Y(ykp)
        ndv = inc(c, "dve", 2)

        def vkp(eng, tile=tile, nld=nld, ndv=ndv):
            d = ndv - 2
            eng.wait_ge(sm["dQ"], nld)
            eng.tensor_scalar(out=c.stg2[:, 0:1024], in0=c.stg[:, 0:1024],
                              scalar1=c.g_peer0, scalar2=None,
                              op0=ALU.mult).then_inc(sm["dve"], 1)
            d += 1
            eng.wait_ge(sm["dve"], d)
            eng.scalar_tensor_tensor(out=tile[:, 1024:2048], in0=c.stg[:, 1024:2048],
                                     scalar=c.g_peer1, in1=c.stg2[:, 0:1024],
                                     op0=ALU.mult, op1=ALU.add).then_inc(sm["dve"], 1)
        c.V(vkp)
        barrier(c)
    for tb in range(8):
        nld = inc(c, "dQ", 32)

        def yvp(eng, tb=tb, pdv=c.cnt["dve"]):
            eng.wait_ge(sm["dve"], pdv)
            for r in range(2):
                eng.dma_start(
                    out=c.stg[:, r * 256:(r + 1) * 256],
                    in_=c.ccB2_out[r, 256 + tb * 32:256 + (tb + 1) * 32, :]
                    .rearrange("a (b q) -> (a b) q", q=256)).then_inc(sm["dQ"], 16)
        c.Y(yvp)
        ndv = inc(c, "dve", 3)

        def vvp(eng, tb=tb, nld=nld, ndv=ndv):
            d = ndv - 3
            eng.wait_ge(sm["dQ"], nld)
            eng.tensor_scalar(out=c.stg2[:, 0:256], in0=c.stg[:, 0:256],
                              scalar1=c.g_peer0, scalar2=None,
                              op0=ALU.mult).then_inc(sm["dve"], 1)
            d += 1
            eng.wait_ge(sm["dve"], d)
            eng.scalar_tensor_tensor(out=c.stg2[:, 256:512], in0=c.stg[:, 256:512],
                                     scalar=c.g_peer1, in1=c.stg2[:, 0:256],
                                     op0=ALU.mult, op1=ALU.add).then_inc(sm["dve"], 1)
            d += 1
            eng.wait_ge(sm["dve"], d)
            slot = 8 + tb
            for g in range(4):
                cpi = eng.tensor_copy(
                    out=c.big[:, VA + slot * 260 + g * 65:VA + slot * 260 + g * 65 + 64],
                    in_=c.stg2[:, 256 + g * 64:256 + (g + 1) * 64])
            cpi.then_inc(sm["dve"], 1)
        c.V(vvp)
    barrier(c)

    ndvn = inc(c, "dve", 2)

    def vnegb(eng, ndvn=ndvn):
        d = ndvn - 2
        eng.tensor_scalar(out=cl[:, 150:151], in0=c.g_ktemp, scalar1=-SC_B,
                          scalar2=None, op0=ALU.mult).then_inc(sm["dve"], 1)
        d += 1
        eng.wait_ge(sm["dve"], d)
        eng.tensor_tensor(out=cl[:, 151:152], in0=cl[:, 150:151], in1=c.g_kill,
                          op=ALU.add).then_inc(sm["dve"], 1)
    c.V(vnegb)
    barrier(c)
    for h in range(16):
        _attn_b_head(c, h, QATT, VA, CX)
    barrier(c)

    _oproj_residual(c, wblocks=c.wobT, nof=8, nlat=8, latbase=CX,
                    bias_row=None, xsrc=None, emit_out=emit_out, tb_count=8)


def _attn_b_head(c, h, QATT, VA, CX):
    sm = c.sem
    cl = c.cols
    g = h // 4
    ktile = c.p4 if g < 2 else c.qtl
    kb = (g % 2) * 64
    chq = (h % 4) + 4 * (h // 8)
    bb = (h % 2) * 64
    gk = h // 4

    qplan = []
    for qt in range(2):
        own = list(range(0, min(qt * 4 + 4, 8)))
        slots = own + list(range(8, 16))
        nmm_sc, act_b, mm_pv, ndvm = {}, {}, {}, {}
        for bi in range(0, len(slots), 2):
            pair = slots[bi:bi + 2]
            nmm_sc[bi] = inc(c, "mm")
            act_b[bi] = inc(c, "act")
            if [sl for sl in pair if sl < 8 and sl - qt * 4 >= 0]:
                ndvm[bi] = inc(c, "dve")
            mm_pv[bi] = inc(c, "mm")
        ndv5 = inc(c, "dve", 2)
        nmm_rl = inc(c, "mm")
        ndv6 = inc(c, "dve", 2)
        qplan.append(dict(slots=slots, nmm_sc=nmm_sc, act_b=act_b,
                          mm_pv=mm_pv, ndvm=ndvm, ndv5=ndv5,
                          nmm_rl=nmm_rl, ndv6=ndv6))

    def t_all(eng):
        for qt in range(2):
            P = qplan[qt]
            slots = P["slots"]
            first_pv = True
            for bi in range(0, len(slots), 2):
                pair = slots[bi:bi + 2]
                if bi >= 2:
                    eng.wait_ge(sm["act"], P["act_b"][bi - 2])
                for j, sl in enumerate(pair):
                    mmi = eng.matmul(
                        c.Pf[:, j * 512:(j + 1) * 512],
                        lhsT=ktile[kb:kb + 64, sl * 128:(sl + 1) * 128],
                        rhs=c.big[kb:kb + 64, QATT + chq * 1024 + qt * 512:
                                  QATT + chq * 1024 + (qt + 1) * 512],
                        start=True, stop=True)
                mmi.then_inc(sm["mm"], 1)
                eng.wait_ge(sm["act"], P["act_b"][bi])
                if bi in P["ndvm"]:
                    eng.wait_ge(sm["dve"], P["ndvm"][bi])
                last = bi + 2 >= len(slots)
                for j, sl in enumerate(pair):
                    mmi = eng.matmul(
                        c.P1[0][0:65, :],
                        lhsT=c.big[:, VA + sl * 260 + gk * 65:
                                   VA + sl * 260 + gk * 65 + 65],
                        rhs=c.stg[:, (bi % 4) // 2 * 1024 + j * 512:
                                  (bi % 4) // 2 * 1024 + (j + 1) * 512],
                        start=(first_pv and j == 0),
                        stop=(last and j == len(pair) - 1))
                mmi.then_inc(sm["mm"], 1)
                first_pv = False
            eng.wait_ge(sm["dve"], P["ndv5"])
            eng.matmul(c.P1[2][0:64, :], lhsT=c.ones1b[0:1, 0:64],
                       rhs=c.browz[0:1, 0:512], start=True,
                       stop=True).then_inc(sm["mm"], 1)
            eng.wait_ge(sm["dve"], P["ndv6"])
    c.T(t_all)

    def a_all(eng):
        for qt in range(2):
            P = qplan[qt]
            slots = P["slots"]
            for bi in range(0, len(slots), 2):
                pair = slots[bi:bi + 2]
                eng.wait_ge(sm["mm"], P["nmm_sc"][bi])
                if bi >= 4:
                    eng.wait_ge(sm["mm"], P["mm_pv"][bi - 4])
                bias = cl[:, 150:151] if pair[0] < 8 else cl[:, 151:152]
                eng.activation(out=c.stg[:, (bi % 4) // 2 * 1024:
                                         ((bi % 4) // 2 + 1) * 1024],
                               in_=c.Pf[:, :], func=AF.Exp,
                               bias=bias).then_inc(sm["act"], 1)
    c.A(a_all)

    def v_all(eng):
        for qt in range(2):
            P = qplan[qt]
            slots = P["slots"]
            for bi in range(0, len(slots), 2):
                if bi not in P["ndvm"]:
                    continue
                pair = slots[bi:bi + 2]
                eng.wait_ge(sm["act"], P["act_b"][bi])
                for j, sl in enumerate(pair):
                    j0 = sl - qt * 4
                    if sl < 8 and j0 >= 0:
                        cpi = eng.tensor_tensor(
                            out=c.stg[:, (bi % 4) // 2 * 1024 + j * 512:
                                      (bi % 4) // 2 * 1024 + (j + 1) * 512],
                            in0=c.stg[:, (bi % 4) // 2 * 1024 + j * 512:
                                      (bi % 4) // 2 * 1024 + (j + 1) * 512],
                            in1=c.kbuf[:, j0 * 512:(j0 + 1) * 512],
                            op=ALU.mult)
                cpi.then_inc(sm["dve"], 1)
            eng.wait_ge(sm["mm"], P["mm_pv"][len(slots) - 2])
            eng.reciprocal(out=c.fcc[0:1, 0:512],
                           in_=c.P1[0][64:65, 0:512]).then_inc(sm["dve"], 1)
            eng.wait_ge(sm["dve"], P["ndv5"] - 1)
            eng.tensor_copy(out=c.browz[0:1, 0:512],
                            in_=c.fcc[0:1, 0:512]).then_inc(sm["dve"], 1)
            eng.wait_ge(sm["mm"], P["nmm_rl"])
            eng.tensor_copy(out=c.stg2[0:64, 1024:1536],
                            in_=c.P1[2][0:64, :]).then_inc(sm["dve"], 1)
            eng.wait_ge(sm["dve"], P["ndv6"] - 1)
            eng.tensor_tensor(
                out=c.big[bb:bb + 64, CX + (h // 2) * 1024 + qt * 512:
                          CX + (h // 2) * 1024 + (qt + 1) * 512],
                in0=c.P1[0][0:64, :], in1=c.stg2[0:64, 1024:1536],
                op=ALU.mult).then_inc(sm["dve"], 1)
    c.V(v_all)


def emit_finish(c):
    sm = c.sem
    tot = c.cnt["dO"]

    def y(eng, tot=tot):
        eng.wait_ge(sm["dO"], tot)
    c.Y(y)
    barrier(c)



def kernel(**inputs):
    in_maps = make_in_maps(inputs)
    nc = build_nc(STAGE)
    res = run_bass_kernel_spmd(nc, in_maps, core_ids=list(range(8)))
    out = np.zeros((4, S, H), np.float32)
    for core in range(8):
        b, half = core // 2, core % 2
        out[b, half * ROWS:(half + 1) * ROWS, :] = res.results[core]["out"]
    return out
